# version 10
# baseline (speedup 1.0000x reference)
import sys
if '/opt/trn_rl_repo' not in sys.path:
    sys.path.insert(0, '/opt/trn_rl_repo')
import numpy as np
import ml_dtypes
import concourse.bass as bass
import concourse.mybir as mybir

F32 = mybir.dt.float32
BF16 = mybir.dt.bfloat16
U8 = mybir.dt.uint8
AF = mybir.ActivationFunctionType
ALU = mybir.AluOpType
AX = mybir.AxisListType.X

ENGINES = ("tensor", "vector", "scalar", "gpsimd", "sync")
D, DK, H, HS, FF, FK, CH = 1024, 8, 16, 64, 4096, 32, 64
EPS = 1e-5


class Sem:
    def __init__(self, nc, name):
        self.h = nc.alloc_semaphore(name=name)
        self.v = 0


class SlabStream:

    def __init__(self, kb, bufs, sems, init_guards=()):
        self.kb = kb
        self.bufs = bufs
        self.sems = sems
        self.seq = []
        self.next_emit = 0
        self.next_use = 0
        self.consumed = {}
        self.t_w = {}
        self.init_guards = list(init_guards)

    def reg(self, key, NOB, cpo, sl):
        for o0 in range(0, NOB, sl):
            self.seq.append((key, o0, min(NOB, o0 + sl), cpo))

    def _emit(self):
        kb = self.kb
        i = self.next_emit
        key, o0, o1, cpo = self.seq[i]
        par = i % 2
        if i < 2:
            for sem, tgt in self.init_guards:
                kb.W("scalar", sem, tgt)
        g = self.consumed.get(i - 2)
        if g:
            kb.W("scalar", g[0], g[1])
        kb.W("scalar", kb.S_cast, kb.cast_tgt[key])
        out = self.bufs[par].ap()[:, 0:(o1 - o0) * cpo].rearrange(
            "p (s c) -> p s c", c=cpo)
        self.t_w[i] = kb.dma("scalar", out, kb.wbf[key].ap()[:, o0:o1, :],
                             sem=self.sems[par])
        self.next_emit += 1

    def use(self, key, la=True):
        i = self.next_use
        assert self.seq[i][0] == key, (self.seq[i], key)
        while self.next_emit <= min(i + (1 if la else 0), len(self.seq) - 1):
            self._emit()
        self.next_use += 1
        return i, i % 2, self.t_w[i], self.seq[i]

    def done(self, i, guard):
        self.consumed[i] = guard


class KB:

    def __init__(self, T=2048, dbg=(), use_cc=True):
        self.T = T
        self.TG = min(512, T)
        self.NTG = T // self.TG
        self.TT4 = self.TG // 128
        self.NCH = T // CH
        self.NQT = T // 128
        self.KMB = min(256, T)
        self.dbg = set(dbg)
        self.use_cc = use_cc
        self.nc = bass.Bass("TRN2", target_bir_lowering=False)
        self.q = {e: [] for e in ENGINES}
        self.dbg_outs = {}
        self._declare_dram()
        self._declare_sbuf()
        self._declare_sems()

    def add(self, e, f):
        self.q[e].append(f)

    def dma(self, qeng, out, in_, sem=None, **kw):
        if sem is None:
            sem = self.S_sq if qeng == "sync" else self.S_gq
        sem.v += 16
        def f(e, out=out, in_=in_, kw=kw, s=sem.h):
            e.dma_start(out=out, in_=in_, **kw).then_inc(s, 16)
        self.add(qeng, f)
        return sem.v

    def W(self, eng, sem, tgt):
        if tgt and tgt > 0:
            self.add(eng, lambda e, s=sem.h, t=tgt: e.wait_ge(s, t))

    def OP(self, eng, fn, sem=None, drain=False):
        if sem is not None:
            sem.v += 1
        tgt = sem.v if sem is not None else None
        def f(e, fn=fn, s=(sem.h if sem else None), dr=drain):
            if dr:
                e.drain()
            ins = fn(e)
            if s is not None:
                ins.then_inc(s, 1)
        self.add(eng, f)
        return tgt

    FM_SHAPES = {"Wx": (D, D), "Wr": (D, D), "Wk": (D, D), "Wo": (D, D),
                 "WqT": (D, D), "WkT": (D, D), "w1": (D, FF), "w2": (FF, D),
                 "tw1": (D, FF), "tw2": (D, FF), "th1": (FF, FF), "th2": (FF, FF),
                 "tw3": (FF, D)}
    TOK_NAMES = ("Ww", "Wv", "WvT")

    def _declare_dram(self):
        nc, T = self.nc, self.T
        def din(name, shape, dtype=F32):
            return nc.dram_tensor(name, list(shape), dtype, kind="ExternalInput")
        self.x_in = din("x", [T, D])
        self.w_in, self.wbf = {}, {}
        for nm, (DIN_, DOUT_) in self.FM_SHAPES.items():
            shape = [128, DOUT_ // 128, DIN_]
            self.w_in[nm] = din(nm, shape)
            self.wbf[nm] = nc.dram_tensor("b_" + nm, shape, BF16, kind="Internal")
        for nm in self.TOK_NAMES:
            shape = [128, 2, 4096]
            self.w_in[nm] = din(nm, shape)
            self.wbf[nm] = nc.dram_tensor("b_" + nm, shape, BF16, kind="Internal")
        self.v_lng = din("v_lng", [128, DK]); self.v_lnb = din("v_lnb", [128, DK])
        self.r_bw_in = din("r_bw", [1, D])
        self.v_b1 = din("v_b1", [128, FK]); self.v_b2 = din("v_b2", [128, DK])
        self.v_s1 = din("v_s1", [128, FK]); self.v_s2 = din("v_s2", [128, FK])
        self.v_sh1 = din("v_sh1", [128, FK]); self.v_sh2 = din("v_sh2", [128, FK])
        self.v_s3 = din("v_s3", [128, DK])
        self.identf_in = din("identf", [128, 128])
        self.identb_in = din("identb", [128, 128], BF16)
        self.u64_in = din("u64f", [128, CH])
        self.mask_in = din("scanmask", [128, 1024], U8)
        self.oh4_in = din("oh4", [128, 4])
        self.x1p_d = nc.dram_tensor("x1p_d", [D, T], F32, kind="Internal")
        self.x1_d = nc.dram_tensor("x1_d", [D, T], F32, kind="Internal")
        self.q_d = nc.dram_tensor("q_d", [D, T], BF16, kind="Internal")
        self.lT_d = nc.dram_tensor("lT_d", [self.NQT, 128], F32, kind="Internal")
        self.ccF_in = nc.dram_tensor("ccF_in", [128, 520], F32, kind="Internal")
        self.ccF_out = nc.dram_tensor("ccF_out", [4 * 128, 520], F32, kind="Internal")
        self.ccK_in = nc.dram_tensor("ccK_in", [D, T], BF16, kind="Internal")
        self.ccK_out = nc.dram_tensor("ccK_out", [4 * D, T], BF16, kind="Internal")
        self.ccV_in = nc.dram_tensor("ccV_in", [T, D], BF16, kind="Internal")
        self.ccV_out = nc.dram_tensor("ccV_out", [4 * T, D], BF16, kind="Internal")
        self.out_d = nc.dram_tensor("out", [D, T], F32, kind="ExternalOutput")

    def ddbg(self, name, shape, dtype=F32):
        if name in self.dbg and name not in self.dbg_outs:
            self.dbg_outs[name] = self.nc.dram_tensor("dbg_" + name, list(shape), dtype,
                                                      kind="ExternalOutput")
        return self.dbg_outs.get(name)

    def dump(self, name, buf_ap):
        if name not in self.dbg:
            return
        if name not in self.dbg_outs:
            self.dbg_outs[name] = self.nc.dram_tensor(
                "dbg_" + name, list(buf_ap.shape), buf_ap.dtype, kind="ExternalOutput")
        dd = self.dbg_outs[name]
        for sem in (self.S_te, self.S_ve, self.S_se):
            self.W("sync", sem, sem.v)
        self.W("sync", self.S_sq, self.S_sq.v)
        t = self.dma("sync", dd.ap(), buf_ap)
        for eng in ("tensor", "vector", "scalar", "gpsimd"):
            self.W(eng, self.S_sq, t)

    def _declare_sbuf(self):
        nc, T, TG, TT4, NCH, NQT = self.nc, self.T, self.TG, self.TT4, self.NCH, self.NQT
        ARENA = 207 * 1024
        arena = nc.alloc_sbuf_tensor("arena", [128, ARENA], U8)
        base = nc.lookup_mloc(arena).addr
        self._arena_sz = ARENA
        TOP = [0]
        def sb(name, shape, dtype, at=None):
            per = int(np.prod(shape[1:])) * mybir.dt.size(dtype)
            per = (per + 31) & ~31
            if at is None:
                at = TOP[0]
                TOP[0] = at + per
            else:
                TOP[0] = max(TOP[0], at + per)
            return nc.alloc_sbuf_tensor_at(name, list(shape), dtype, offset=base + at)

        self.c_identf = sb("c_identf", [128, 128], F32)
        self.c_identb = sb("c_identb", [128, 128], BF16)
        self.c_u64 = sb("c_u64", [128, CH], F32)
        self.c_mask = sb("c_mask", [128, 1024], U8)
        self.c_oh4 = sb("c_oh4", [128, 4], F32)
        self.c_ones1f = sb("c_ones1f", [1, 128], F32)
        self.c_onescb = sb("c_onescb", [128, 1], BF16)
        self.s_lng = sb("s_lng", [128, DK], F32)
        self.s_lnb = sb("s_lnb", [128, DK], F32)
        self.s_b1 = sb("s_b1", [128, FK], F32)
        self.s_b2 = sb("s_b2", [128, DK], F32)
        self.s_s1 = sb("s_s1", [128, FK], F32)
        self.s_s2 = sb("s_s2", [128, FK], F32)
        self.s_sh1 = sb("s_sh1", [128, FK], F32)
        self.s_sh2 = sb("s_sh2", [128, FK], F32)
        self.s_s3 = sb("s_s3", [128, DK], F32)
        self.bc_bw = sb("bc_bw", [128, 1024], BF16)
        self.r_mean = sb("r_mean", [1, TG], F32)
        self.r_rstd = sb("r_rstd", [1, TG], F32)
        self.r_mrstd = sb("r_mrstd", [1, TG], F32)
        self.r_tmp = sb("r_tmp", [1, TG], F32)
        self.bc_rstd = sb("bc_rstd", [128, TG], BF16)
        self.bc_mrstd = sb("bc_mrstd", [128, TG], BF16)
        self.stg = [sb("stg0", [128, 512], F32), sb("stg1", [128, 512], F32)]
        self._stg0_off = None
        self.x1l = [sb("x1l0", [128, 512], F32), sb("x1l1", [128, 512], F32)]
        self.sqb = [sb("sqb0", [128, 512], BF16), sb("sqb1", [128, 512], BF16)]
        NMB = 4 * T // self.KMB
        LP = NQT * NMB
        self.misc = sb("misc", [128, LP + 5 * NQT + 16], F32)
        self.r_bwrow = None
        m = self.misc.ap()
        self.v_lparts = m[:, 0:LP]
        o = LP
        self.v_qn2 = m[:, o:o + NQT]; o += NQT
        self.v_negB = m[:, o:o + NQT]; o += NQT
        self.v_lsum = m[:, o:o + NQT]; o += NQT
        self.v_linv = m[:, o:o + NQT]; o += NQT
        self.v_Mb = m[:, o:o + 1]

        SCR0 = TOP[0]
        self.s_arg = sb("s_arg", [128, 512], F32)
        self.s_eL = sb("s_eL", [128, 512], BF16)
        self.s_eN = sb("s_eN", [128, 512], BF16)
        self.s_rtl = sb("s_rtl", [128, 2 * 512], BF16)
        self.s_ktl = sb("s_ktl", [128, 2 * 512], BF16)
        self.s_kbr = sb("s_kbr", [128, 2 * 512], BF16)
        self.s_mSc = sb("s_mSc", [128, 1024], BF16)
        self.s_kbtok = sb("s_kbtok", [128, 1024], BF16)
        self.s_S = sb("s_S", [128, 512], F32)
        self.s_Sbf = sb("s_Sbf", [128, 512], BF16)
        self.s_Dpre = sb("s_Dpre", [128, NCH * 8], F32)
        self.s_bb = sb("s_bb", [128, 8], F32)
        self.s_pch = sb("s_pch", [128, 8], F32)
        self.s_pcf = sb("s_pcf", [128, 2 * 8], F32)
        self.s_Dcur = sb("s_Dcur", [128, 8], F32)
        SCR_END = TOP[0]
        KMB = self.KMB
        NKC = KMB // 128
        off = SCR0
        self.s_Kmb = sb("s_Kmb", [128, 2 * DK * KMB], BF16, at=off); off += 2 * DK * KMB * 2
        self.s_Vmb = sb("s_Vmb", [128, 2 * NKC * 1024], BF16, at=off); off += 2 * NKC * 1024 * 2
        self.s_PT = sb("s_PT", [128, NKC * 512], BF16, at=off); off += NKC * 512 * 2
        TOP[0] = max(SCR_END, off)

        self.r_bwrow = nc.alloc_sbuf_tensor_at("r_bwrow", [1, D], F32,
                                                offset=nc.lookup_mloc(self.stg[0]).addr)
        self.wb = [sb("wb0", [128, FF], BF16), sb("wb1", [128, FF], BF16)]
        self.wbBIG = nc.alloc_sbuf_tensor_at("wbBIG", [128, 2 * FF], BF16,
                                             offset=base + TOP[0] - 2 * FF * 2)

        RA0 = TOP[0]
        self.s_xn2g = sb("s_xn2g", [128, DK * TG], BF16, at=RA0)
        self.s_t1g = sb("s_t1g", [128, DK * TG], BF16, at=RA0 + DK * TG * 2)
        self.s_kTg = sb("s_kTg", [128, DK * TG], BF16, at=RA0 + 2 * DK * TG * 2)
        self.s_rTg = sb("s_rTg", [128, DK * TG], BF16, at=RA0 + 3 * DK * TG * 2)
        self.s_hg = sb("s_hg", [128, FK * TG], BF16, at=RA0)
        self.s_xn1 = sb("s_xn1", [128, DK * T], BF16, at=RA0)
        self.s_q = self.s_xn1
        TOP[0] = RA0 + max(4 * DK * TG * 2, FK * TG * 2, DK * T * 2)
        RB0 = TOP[0]
        self.s_ypre = sb("s_ypre", [128, DK * T], BF16, at=RB0)
        self.s_u2 = sb("s_u2", [128, FK * TG], BF16, at=RB0)
        self.s_accL = sb("s_accL", [128, 4 * T], F32, at=RB0)
        TOP[0] = RB0 + max(DK * T * 2, FK * TG * 2, 4 * T * 4)
        RC0 = TOP[0]
        self.s_rtlp = sb("s_rtlp", [128, DK * T], BF16, at=RC0)
        self.s_yffn1 = sb("s_yffn1", [128, DK * T], BF16, at=RC0)
        TOP[0] = RC0 + DK * T * 2
        RD0 = TOP[0]
        self.s_xTg = sb("s_xTg", [128, DK * TG], F32, at=RD0)
        self.s_xng = sb("s_xng", [128, DK * TG], BF16, at=RD0 + DK * TG * 4)
        self.s_vtg = sb("s_vtg", [128, TT4 * 1024], BF16, at=RD0 + DK * TG * 4 + DK * TG * 2)
        self.s_gbuf = sb("s_gbuf", [128, 4 * 520], F32, at=RD0)
        self.s_Sin = sb("s_Sin", [128, 512], F32, at=RD0 + 8320)
        self.s_Sinb = sb("s_Sinb", [128, 512], BF16, at=RD0 + 8320 + 2048)
        self.s_Spre = sb("s_Spre", [128, 512], F32, at=RD0 + 8320 + 3072)
        self.s_rpp = sb("s_rpp", [128, 512], BF16, at=RD0 + 8320 + 5120)
        self.s_u1 = sb("s_u1", [128, FK * TG], BF16, at=RD0)
        self.s_accH = sb("s_accH", [128, 4 * T], F32, at=RD0)
        TOP[0] = RD0 + max(DK * TG * 4 + DK * TG * 2 + TT4 * 1024 * 2, FK * TG * 2, 4 * T * 4)
        RL0 = TOP[0]
        self.s_latg = sb("s_latg", [128, TT4 * 1024], F32, at=RL0)
        self.s_xrow = sb("s_xrow", [128, TT4 * 1024], F32, at=RL0)
        self.s_x1tw = sb("s_x1tw", [128, DK * 512], F32, at=RL0)
        self.wbRL = nc.alloc_sbuf_tensor_at("wbRL", [128, 2 * FF], BF16, offset=base + RL0)
        self.s_linvb = nc.alloc_sbuf_tensor_at("s_linvb", [128, 512], F32, offset=base + RL0)
        TOP[0] = RL0 + max(TT4 * 1024 * 4, DK * 512 * 4)
        assert TOP[0] <= ARENA, f"sbuf over arena: {TOP[0]} > {ARENA}"
        self.sbuf_used = TOP[0]

        self.P8 = nc.alloc_psum_tensor("P8", [128, 4096], F32)

        def v3(t, c, w):
            return t.ap().rearrange("p (c w) -> p c w", c=c)
        T_, TG_ = T, TG
        self.xn2g3 = v3(self.s_xn2g, DK, TG_); self.t1g3 = v3(self.s_t1g, DK, TG_)
        self.kTg3 = v3(self.s_kTg, DK, TG_); self.rTg3 = v3(self.s_rTg, DK, TG_)
        self.xng3 = v3(self.s_xng, DK, TG_); self.xTg3 = v3(self.s_xTg, DK, TG_)
        self.vtg3 = v3(self.s_vtg, TT4, 1024); self.latg3 = v3(self.s_latg, TT4, 1024)
        self.xrow3 = v3(self.s_xrow, TT4, 1024)
        self.ypre3 = v3(self.s_ypre, DK, T_); self.rtlp3 = v3(self.s_rtlp, DK, T_)
        self.hg3 = v3(self.s_hg, FK, TG_)
        self.xn13 = v3(self.s_xn1, DK, T_); self.q3 = v3(self.s_q, DK, T_)
        self.u13 = v3(self.s_u1, FK, TG_); self.u23 = v3(self.s_u2, FK, TG_)
        self.yffn13 = v3(self.s_yffn1, DK, T_)
        self.eL3 = v3(self.s_eL, 8, CH); self.eN3 = v3(self.s_eN, 8, CH)
        self.rtl4 = self.s_rtl.ap().rearrange("p (b c w) -> p b c w", b=2, c=8)
        self.ktl4 = self.s_ktl.ap().rearrange("p (b c w) -> p b c w", b=2, c=8)
        self.kbr4 = self.s_kbr.ap().rearrange("p (b c w) -> p b c w", b=2, c=8)
        self.pcf3 = self.s_pcf.ap().rearrange("p (b c) -> p b c", b=2)
        self.kbtok3 = v3(self.s_kbtok, 8, 128)
        self.S3 = v3(self.s_S, 8, HS); self.Sbf3 = v3(self.s_Sbf, 8, HS)
        self.Sin3 = v3(self.s_Sin, 8, HS); self.Sinb3 = v3(self.s_Sinb, 8, HS)
        self.Spre3 = v3(self.s_Spre, 8, HS)
        self.Dpre3 = v3(self.s_Dpre, NCH, 8)
        self.rpp3 = v3(self.s_rpp, 8, CH)
        self.Kmb4 = self.s_Kmb.ap().rearrange("p (b c w) -> p b c w", b=2, c=DK)
        self.Vmb4 = self.s_Vmb.ap().rearrange("p (b c w) -> p b c w", b=2, c=self.KMB // 128)
        self.PT3 = v3(self.s_PT, self.KMB // 128, 512)
        self.accL3 = v3(self.s_accL, 4, T_); self.accH3 = v3(self.s_accH, 4, T_)
        self.x1tw3 = v3(self.s_x1tw, DK, 512)
        self.gbuf3 = v3(self.s_gbuf, 4, 520)
        self.lp3 = self.v_lparts.rearrange("p (qt mb) -> p qt mb", qt=NQT)
        self.arg3 = self.s_arg.ap().rearrange("p (c w) -> p c w", c=8)
        self.mSc = self.s_mSc.ap()

    def pbank(self, i):
        return self.P8.ap()[:, 512 * i:512 * (i + 1)]

    def _declare_sems(self):
        nc = self.nc
        self.S_sq, self.S_gq = Sem(nc, "sq"), Sem(nc, "gq")
        self.S_te, self.S_ve, self.S_se = Sem(nc, "te"), Sem(nc, "ve"), Sem(nc, "se")
        self.S_ge = Sem(nc, "ge")
        self.S_cc1, self.S_cc2 = Sem(nc, "cc1"), Sem(nc, "cc2")
        self.S_wbs = [Sem(nc, "wb0s"), Sem(nc, "wb1s")]
        self.S_w2s = [Sem(nc, "w2s0"), Sem(nc, "w2s1")]
        self.S_cast = Sem(nc, "cast")
        self.S_wbB = Sem(nc, "wbBs")
        self.S_xrow = Sem(nc, "xrow")
        self.S_stgs = [Sem(nc, "stg0s"), Sem(nc, "stg1s")]
        self.S_x1ls = [Sem(nc, "x1l0s"), Sem(nc, "x1l1s")]
        self.S_sqbs = [Sem(nc, "sqb0s"), Sem(nc, "sqb1s")]
        self.S_eld = Sem(nc, "eld")
        self.S_ktld = Sem(nc, "ktld")
        self.S_nrm = Sem(nc, "nrm")
        self.S_gbuf = Sem(nc, "gbufs")
        self.S_kmb = Sem(nc, "kmbs")
        self.S_vmb = Sem(nc, "vmbs")
        self.S_mld = Sem(nc, "mlds")
        self.S_qld = Sem(nc, "qlds")
        self.S_lt = Sem(nc, "lts")
        self.S_pub = Sem(nc, "pubs")
        self.ALL_DMA_SEMS = ([self.S_sq, self.S_gq, self.S_wbB, self.S_xrow, self.S_eld,
                              self.S_ktld, self.S_nrm, self.S_gbuf, self.S_kmb, self.S_vmb,
                              self.S_mld, self.S_qld, self.S_lt, self.S_pub, self.S_cast]
                             + self.S_wbs + self.S_w2s + self.S_stgs + self.S_x1ls + self.S_sqbs)
        self.bankg = [(self.S_ve, 0)] * 8
        self.wbg = [(self.S_te, 0), (self.S_te, 0)]
        self.wbg_big = (self.S_te, 0)
        self.stgg = [(self.S_se, 0), (self.S_se, 0)]

    def wait_bank(self, i, eng="tensor"):
        sem, tgt = self.bankg[i]
        self.W(eng, sem, tgt)

    def emit_init(self):
        t_c = 0
        for dst, src in ((self.c_identf, self.identf_in), (self.c_identb, self.identb_in),
                         (self.c_u64, self.u64_in), (self.c_mask, self.mask_in),
                         (self.c_oh4, self.oh4_in), (self.s_lng, self.v_lng),
                         (self.s_lnb, self.v_lnb), (self.s_b1, self.v_b1), (self.s_b2, self.v_b2),
                         (self.s_s1, self.v_s1), (self.s_s2, self.v_s2), (self.s_sh1, self.v_sh1),
                         (self.s_sh2, self.v_sh2), (self.s_s3, self.v_s3),
                         (self.r_bwrow, self.r_bw_in)):
            t_c = self.dma("sync", dst.ap(), src.ap())
        self.T_CONST = t_c
        self.OP("gpsimd", lambda e: e.memset(self.c_ones1f.ap(), 1.0))
        self.T_GE_INIT = self.OP("gpsimd", lambda e: e.memset(self.c_onescb.ap(), 1.0),
                                 sem=self.S_ge, drain=True)
        self.W("gpsimd", self.S_sq, self.T_CONST)
        self.cast_tgt = {}
        for nm in ("Wx", "Ww", "Wr", "Wk", "Wv", "w1", "w2", "Wo", "WqT", "WkT",
                   "WvT"):
            self.emit_cast(nm)
        self.OP("vector", lambda e: e.memset(self.s_mSc.ap(), 0.0))
        self.OP("vector", lambda e: e.memset(self.s_S.ap(), 0.0), drain=True)
        self.OP("vector", lambda e: e.memset(self.s_Sbf.ap(), 0.0), drain=True)
        self.T_VE_INIT = self.OP("vector", lambda e: e.memset(self.s_Dcur.ap(), 1.0),
                                 sem=self.S_ve, drain=True)
        self.W("tensor", self.S_sq, self.T_CONST)
        self.W("tensor", self.S_ge, self.T_GE_INIT)
        self.OP("tensor", lambda e: e.matmul(self.pbank(0), self.c_ones1f.ap(),
                                             self.r_bwrow.ap()[:, 0:512], start=True, stop=True))
        t = self.OP("tensor", lambda e: e.matmul(self.pbank(1), self.c_ones1f.ap(),
                                                 self.r_bwrow.ap()[:, 512:1024], start=True, stop=True),
                    sem=self.S_te)
        self.W("vector", self.S_te, t)
        self.OP("vector", lambda e: e.tensor_copy(self.bc_bw.ap()[:, 0:512], self.pbank(0)), drain=True)
        t2 = self.OP("vector", lambda e: e.tensor_copy(self.bc_bw.ap()[:, 512:1024], self.pbank(1)),
                     sem=self.S_ve, drain=True)
        self.bankg[0] = (self.S_ve, t2)
        self.bankg[1] = (self.S_ve, t2)

    def emit_cast(self, nm):
        src, dst = self.w_in[nm], self.wbf[nm]
        NOB, cpo = src.shape[1], src.shape[2]
        chunk = max(1, (16 * 1024 * 1024) // (128 * cpo * 4))
        t = 0
        for o0 in range(0, NOB, chunk):
            o1 = min(NOB, o0 + chunk)
            t = self.dma("gpsimd", dst.ap()[:, o0:o1, :], src.ap()[:, o0:o1, :],
                         sem=self.S_cast)
        self.cast_tgt[nm] = t

    def fm_proj(self, key, DIN, DOUT, rhs_fn, epi, rhs_waits=(), bank_base=0, n_tok=None):
        KIN = DIN // 128
        n = n_tok or 512
        cpo = KIN * 128
        st = self.stream
        last = None
        ob = 0
        while ob < DOUT // 128:
            i, par, t_w, (_k, o0, o1, _c) = st.use(key)
            assert o0 == ob and _c == cpo
            t_mm = None
            for j in range(o0, o1):
                if j == o0:
                    self.W("tensor", st.sems[par], t_w)
                if j == 0:
                    for sem, tgt in rhs_waits:
                        self.W("tensor", sem, tgt)
                bi = bank_base + j % 2
                self.wait_bank(bi)
                bank = self.pbank(bi)[:, 0:n]
                base = (j - o0) * cpo
                for kb in range(KIN):
                    t_mm = self.OP("tensor", (lambda e, b=bank,
                                              w=st.bufs[par].ap()[:, base + kb * 128:base + (kb + 1) * 128],
                                              r=rhs_fn(kb),
                                              st_=(kb == 0), sp=(kb == KIN - 1):
                                              e.matmul(b, w, r, start=st_, stop=sp)),
                                   sem=self.S_te if kb == KIN - 1 else None)
                last = epi(j, bank, t_mm)
                self.bankg[bi] = last
            st.done(i, (self.S_te, t_mm))
            ob = o1
        return last

    def tok_proj(self, key, lhs_waits, lhs_fn, epi, bank_base=0):
        st = self.stream
        halves = [st.use(key), st.use(key, la=False)]
        for h, (i, par, t_w, _s) in enumerate(halves):
            self.W("tensor", st.sems[par], t_w)
        for sem, tgt in lhs_waits:
            self.W("tensor", sem, tgt)
        t_last = None
        for tt in range(self.TT4):
            for oh in range(2):
                bi = bank_base + (tt * 2 + oh) % 2
                self.wait_bank(bi)
                bank = self.pbank(bi)
                t_mm = None
                for kb in range(DK):
                    par = halves[kb // 4][1]
                    t_mm = self.OP("tensor", (lambda e, b=bank, w=lhs_fn(kb, tt),
                                              r=st.bufs[par].ap()[:, (kb % 4) * 1024 + oh * 512:
                                                                  (kb % 4) * 1024 + (oh + 1) * 512],
                                              st_=(kb == 0), sp=(kb == DK - 1):
                                              e.matmul(b, w, r, start=st_, stop=sp)),
                                   sem=self.S_te if kb == DK - 1 else None)
                t_last = t_mm
                self.bankg[bi] = epi(tt, oh, bank, t_mm)
        for i, par, t_w, _s in halves:
            st.done(i, (self.S_te, t_last))

    def ln_stats(self, srcb_fn, n, sq_waits=()):
        self.wait_bank(6)
        self.wait_bank(7)
        t_sq_guard = [0, 0]
        for kb in range(DK):
            for sem, tgt in (sq_waits if kb == 0 else ()):
                self.W("scalar", sem, tgt)
                self.W("tensor", sem, tgt)
            if kb >= 2:
                self.W("scalar", self.S_te, t_sq_guard[kb % 2])
            t_s = self.OP("scalar", (lambda e, o=self.sqb[kb % 2].ap()[:, 0:n], i=srcb_fn(kb):
                                     e.activation(o, i, AF.Square)), sem=self.S_se, drain=True)
            self.OP("tensor", (lambda e, b=self.pbank(6)[0:1, 0:n], r=srcb_fn(kb),
                               st=(kb == 0), sp=(kb == DK - 1):
                               e.matmul(b, self.c_onescb.ap(), r, start=st, stop=sp)))
            self.W("tensor", self.S_se, t_s)
            t_sq_guard[kb % 2] = self.OP("tensor", (lambda e, b=self.pbank(7)[0:1, 0:n],
                                                    r=self.sqb[kb % 2].ap()[:, 0:n],
                                                    st=(kb == 0), sp=(kb == DK - 1):
                                                    e.matmul(b, self.c_onescb.ap(), r, start=st, stop=sp)),
                                         sem=self.S_te)
        t_mm_done = t_sq_guard[(DK - 1) % 2]
        self.W("vector", self.S_te, t_mm_done)
        mean, rstd, mrstd, tmp = self.r_mean.ap()[:, 0:n], self.r_rstd.ap()[:, 0:n], \
            self.r_mrstd.ap()[:, 0:n], self.r_tmp.ap()[:, 0:n]
        self.OP("vector", lambda e: e.tensor_scalar(mean, self.pbank(6)[0:1, 0:n], 1.0 / D, None, ALU.mult))
        self.OP("vector", lambda e: e.tensor_scalar(tmp, self.pbank(7)[0:1, 0:n], 1.0 / D, EPS, ALU.mult, ALU.add), drain=True)
        self.OP("vector", lambda e: e.tensor_tensor(rstd, mean, mean, ALU.mult), drain=True)
        t_v = self.OP("vector", lambda e: e.tensor_tensor(tmp, tmp, rstd, ALU.subtract),
                      sem=self.S_ve, drain=True)
        self.W("scalar", self.S_ve, t_v)
        self.OP("scalar", lambda e: e.activation(tmp, tmp, AF.Ln), drain=True)
        t_r = self.OP("scalar", lambda e: e.activation(rstd, tmp, AF.Exp, scale=-0.5),
                      sem=self.S_se, drain=True)
        self.W("vector", self.S_se, t_r)
        t_v2 = self.OP("vector", lambda e: e.tensor_tensor(mrstd, mean, rstd, ALU.mult),
                       sem=self.S_ve, drain=True)
        self.W("tensor", self.S_ve, t_v2)
        self.OP("tensor", lambda e: e.matmul(self.pbank(6)[:, 0:n], self.c_ones1f.ap(), rstd, start=True, stop=True))
        t_b = self.OP("tensor", lambda e: e.matmul(self.pbank(7)[:, 0:n], self.c_ones1f.ap(), mrstd, start=True, stop=True),
                      sem=self.S_te)
        self.W("vector", self.S_te, t_b)
        self.OP("vector", lambda e: e.tensor_copy(self.bc_rstd.ap()[:, 0:n], self.pbank(6)[:, 0:n]), drain=True)
        t_e = self.OP("vector", lambda e: e.tensor_copy(self.bc_mrstd.ap()[:, 0:n], self.pbank(7)[:, 0:n]),
                      sem=self.S_ve, drain=True)
        self.bankg[6] = (self.S_ve, t_e)
        self.bankg[7] = (self.S_ve, t_e)
        return t_e

    def ln_apply(self, srcb_fn, dst_fn, n, affine, t_bc):
        t_last = None
        for kb in range(DK):
            if kb == 0:
                self.W("vector", self.S_ve, t_bc)
            self.OP("vector", (lambda e, o=dst_fn(kb), i=srcb_fn(kb):
                               e.tensor_tensor(o, i, self.bc_rstd.ap()[:, 0:n], ALU.mult)), drain=True)
            if affine:
                self.OP("vector", (lambda e, o=dst_fn(kb):
                                   e.tensor_tensor(o, o, self.bc_mrstd.ap()[:, 0:n], ALU.subtract)), drain=True)
                t_last = self.OP("vector", (lambda e, o=dst_fn(kb), g=self.s_lng.ap()[:, kb:kb + 1],
                                            b=self.s_lnb.ap()[:, kb:kb + 1]:
                                            e.tensor_scalar(o, o, g, b, ALU.mult, ALU.add)),
                                 sem=self.S_ve, drain=True)
            else:
                t_last = self.OP("vector", (lambda e, o=dst_fn(kb):
                                            e.tensor_tensor(o, o, self.bc_mrstd.ap()[:, 0:n], ALU.subtract)),
                                 sem=self.S_ve, drain=True)
        return t_last

    def emit_l0_group(self, g):
        TG, TT4 = self.TG, self.TT4
        t0 = g * TG
        self.W("sync", self.S_te, self.t_latg_free)
        self.W("sync", self.S_ve, self.t_ypre_ve)
        t_x = self.dma("sync", self.xrow3,
                       self.x_in.ap()[t0:t0 + TG, :].rearrange("(tt p) d -> p tt d", p=128),
                       sem=self.S_xrow)
        self.W("tensor", self.S_xrow, t_x)
        self.W("tensor", self.S_ve, self.t_xT_free)
        t_ev = 0
        for tt in range(TT4):
            for dk in range(DK):
                bi = 2 + (tt * DK + dk) % 2
                self.wait_bank(bi)
                t_tr = self.OP("tensor", (lambda e, b=self.pbank(bi)[:, 0:128],
                                          src=self.xrow3[:, tt, dk * 128:(dk + 1) * 128]:
                                          e.transpose(b, src, self.c_identf.ap())), sem=self.S_te)
                self.W("vector", self.S_te, t_tr)
                t_ev = self.OP("vector", (lambda e, o=self.xTg3[:, dk, tt * 128:(tt + 1) * 128],
                                          b=self.pbank(bi)[:, 0:128]:
                                          e.tensor_copy(o, b)), sem=self.S_ve, drain=True)
                self.bankg[bi] = (self.S_ve, t_ev)
        t_xb = 0
        for dk in range(DK):
            t_xb = self.OP("vector", (lambda e, o=self.xng3[:, dk, :], i=self.xTg3[:, dk, :]:
                                      e.tensor_copy(o, i)), sem=self.S_ve, drain=True)
        t_bc = self.ln_stats(lambda kb: self.xng3[:, kb, :], TG, sq_waits=((self.S_ve, t_xb),))
        t_xn = self.ln_apply(lambda kb: self.xng3[:, kb, :], lambda kb: self.xng3[:, kb, :],
                             TG, affine=False, t_bc=t_bc)
        t_bc2 = self.ln_stats(lambda kb: self.xng3[:, kb, :], TG, sq_waits=((self.S_ve, t_xn),))
        t_xn2 = self.ln_apply(lambda kb: self.xng3[:, kb, :], lambda kb: self.xn2g3[:, kb, :],
                              TG, affine=True, t_bc=t_bc2)

        def epi_copy(dst3, func=AF.Copy):
            def epi(ob, bank, t_mm):
                self.W("scalar", self.S_te, t_mm)
                t_e = self.OP("scalar", (lambda e, o=dst3[:, ob, :], b=bank:
                                         e.activation(o, b, func)), sem=self.S_se, drain=True)
                return (self.S_se, t_e)
            return epi
        self.fm_proj("Wx", D, D, lambda kb: self.xn2g3[:, kb, :], epi_copy(self.t1g3),
                     rhs_waits=((self.S_ve, t_xn2),))
        t_t1 = self.S_se.v
        def epi_z(tt, oh, bank, t_mm):
            self.W("vector", self.S_te, t_mm)
            ssem, stgt = self.stgg[oh]
            self.W("vector", ssem, stgt)
            t_v = self.OP("vector", (lambda e, o=self.stg[oh].ap(), b=bank,
                                     c=self.bc_bw.ap()[:, oh * 512:(oh + 1) * 512]:
                                     e.tensor_tensor(o, b, c, ALU.add)), sem=self.S_ve, drain=True)
            self.W("scalar", self.S_ve, t_v)
            t_s = self.OP("scalar", (lambda e, o=self.latg3[:, tt, oh * 512:(oh + 1) * 512],
                                     i=self.stg[oh].ap():
                                     e.activation(o, i, AF.Sigmoid, scale=-1.0)),
                          sem=self.S_se, drain=True)
            self.stgg[oh] = (self.S_se, t_s)
            return (self.S_ve, t_v)
        self.tok_proj("Ww", ((self.S_se, t_t1),),
                      lambda kb, tt: self.t1g3[:, kb, tt * 128:(tt + 1) * 128], epi_z)
        self.fm_proj("Wr", D, D, lambda kb: self.xn2g3[:, kb, :], epi_copy(self.rTg3, AF.Sigmoid))
        t_la = None
        for tt in range(TT4):
            for oh in range(2):
                t_la = self.OP("scalar", (lambda e, o=self.latg3[:, tt, oh * 512:(oh + 1) * 512]:
                                          e.activation(o, o, AF.Ln)), sem=self.S_se, drain=True)
        self.t_la = t_la
        self.fm_proj("Wk", D, D, lambda kb: self.xn2g3[:, kb, :], epi_copy(self.kTg3))
        def epi_v(tt, oh, bank, t_mm):
            self.W("vector", self.S_te, t_mm)
            t_e = self.OP("vector", (lambda e, o=self.vtg3[:, tt, oh * 512:(oh + 1) * 512], b=bank:
                                     e.tensor_copy(o, b)), sem=self.S_ve, drain=True)
            self.t_vtg = t_e
            return (self.S_ve, t_e)
        self.tok_proj("Wv", (), lambda kb, tt: self.xn2g3[:, kb, tt * 128:(tt + 1) * 128], epi_v)
        if g == 0:
            self.dump("xT", self.s_xTg.ap())
            self.dump("xn", self.s_xng.ap())
            self.dump("xn2", self.s_xn2g.ap())
            self.dump("t1", self.s_t1g.ap())
            self.dump("la", self.s_latg.ap())
            self.dump("kT", self.s_kTg.ap())
            self.dump("rT", self.s_rTg.ap())
            self.dump("vt", self.s_vtg.ap())

    def emit_scan_h1(self, g, ct, cg):
        par = ct % 2
        b0 = 64 * par
        tt = ct // 2
        PL = 0
        self.wait_bank(PL)
        self.W("tensor", self.S_se, self.t_la)
        t_cum = None
        for dk in range(DK):
            t_cum = self.OP("tensor", (lambda e, b=self.pbank(PL)[:, dk * 64:(dk + 1) * 64],
                                       w=self.latg3[b0:b0 + 64, tt, dk * 128:(dk + 1) * 128],
                                       r=self.c_u64.ap()[b0:b0 + 64, :],
                                       st=(dk == 0), sp=(dk == DK - 1):
                                       e.matmul(b, w, r, start=st, stop=sp)),
                            sem=self.S_te if dk == DK - 1 else None)
        self.t_cum_last = t_cum
        self.W("vector", self.S_te, t_cum)
        self.W("vector", self.S_se, self.t_exp_prev)
        PL3 = self.pbank(PL).rearrange("p (c w) -> p c w", c=8)
        self.OP("vector", lambda e: e.tensor_scalar(self.s_bb.ap(), PL3[:, :, 63:64], 0.5, None, ALU.mult), drain=True)
        t_arg = self.OP("vector", (lambda e:
                        e.tensor_tensor(self.arg3, PL3,
                                        self.s_bb.ap()[:, :, None].broadcast_to([128, 8, CH]),
                                        ALU.subtract)), sem=self.S_ve, drain=True)
        self.bankg[PL] = (self.S_ve, t_arg)
        self.W("scalar", self.S_ve, t_arg)
        self.W("scalar", self.S_ve, self.t_fac_prev)
        self.OP("scalar", lambda e: e.activation(self.s_eL.ap(), self.s_arg.ap(), AF.Exp), drain=True)
        self.OP("scalar", lambda e: e.activation(self.s_eN.ap(), self.s_arg.ap(), AF.Exp, scale=-1.0), drain=True)
        self.OP("scalar", lambda e: e.activation(self.s_pch.ap(), self.s_bb.ap(), AF.Exp), drain=True)
        t_exp = self.OP("scalar", (lambda e, par=par: e.activation(
            self.pcf3[:, par, :], self.s_bb.ap(), AF.Exp, scale=2.0)),
            sem=self.S_se, drain=True)
        self.t_exp_prev = t_exp
        self.W("vector", self.S_se, t_exp)
        self.W("vector", self.S_te, self.scan_te_g[par])
        pch_b = self.s_pch.ap()[:, :, None].broadcast_to([128, 8, CH])
        self.OP("vector", (lambda e, par=par: e.tensor_tensor(
            self.rtl4[:, par], self.rTg3[:, :, ct * CH:(ct + 1) * CH], self.eL3, ALU.mult)), drain=True)
        self.OP("vector", (lambda e, par=par: e.tensor_tensor(
            self.ktl4[:, par], self.kTg3[:, :, ct * CH:(ct + 1) * CH], self.eN3, ALU.mult)), drain=True)
        self.OP("vector", (lambda e, par=par, p=pch_b: e.tensor_tensor(
            self.kbr4[:, par], self.ktl4[:, par], p, ALU.mult)), drain=True)
        t_fac = self.OP("vector", (lambda e, par=par, p=pch_b: e.tensor_tensor(
            self.rtlp3[:, :, cg * CH:(cg + 1) * CH], self.rtl4[:, par], p, ALU.mult)),
            sem=self.S_ve, drain=True)
        self.t_fac_prev = t_fac
        if not hasattr(self, "t_fac_of"):
            self.t_fac_of = {}
        self.t_fac_of[cg] = t_fac

    def emit_scan_h2(self, g, ct, cg):
        parts = getattr(self, "scan_parts", {"scores", "msk", "tr", "y", "state"})
        par = ct % 2
        b0 = 64 * par
        tt = ct // 2
        PL, PS1, PS2, PST = 0, 1, 2, 6
        PYA, PYB = 3, 4
        t_fac = self.t_fac_of[cg]
        if "scores" not in parts:
            return
        self.W("tensor", self.S_ve, t_fac)
        self.wait_bank(PS1)
        self.wait_bank(PS2)
        t_sc = None
        for h in range(H):
            bank = self.pbank(PS1 if h % 2 == 0 else PS2)
            hb = 64 * (h % 2)
            t_sc = self.OP("tensor", (lambda e, b=bank[b0:b0 + 64, 64 * (h // 2):64 * (h // 2) + 64],
                                      w=self.ktl4[hb:hb + 64, par, h // 2, :],
                                      r=self.rtl4[hb:hb + 64, par, h // 2, :],
                                      st=(h < 2), sp=(h >= H - 2):
                                      e.matmul(b, w, r, start=st, stop=sp)),
                           sem=self.S_te if h == H - 1 else None)
        if "msk" not in parts:
            return
        self.W("vector", self.S_te, t_sc)
        self.OP("vector", (lambda e: e.copy_predicated(self.mSc[b0:b0 + 64, 0:512],
                                                       self.c_mask.ap()[b0:b0 + 64, 0:512],
                                                       self.pbank(PS1)[b0:b0 + 64, :])), drain=True)
        t_msk = self.OP("vector", (lambda e: e.copy_predicated(self.mSc[b0:b0 + 64, 512:1024],
                                                               self.c_mask.ap()[b0:b0 + 64, 512:1024],
                                                               self.pbank(PS2)[b0:b0 + 64, :])),
                        sem=self.S_ve, drain=True)
        self.bankg[PS1] = (self.S_ve, t_msk)
        self.bankg[PS2] = (self.S_ve, t_msk)
        if cg == 0:
            self.dump("c0_mSc", self.s_mSc.ap())
        if "tr" not in parts:
            return
        t_kbt = 0
        t_ktr = None
        for kc in range(DK):
            bi = 5 if kc % 2 == 0 else 7
            self.wait_bank(bi)
            t_ktr = self.OP("tensor", (lambda e, b=self.pbank(bi).bitcast(BF16)[b0:b0 + 64, 0:128],
                                       src=self.kbr4[:, par, kc, :]:
                                       e.transpose(b, src, self.c_identb.ap())), sem=self.S_te)
            self.W("vector", self.S_te, t_ktr)
            t_kbt = self.OP("vector", (lambda e, o=self.kbtok3[b0:b0 + 64, kc, :],
                                       b=self.pbank(bi).bitcast(BF16)[b0:b0 + 64, 0:128]:
                                       e.tensor_copy(o, b)), sem=self.S_ve, drain=True)
            self.bankg[bi] = (self.S_ve, t_kbt)
        self.scan_te_g[par] = t_ktr
        if "y" not in parts:
            return
        self.W("tensor", self.S_ve, t_msk)
        self.W("tensor", self.S_ve, self.t_supd_prev)
        self.W("tensor", self.S_ve, self.t_vtg)
        self.wait_bank(PYA)
        self.wait_bank(PYB)
        t_y = None
        started = [False, False]
        for h in range(H):
            hb = 64 * (h % 2)
            yout = self.pbank(PYA)[hb:hb + 64, 64 * (h // 2):64 * (h // 2) + 64]
            st = not started[h % 2]
            started[h % 2] = True
            mcol = 512 * (h % 2) + 64 * (h // 2)
            self.OP("tensor", (lambda e, b=yout,
                               w=self.vtg3[b0:b0 + 64, tt, 64 * h:64 * h + 64],
                               r=self.mSc[b0:b0 + 64, mcol:mcol + 64],
                               st=st, sp=(h >= H - 2): e.matmul(b, w, r, start=st, stop=sp)))
        started = [False, False]
        for h in range(H):
            hb = 64 * (h % 2)
            yout = self.pbank(PYB)[hb:hb + 64, 64 * (h // 2):64 * (h // 2) + 64]
            st = not started[h % 2]
            started[h % 2] = True
            t_y = self.OP("tensor", (lambda e, b=yout,
                                     w=self.Sbf3[hb:hb + 64, h // 2, :],
                                     r=self.rtlp3[hb:hb + 64, h // 2, cg * CH:(cg + 1) * CH],
                                     st=st, sp=(h >= H - 2): e.matmul(b, w, r, start=st, stop=sp)),
                          sem=self.S_te if h == H - 1 else None)
        self.W("vector", self.S_te, t_y)
        PA3 = self.pbank(PYA).rearrange("p (c w) -> p c w", c=8)
        PB3 = self.pbank(PYB).rearrange("p (c w) -> p c w", c=8)
        self.OP("vector", (lambda e: e.tensor_copy(self.ypre3[:, :, cg * CH:(cg + 1) * CH], PA3)), drain=True)
        t_yev = self.OP("vector", (lambda e: e.tensor_tensor(self.ypre3[:, :, cg * CH:(cg + 1) * CH],
                                                             self.ypre3[:, :, cg * CH:(cg + 1) * CH],
                                                             PB3, ALU.add)),
                        sem=self.S_ve, drain=True)
        self.bankg[PYA] = (self.S_ve, t_yev)
        self.bankg[PYB] = (self.S_ve, t_yev)
        self.t_ypre_ve = t_yev
        if "state" not in parts:
            self.t_latg_free = self.t_cum_last
            return
        self.W("tensor", self.S_ve, t_kbt)
        self.wait_bank(PST)
        t_st = None
        started = [False, False]
        for h in range(H):
            hb = 64 * (h % 2)
            st = not started[h % 2]
            started[h % 2] = True
            t_st = self.OP("tensor", (lambda e, b=self.pbank(PST)[hb:hb + 64, 64 * (h // 2):64 * (h // 2) + 64],
                                      w=self.kbtok3[b0:b0 + 64, h // 2, 64 * (h % 2):64 * (h % 2) + 64],
                                      r=self.vtg3[b0:b0 + 64, tt, 64 * h:64 * h + 64],
                                      st=st, sp=(h >= H - 2):
                                      e.matmul(b, w, r, start=st, stop=sp)),
                           sem=self.S_te if h == H - 1 else None)
        self.W("vector", self.S_te, t_st)
        pcf_b = self.pcf3[:, par, :, None].broadcast_to([128, 8, HS])
        self.OP("vector", (lambda e: e.tensor_copy(self.Dpre3[:, cg, :], self.s_Dcur.ap())), drain=True)
        self.OP("vector", (lambda e, par=par: e.tensor_tensor(self.s_Dcur.ap(), self.s_Dcur.ap(),
                                                              self.pcf3[:, par, :], ALU.mult)), drain=True)
        self.OP("vector", (lambda e, p=pcf_b: e.tensor_tensor(self.S3, self.S3, p, ALU.mult)), drain=True)
        PST3 = self.pbank(PST).rearrange("p (c w) -> p c w", c=8)
        self.OP("vector", (lambda e: e.tensor_tensor(self.S3, self.S3, PST3, ALU.add)), drain=True)
        t_supd = self.OP("vector", (lambda e: e.tensor_copy(self.s_Sbf.ap(), self.s_S.ap())),
                         sem=self.S_ve, drain=True)
        self.bankg[PST] = (self.S_ve, t_supd)
        self.t_supd_prev = t_supd
        self.t_latg_free = self.t_cum_last
        self.t_kv_scan_free = max(t_sc or 0, t_st or 0)

    def emit_ffn0_group(self, g):
        TG = self.TG
        t0 = g * TG
        def epi_h(ob, bank, t_mm):
            self.W("scalar", self.S_te, t_mm)
            if ob == 0:
                self.W("scalar", self.S_te, self.t_kv_scan_free)
                self.W("scalar", self.S_ve, self.t_fac_prev)
            t_e = self.OP("scalar", (lambda e, o=self.hg3[:, ob, :], b=bank,
                                     bi=self.s_b1.ap()[:, ob:ob + 1]:
                                     e.activation(o, b, AF.Gelu, bias=bi)), sem=self.S_se, drain=True)
            return (self.S_se, t_e)
        self.fm_proj("w1", D, FF, lambda kb: self.xng3[:, kb, :], epi_h)
        t_h = self.S_se.v
        def epi_x1p(ob, bank, t_mm):
            par = ob % 2
            self.W("vector", self.S_te, t_mm)
            ssem, stgt = self.stgg[par]
            self.W("vector", ssem, stgt)
            t_v = self.OP("vector", (lambda e, o=self.stg[par].ap(), b=bank,
                                     s=self.s_b2.ap()[:, ob:ob + 1], x=self.xTg3[:, ob, :]:
                                     e.scalar_tensor_tensor(o, b, s, x, ALU.add, ALU.add)),
                          sem=self.S_ve, drain=True)
            self.t_xT_free = t_v
            self.W("sync", self.S_ve, t_v)
            t_d = self.dma("sync",
                           self.x1p_d.ap().rearrange("(dk p) t -> p dk t", p=128)[:, ob, t0:t0 + TG],
                           self.stg[par].ap(), sem=self.S_stgs[par])
            self.stgg[par] = (self.S_stgs[par], t_d)
            return (self.S_ve, t_v)
        self.fm_proj("w2", FF, D, lambda kb: self.hg3[:, kb, :], epi_x1p,
                     rhs_waits=((self.S_se, t_h),))

    def emit_cc1(self):
        self.W("gpsimd", self.S_ve, self.t_supd_prev)
        self.dma("gpsimd", self.ccF_in.ap()[:, 0:512], self.s_S.ap(), sem=self.S_pub)
        t_p = self.dma("gpsimd", self.ccF_in.ap()[:, 512:520], self.s_Dcur.ap(), sem=self.S_pub)
        self.W("gpsimd", self.S_pub, t_p)
        self.OP("gpsimd", (lambda e: e.collective_compute(
            "AllGather", ALU.bypass,
            replica_groups=[[0, 1, 2, 3], [4, 5, 6, 7]],
            ins=[self.ccF_in.ap().opt()],
            outs=[self.ccF_out.ap().opt()])), sem=self.S_cc1)

    def emit_p3(self):
        T, NCH_ = self.T, self.NCH
        self.W("gpsimd", self.S_cc1, 1)
        self.W("gpsimd", self.S_ve, self.t_xT_free)
        self.W("gpsimd", self.S_te, self.t_kv_scan_free)
        t_gb = self.dma("gpsimd", self.gbuf3,
                        self.ccF_out.ap().rearrange("(r p) c -> p r c", p=128),
                        sem=self.S_gbuf)
        self.W("vector", self.S_gbuf, t_gb)
        self.OP("vector", lambda e: e.memset(self.s_Spre.ap(), 0.0), drain=True)
        self.OP("vector", lambda e: e.memset(self.s_Sin.ap(), 0.0), drain=True)
        for c in range(4):
            self.OP("vector", (lambda e, c=c: e.scalar_tensor_tensor(
                self.s_Sin.ap(), self.s_Spre.ap(), self.c_oh4.ap()[:, c:c + 1],
                self.s_Sin.ap(), ALU.mult, ALU.add)), drain=True)
            self.OP("vector", (lambda e, c=c: e.tensor_tensor(
                self.Spre3, self.Spre3,
                self.gbuf3[:, c, 512:520][:, :, None].broadcast_to([128, 8, HS]),
                ALU.mult)), drain=True)
            self.OP("vector", (lambda e, c=c: e.tensor_tensor(
                self.s_Spre.ap(), self.s_Spre.ap(), self.gbuf3[:, c, 0:512], ALU.add)), drain=True)
        t_sin = self.OP("vector", lambda e: e.tensor_copy(self.s_Sinb.ap(), self.s_Sin.ap()),
                        sem=self.S_ve, drain=True)
        PYC = 3
        t_rpp_guard = 0
        for cg in range(NCH_):
            self.W("vector", self.S_te, t_rpp_guard)
            dpre_b = self.Dpre3[:, cg, :][:, :, None].broadcast_to([128, 8, CH])
            t_rpp = self.OP("vector", (lambda e, cg=cg, d=dpre_b: e.tensor_tensor(
                self.rpp3, self.rtlp3[:, :, cg * CH:(cg + 1) * CH], d, ALU.mult)),
                sem=self.S_ve, drain=True)
            self.W("tensor", self.S_ve, t_rpp)
            if cg == 0:
                self.W("tensor", self.S_ve, t_sin)
            self.wait_bank(PYC)
            t_cor = None
            started = [False, False]
            for h in range(H):
                hb = 64 * (h % 2)
                st = not started[h % 2]
                started[h % 2] = True
                t_cor = self.OP("tensor", (lambda e, b=self.pbank(PYC)[hb:hb + 64, 64 * (h // 2):64 * (h // 2) + 64],
                                           w=self.Sinb3[hb:hb + 64, h // 2, :],
                                           r=self.rpp3[hb:hb + 64, h // 2, :],
                                           st=st, sp=(h >= H - 2):
                                           e.matmul(b, w, r, start=st, stop=sp)),
                                sem=self.S_te if h == H - 1 else None)
            t_rpp_guard = t_cor
            self.W("vector", self.S_te, t_cor)
            PYC3 = self.pbank(PYC).rearrange("p (c w) -> p c w", c=8)
            t_yc = self.OP("vector", (lambda e, cg=cg, p3=PYC3: e.tensor_tensor(
                self.ypre3[:, :, cg * CH:(cg + 1) * CH],
                self.ypre3[:, :, cg * CH:(cg + 1) * CH], p3, ALU.add)),
                sem=self.S_ve, drain=True)
            self.bankg[PYC] = (self.S_ve, t_yc)
            t_yc_last = t_yc
        self.dump("ypre", self.s_ypre.ap())
        self.W("sync", self.S_stgs[0], self.S_stgs[0].v)
        self.W("sync", self.S_stgs[1], self.S_stgs[1].v)
        for tw in range(T // 512):
            w0 = tw * 512
            def epi_x1(ob, bank, t_mm, w0=w0):
                par = ob % 2
                ssem, stgt = self.x1lg[par]
                self.W("sync", ssem, stgt)
                t_l = self.dma("sync", self.x1l[par].ap(),
                               self.x1p_d.ap().rearrange("(dk p) t -> p dk t", p=128)[:, ob, w0:w0 + 512],
                               sem=self.S_x1ls[par])
                self.W("vector", self.S_te, t_mm)
                self.W("vector", self.S_x1ls[par], t_l)
                self.W("vector", self.S_stgs[par], self.t_x1s_guard[par])
                self.OP("vector", (lambda e, b=bank, p=par, ob=ob: e.tensor_tensor(
                    self.x1tw3[:, ob, :], b, self.x1l[p].ap(), ALU.add)), drain=True)
                t_v = self.OP("vector", (lambda e, ob=ob, w0=w0: e.tensor_copy(
                    self.xn13[:, ob, w0:w0 + 512], self.x1tw3[:, ob, :])),
                    sem=self.S_ve, drain=True)
                self.W("sync", self.S_ve, t_v)
                t_d = self.dma("sync",
                               self.x1_d.ap().rearrange("(dk p) t -> p dk t", p=128)[:, ob, w0:w0 + 512],
                               self.x1tw3[:, ob, :], sem=self.S_stgs[par])
                self.x1lg[par] = (self.S_ve, t_v)
                self.t_x1s_guard[par] = self.S_stgs[par].v
                return (self.S_ve, t_v)
            last = self.fm_proj("Wo", D, D, lambda kb: self.ypre3[:, kb, w0:w0 + 512], epi_x1,
                                rhs_waits=((self.S_ve, t_yc_last),))
            t_bc = self.ln_stats(lambda kb, w0=w0: self.xn13[:, kb, w0:w0 + 512], 512,
                                 sq_waits=(last,))
            self.t_xn1_ready = self.ln_apply(
                lambda kb, w0=w0: self.xn13[:, kb, w0:w0 + 512],
                lambda kb, w0=w0: self.xn13[:, kb, w0:w0 + 512], 512,
                affine=False, t_bc=t_bc)
        self.dump("xn1", self.s_xn1.ap())

    def emit_p4(self):
        T = self.T
        NB = 6
        t_q_store = 0
        for tw in range(T // 512):
            w0 = tw * 512
            def epi_q(ob, bank, t_mm, w0=w0):
                par = ob % 2
                self.W("scalar", self.S_te, t_mm)
                self.W("scalar", self.S_sqbs[par], self.sqbg[par])
                t_e = self.OP("scalar", (lambda e, b=bank, p=par: e.activation(
                    self.sqb[p].ap(), b, AF.Copy)), sem=self.S_se, drain=True)
                self.W("sync", self.S_se, t_e)
                t_d = self.dma("sync",
                               self.q_d.ap().rearrange("(dk p) t -> p dk t", p=128)[:, ob, w0:w0 + 512],
                               self.sqb[par].ap(), sem=self.S_sqbs[par])
                self.sqbg[par] = t_d
                return (self.S_se, t_e)
            self.fm_proj("WqT", D, D, lambda kb, w0=w0: self.xn13[:, kb, w0:w0 + 512], epi_q,
                         rhs_waits=((self.S_ve, self.t_xn1_ready),))
            def epi_k(ob, bank, t_mm, w0=w0):
                self.W("scalar", self.S_te, t_mm)
                self.W("scalar", self.S_eld, self.t_kstage_guard)
                t_e = self.OP("scalar", (lambda e, b=bank: e.activation(
                    self.s_eL.ap(), b, AF.Copy)), sem=self.S_se, drain=True)
                self.W("sync", self.S_se, t_e)
                self.t_kstage_guard = self.dma(
                    "sync",
                    self.ccK_in.ap().rearrange("(dk p) t -> p dk t", p=128)[:, ob, w0:w0 + 512],
                    self.s_eL.ap(), sem=self.S_eld)
                return (self.S_se, t_e)
            self.t_kstage_guard = 0
            self.fm_proj("WkT", D, D, lambda kb, w0=w0: self.xn13[:, kb, w0:w0 + 512], epi_k)
            def epi_vv(tt, oh, bank, t_mm, tw=tw):
                self.W("vector", self.S_te, t_mm)
                self.W("vector", self.S_ktld, self.t_vstage_guard)
                t_e = self.OP("vector", (lambda e, b=bank: e.tensor_copy(self.s_ktl.ap()[:, 0:512], b)),
                              sem=self.S_ve, drain=True)
                self.W("sync", self.S_ve, t_e)
                row0 = (tw * 4 + tt) * 128
                self.t_vstage_guard = self.dma(
                    "sync", self.ccV_in.ap()[row0:row0 + 128, oh * 512:(oh + 1) * 512],
                    self.s_ktl.ap()[:, 0:512], sem=self.S_ktld)
                return (self.S_ve, t_e)
            self.t_vstage_guard = 0
            self.tok_proj("WvT", (),
                          lambda kb, tt, w0=w0: self.xn13[:, kb, w0 + tt * 128:w0 + (tt + 1) * 128],
                          epi_vv)
        self.W("gpsimd", self.S_eld, self.S_eld.v)
        self.W("gpsimd", self.S_ktld, self.S_ktld.v)
        for cin, cout in ((self.ccK_in, self.ccK_out), (self.ccV_in, self.ccV_out)):
            self.OP("gpsimd", (lambda e, ci=cin, co=cout: e.collective_compute(
                "AllGather", ALU.bypass,
                replica_groups=[[0, 1, 2, 3], [4, 5, 6, 7]],
                ins=[ci.ap().opt()],
                outs=[co.ap().opt()])), sem=self.S_cc2)

    def emit_p5(self):
        s2 = SlabStream(self, [self.wbBIG, self.wbRL], self.S_w2s,
                        init_guards=[(self.S_te, self.S_te.v),
                                     (self.S_stgs[0], self.S_stgs[0].v),
                                     (self.S_stgs[1], self.S_stgs[1].v)])
        for g in range(self.NTG):
            s2.reg("tw1", 32, 1024, 8)
            s2.reg("tw2", 32, 1024, 8)
            s2.reg("th1", 32, 4096, 2)
            s2.reg("th2", 32, 4096, 2)
            s2.reg("tw3", 8, 4096, 2)
        self.stream = s2
        for g in range(self.NTG):
            t0 = g * self.TG
            def epi_act(dst3, scale_col, func):
                def epi(ob, bank, t_mm):
                    self.W("scalar", self.S_te, t_mm)
                    t_e = self.OP("scalar", (lambda e, o=dst3[:, ob, :], b=bank,
                                             s=scale_col[:, ob:ob + 1]:
                                             e.activation(o, b, func, scale=s)),
                                  sem=self.S_se, drain=True)
                    return (self.S_se, t_e)
                return epi
            self.fm_proj("tw1", D, FF, lambda kb, t0=t0: self.xn13[:, kb, t0:t0 + self.TG],
                         epi_act(self.u13, self.s_s1.ap(), AF.Silu),
                         rhs_waits=((self.S_te, self.t_u1_guard),))
            t_u1 = self.S_se.v
            self.fm_proj("tw2", D, FF, lambda kb, t0=t0: self.xn13[:, kb, t0:t0 + self.TG],
                         epi_act(self.u23, self.s_s2.ap(), AF.Copy),
                         rhs_waits=((self.S_te, self.t_u2_guard),))
            t_u2 = self.S_se.v
            t_h = None
            for ob in range(FK):
                if ob == 0:
                    self.W("vector", self.S_se, max(t_u1, t_u2))
                t_h = self.OP("vector", (lambda e, ob=ob: e.tensor_tensor(
                    self.u13[:, ob, :], self.u13[:, ob, :], self.u23[:, ob, :], ALU.mult)),
                    sem=self.S_ve, drain=True)
            def epi_g1(ob, bank, t_mm):
                self.W("scalar", self.S_te, t_mm)
                t_e = self.OP("scalar", (lambda e, o=self.u23[:, ob, :], b=bank,
                                         s=self.s_sh1.ap()[:, ob:ob + 1]:
                                         e.activation(o, b, AF.Gelu, scale=s)),
                              sem=self.S_se, drain=True)
                return (self.S_se, t_e)
            self.fm_proj("th1", FF, FF, lambda kb: self.u13[:, kb, :], epi_g1,
                         rhs_waits=((self.S_ve, t_h),))
            t_g1 = self.S_se.v
            self.fm_proj("th2", FF, FF, lambda kb: self.u23[:, kb, :],
                         epi_act(self.u13, self.s_sh2.ap(), AF.Gelu),
                         rhs_waits=((self.S_se, t_g1),))
            t_g2 = self.S_se.v
            self.t_u2_guard = self.S_te.v
            def epi_y3(ob, bank, t_mm, t0=t0):
                self.W("scalar", self.S_te, t_mm)
                t_e = self.OP("scalar", (lambda e, o=self.yffn13[:, ob, t0:t0 + self.TG], b=bank,
                                         s=self.s_s3.ap()[:, ob:ob + 1]:
                                         e.activation(o, b, AF.Copy, scale=s)),
                              sem=self.S_se, drain=True)
                return (self.S_se, t_e)
            self.fm_proj("tw3", FF, D, lambda kb: self.u13[:, kb, :], epi_y3,
                         rhs_waits=((self.S_se, t_g2),))
            self.t_u1_guard = self.S_te.v
        self.dump("yffn1", self.s_yffn1.ap())

    def emit_p6(self):
        T = self.T
        KMB = self.KMB
        NKC = KMB // 128
        NMB = 4 * T // KMB
        NQG = T // 512
        self.W("sync", self.S_te, self.S_te.v)
        self.W("sync", self.S_se, self.S_se.v)
        t_q = self.dma("sync", self.q3, self.q_d.ap().rearrange("(dk p) t -> p dk t", p=128),
                       sem=self.S_qld)
        self.W("sync", self.S_cc2, 2)
        kv_guard = [0, 0]
        t_kld = [0, 0]
        t_vld = [0, 0]
        LBANK = (2, 5, 6, 7)
        t_avadd = 0
        t_av_prev = 0
        for mbi in range(NMB):
            par = mbi % 2
            r, l = (mbi * KMB) // T, (mbi * KMB) % T
            self.W("sync", self.S_te, kv_guard[par])
            t_kld[par] = self.dma(
                "sync", self.Kmb4[:, par, :, :],
                self.ccK_out.ap().rearrange("(r kb p) t -> p r kb t", p=128, kb=DK)[:, r, :, l:l + KMB],
                sem=self.S_kmb)
            a0 = (r * T + l) // 128
            t_vld[par] = self.dma(
                "sync", self.Vmb4[:, par, :, :],
                self.ccV_out.ap().rearrange("(a p) d -> p a d", p=128)[:, a0:a0 + NKC, :],
                sem=self.S_vmb)
            for qw in range(NQG):
                q0 = qw * 512
                t_exp = [None, None]
                for kc in range(NKC):
                    bi = kc
                    self.wait_bank(bi)
                    if qw == 0 and kc == 0:
                        self.W("tensor", self.S_kmb, t_kld[par])
                        if mbi == 0:
                            self.W("tensor", self.S_qld, t_q)
                    t_sc = None
                    for kb in range(DK):
                        t_sc = self.OP("tensor", (lambda e, b=self.pbank(bi)[:, 0:512],
                                                  w=self.Kmb4[:, par, kb, kc * 128:(kc + 1) * 128],
                                                  r=self.q3[:, kb, q0:q0 + 512],
                                                  st=(kb == 0), sp=(kb == DK - 1):
                                                  e.matmul(b, w, r, start=st, stop=sp)),
                                       sem=self.S_te if kb == DK - 1 else None)
                    self.W("scalar", self.S_te, t_sc)
                    self.W("scalar", self.S_te, t_av_prev)
                    t_exp[kc] = self.OP("scalar", (lambda e, b=self.pbank(bi)[:, 0:512], kc=kc:
                                                   e.activation(self.PT3[:, kc, 0:512], b, AF.Exp,
                                                                scale=1.0 / 32.0)),
                                        sem=self.S_se, drain=True)
                    self.bankg[bi] = (self.S_se, t_exp[kc])
                if qw == 0:
                    self.W("tensor", self.S_vmb, t_vld[par])
                for oc in range(DK):
                    bi = 3 + (oc % 2)
                    self.wait_bank(bi)
                    if oc == 0:
                        self.W("tensor", self.S_se, t_exp[0])
                    t_av = None
                    for kc in range(NKC):
                        if oc == 0 and kc == 1:
                            self.W("tensor", self.S_se, t_exp[NKC - 1])
                        t_av = self.OP("tensor", (lambda e, b=self.pbank(bi)[:, 0:512],
                                                  w=self.Vmb4[:, par, kc, oc * 128:(oc + 1) * 128],
                                                  r=self.PT3[:, kc, 0:512],
                                                  st=(kc == 0), sp=(kc == NKC - 1):
                                                  e.matmul(b, w, r, start=st, stop=sp)),
                                       sem=self.S_te if kc == NKC - 1 else None)
                    self.W("vector", self.S_te, t_av)
                    acc = self.accL3 if oc < 4 else self.accH3
                    aslice = acc[:, oc % 4, q0:q0 + 512]
                    if mbi == 0:
                        t_avadd = self.OP("vector", (lambda e, a=aslice, b=self.pbank(bi)[:, 0:512]:
                                                     e.tensor_copy(a, b)), sem=self.S_ve, drain=True)
                    else:
                        t_avadd = self.OP("vector", (lambda e, a=aslice, b=self.pbank(bi)[:, 0:512]:
                                                     e.tensor_tensor(a, a, b, ALU.add)), sem=self.S_ve, drain=True)
                    self.bankg[bi] = (self.S_ve, t_avadd)
                t_l = None
                for kc in range(NKC):
                    t_l = self.OP("tensor", (lambda e, b=self.pbank(LBANK[qw])[0:1, 0:512], kc=kc,
                                             st=(mbi == 0 and kc == 0),
                                             sp=(mbi == NMB - 1 and kc == NKC - 1):
                                             e.matmul(b, self.c_onescb.ap(), self.PT3[:, kc, 0:512],
                                                      start=st, stop=sp)),
                                  sem=self.S_te if kc == NKC - 1 else None)
                t_av_prev = t_l
            kv_guard[par] = self.S_te.v
        self.t_attn_done = t_avadd
        self.LBANK = LBANK

    def emit_p7(self):
        T = self.T
        self.W("sync", self.S_stgs[0], self.S_stgs[0].v)
        self.W("sync", self.S_stgs[1], self.S_stgs[1].v)
        self.W("vector", self.S_ve, self.t_attn_done)
        self.W("vector", self.S_se, self.S_se.v)
        for tw in range(T // 512):
            w0 = tw * 512
            t_lr = self.OP("vector", (lambda e, tw=tw: e.reciprocal(
                self.r_tmp.ap()[:, 0:512], self.pbank(self.LBANK[tw])[0:1, 0:512])),
                sem=self.S_ve, drain=True)
            self.W("tensor", self.S_ve, t_lr)
            self.wait_bank(2)
            t_b = self.OP("tensor", (lambda e: e.matmul(
                self.pbank(2), self.c_ones1f.ap(), self.r_tmp.ap()[0:1, 0:512],
                start=True, stop=True)), sem=self.S_te)
            self.W("vector", self.S_te, t_b)
            t_lb = self.OP("vector", (lambda e: e.tensor_copy(self.s_linvb.ap(), self.pbank(2))),
                           sem=self.S_ve, drain=True)
            self.bankg[2] = (self.S_ve, t_lb)
            for ob in range(DK):
                par = ob % 2
                ssem, stgt = self.x1lg[par]
                self.W("sync", ssem, stgt)
                t_l = self.dma("sync", self.x1l[par].ap(),
                               self.x1_d.ap().rearrange("(dk p) t -> p dk t", p=128)[:, ob, w0:w0 + 512],
                               sem=self.S_x1ls[par])
                acc = self.accL3 if ob < 4 else self.accH3
                aslice = acc[:, ob % 4, w0:w0 + 512]
                self.W("vector", self.S_x1ls[par], t_l)
                if ob == 0:
                    self.W("vector", self.S_ve, t_lb)
                self.W("vector", self.S_stgs[par], self.stg_p7g[par])
                self.OP("vector", (lambda e, a=aslice, p=par: e.tensor_tensor(
                    self.stg[p].ap(), a, self.s_linvb.ap(), ALU.mult)), drain=True)
                self.OP("vector", (lambda e, p=par: e.tensor_tensor(
                    self.stg[p].ap(), self.stg[p].ap(), self.x1l[p].ap(), ALU.add)), drain=True)
                t_f = self.OP("vector", (lambda e, p=par, ob=ob, w0=w0: e.tensor_tensor(
                    self.stg[p].ap(), self.stg[p].ap(), self.yffn13[:, ob, w0:w0 + 512], ALU.add)),
                    sem=self.S_ve, drain=True)
                self.x1lg[par] = (self.S_ve, t_f)
                self.W("sync", self.S_ve, t_f)
                self.W("sync", self.S_stgs[par], self.stg_p7g[par])
                t_d = self.dma("sync",
                               self.out_d.ap().rearrange("(dk p) t -> p dk t", p=128)[:, ob, w0:w0 + 512],
                               self.stg[par].ap(), sem=self.S_stgs[par])
                self.stg_p7g[par] = t_d
        for s in self.ALL_DMA_SEMS:
            self.W("sync", s, s.v)

    def build(self, stop_after=None):
        self._stop = stop_after
        self.t_latg_free = 0
        self.t_ypre_ve = 0
        self.t_xT_free = 0
        self.t_vtg = 0
        self.t_la = 0
        self.t_exp_prev = 0
        self.t_fac_prev = 0
        self.scan_te_g = [0, 0]
        self.t_cum_last = 0
        self.t_supd_prev = self.T_VE_INIT if hasattr(self, "T_VE_INIT") else 0
        self.t_kv_scan_free = 0
        self.t_u1_guard = 0
        self.t_u2_guard = 0
        self.t_kmb_guard = 0
        self.t_vmb_guard = 0
        self.t_pt_guard = 0
        self.x1lg = [(self.S_ve, 0), (self.S_ve, 0)]
        self.t_x1s_guard = [0, 0]
        self.sqbg = [0, 0]
        self.stg_p7g = [0, 0]
        self.emit_init()
        s1 = SlabStream(self, self.wb, self.S_wbs)
        for g in range(self.NTG):
            s1.reg("Wx", 8, 1024, 4)
            s1.reg("Ww", 2, 4096, 1)
            s1.reg("Wr", 8, 1024, 4)
            s1.reg("Wk", 8, 1024, 4)
            s1.reg("Wv", 2, 4096, 1)
            s1.reg("w1", 32, 1024, 4)
            s1.reg("w2", 8, 4096, 1)
        for tw in range(self.T // 512):
            s1.reg("Wo", 8, 1024, 4)
        for tw in range(self.T // 512):
            s1.reg("WqT", 8, 1024, 4)
            s1.reg("WkT", 8, 1024, 4)
            s1.reg("WvT", 2, 4096, 1)
        self.stream = s1
        self.t_supd_prev = self.T_VE_INIT
        def done(stage):
            return self._stop == stage
        go = True
        CAST_DRIP = {1: ("tw1", "tw2"), 2: ("th1",), 3: ("th2", "tw3")}
        for g in range(self.NTG):
            for nm in CAST_DRIP.get(g, ()):
                self.W("gpsimd", self.S_te, self.S_te.v)
                self.emit_cast(nm)
            self.emit_l0_group(g)
            if done("p1"):
                go = False
                break
            NCG = self.TG // CH
            self.emit_scan_h1(g, 0, g * NCG)
            for ct in range(NCG):
                if ct + 1 < NCG:
                    self.emit_scan_h1(g, ct + 1, g * NCG + ct + 1)
                self.emit_scan_h2(g, ct, g * NCG + ct)
            if done("scan"):
                go = False
                break
            self.emit_ffn0_group(g)
        self.dump("ypre_s", self.s_ypre.ap())
        if go and not done("ffn0"):
            self.emit_cc1()
            if not done("cc1"):
                self.emit_p3()
                if not done("p3"):
                    self.emit_p4()
                    if not done("p4"):
                        self.emit_p5()
                        if not done("p5"):
                            self.emit_p6()
                            if not done("p6"):
                                self.emit_p7()
        if self._stop is not None:
            for s in self.ALL_DMA_SEMS:
                self.W("sync", s, s.v)
        nc = self.nc
        with nc.Block() as block:
            @block.tensor
            def _(e):
                for f in self.q["tensor"]:
                    f(e)

            @block.vector
            def _(e):
                for f in self.q["vector"]:
                    f(e)

            @block.scalar
            def _(e):
                for f in self.q["scalar"]:
                    f(e)

            @block.gpsimd
            def _(e):
                for f in self.q["gpsimd"]:
                    f(e)

            @block.sync
            def _(e):
                for f in self.q["sync"]:
                    f(e)
        return nc


def _build_scanmask():
    m64 = np.triu(np.ones((CH, CH), np.uint8))
    block = np.concatenate([m64] * 16, axis=1)
    return np.concatenate([block, block], axis=0)


def host_inputs(inputs, T=2048):
    f32 = np.float32
    ct = np.ascontiguousarray

    def fmr(w):
        DIN_, DOUT_ = w.shape
        KIN, NOB = DIN_ // 128, DOUT_ // 128
        return ct(w.reshape(KIN, 128, NOB, 128).transpose(1, 2, 0, 3)
                  .reshape(128, NOB, KIN * 128))

    def tokr(w):
        return ct(w.reshape(8, 128, 1024).transpose(1, 0, 2).reshape(128, 2, 4096))

    x = np.asarray(inputs["L0_attn_x"], f32)
    base = {
        "Wx": fmr(np.asarray(inputs["L0_attn_Wx"], f32)),
        "Ww": tokr(np.asarray(inputs["L0_attn_Ww"], f32)),
        "Wk": fmr(np.asarray(inputs["L0_attn_Wk"], f32)),
        "Wv": tokr(np.asarray(inputs["L0_attn_Wv"], f32)),
        "Wr": fmr(np.asarray(inputs["L0_attn_Wr"], f32)),
        "Wo": fmr(np.asarray(inputs["L0_attn_Wo"], f32)),
        "w1": fmr(np.asarray(inputs["L0_ffn_w1"], f32)),
        "w2": fmr(np.asarray(inputs["L0_ffn_w2"], f32)),
        "WqT": fmr(np.asarray(inputs["L1_attn_Wq"], f32).T),
        "WkT": fmr(np.asarray(inputs["L1_attn_Wk"], f32).T),
        "WvT": tokr(np.asarray(inputs["L1_attn_Wv"], f32).T),
        "tw1": fmr(np.asarray(inputs["L1_ffn_w1_w"], f32).T),
        "tw2": fmr(np.asarray(inputs["L1_ffn_w2_w"], f32).T),
        "th1": fmr(np.asarray(inputs["L1_ffn_h1_w"], f32).T),
        "th2": fmr(np.asarray(inputs["L1_ffn_h2_w"], f32).T),
        "tw3": fmr(np.asarray(inputs["L1_ffn_w3_w"], f32).T),
        "v_lng": ct(np.asarray(inputs["L0_attn_ln_g"], f32).reshape(DK, 128).T),
        "v_lnb": ct(np.asarray(inputs["L0_attn_ln_b"], f32).reshape(DK, 128).T),
        "r_bw": np.asarray(inputs["L0_attn_bw"], f32).reshape(1, D),
        "v_b1": ct(np.asarray(inputs["L0_ffn_b1"], f32).reshape(FK, 128).T),
        "v_b2": ct(np.asarray(inputs["L0_ffn_b2"], f32).reshape(DK, 128).T),
        "v_s1": ct(np.asarray(inputs["L1_ffn_w1_s"], f32).reshape(FK, 128).T),
        "v_s2": ct(np.asarray(inputs["L1_ffn_w2_s"], f32).reshape(FK, 128).T),
        "v_sh1": ct(np.asarray(inputs["L1_ffn_h1_s"], f32).reshape(FK, 128).T),
        "v_sh2": ct(np.asarray(inputs["L1_ffn_h2_s"], f32).reshape(FK, 128).T),
        "v_s3": ct(np.asarray(inputs["L1_ffn_w3_s"], f32).reshape(DK, 128).T),
        "identf": np.eye(128, dtype=f32),
        "identb": np.eye(128, dtype=ml_dtypes.bfloat16),
        "u64f": np.tile(np.triu(np.ones((CH, CH), f32)), (2, 1)),
        "scanmask": _build_scanmask(),
    }
    in_maps = []
    for c in range(8):
        b, pos = c // 4, c % 4
        m = dict(base)
        m["x"] = ct(x[b, pos * T:(pos + 1) * T, :])
        oh = np.zeros((128, 4), f32)
        oh[:, pos] = 1.0
        m["oh4"] = oh
        in_maps.append(m)
    return in_maps


def kernel(**inputs):
    from concourse.bass_utils import run_bass_kernel_spmd
    T = 2048
    kb = KB(T=T)
    nc = kb.build()
    in_maps = host_inputs(inputs, T)
    res = run_bass_kernel_spmd(nc, in_maps, core_ids=list(range(8)))
    out = np.zeros((2, 4 * T, D), np.float32)
    for c in range(8):
        b, pos = c // 4, c % 4
        out[b, pos * T:(pos + 1) * T, :] = np.asarray(res.results[c]["out"]).T
    return out


# version 11
# speedup vs baseline: 1.0073x; 1.0073x over previous
import sys
if '/opt/trn_rl_repo' not in sys.path:
    sys.path.insert(0, '/opt/trn_rl_repo')
import numpy as np
import ml_dtypes
import concourse.bass as bass
import concourse.mybir as mybir

F32 = mybir.dt.float32
BF16 = mybir.dt.bfloat16
U8 = mybir.dt.uint8
AF = mybir.ActivationFunctionType
ALU = mybir.AluOpType
AX = mybir.AxisListType.X

ENGINES = ("tensor", "vector", "scalar", "gpsimd", "sync")
D, DK, H, HS, FF, FK, CH = 1024, 8, 16, 64, 4096, 32, 64
EPS = 1e-5


class Sem:
    def __init__(self, nc, name):
        self.h = nc.alloc_semaphore(name=name)
        self.v = 0


class SlabStream:

    def __init__(self, kb, bufs, sems, init_guards=()):
        self.kb = kb
        self.bufs = bufs
        self.sems = sems
        self.seq = []
        self.next_emit = 0
        self.next_use = 0
        self.consumed = {}
        self.t_w = {}
        self.init_guards = list(init_guards)

    def reg(self, key, NOB, cpo, sl):
        for o0 in range(0, NOB, sl):
            self.seq.append((key, o0, min(NOB, o0 + sl), cpo))

    def _emit(self):
        kb = self.kb
        i = self.next_emit
        key, o0, o1, cpo = self.seq[i]
        par = i % 2
        if i < 2:
            for sem, tgt in self.init_guards:
                kb.W("scalar", sem, tgt)
        g = self.consumed.get(i - 2)
        if g:
            kb.W("scalar", g[0], g[1])
        kb.W("scalar", kb.S_cast, kb.cast_tgt[key])
        out = self.bufs[par].ap()[:, 0:(o1 - o0) * cpo].rearrange(
            "p (s c) -> p s c", c=cpo)
        self.t_w[i] = kb.dma("scalar", out, kb.wbf[key].ap()[:, o0:o1, :],
                             sem=self.sems[par])
        self.next_emit += 1

    def use(self, key, la=True):
        i = self.next_use
        assert self.seq[i][0] == key, (self.seq[i], key)
        while self.next_emit <= min(i + (1 if la else 0), len(self.seq) - 1):
            self._emit()
        self.next_use += 1
        return i, i % 2, self.t_w[i], self.seq[i]

    def done(self, i, guard):
        self.consumed[i] = guard


class KB:

    def __init__(self, T=2048, dbg=(), use_cc=True):
        self.T = T
        self.TG = min(512, T)
        self.NTG = T // self.TG
        self.TT4 = self.TG // 128
        self.NCH = T // CH
        self.NQT = T // 128
        self.KMB = min(256, T)
        self.dbg = set(dbg)
        self.use_cc = use_cc
        self.nc = bass.Bass("TRN2", target_bir_lowering=False)
        self.q = {e: [] for e in ENGINES}
        self.dbg_outs = {}
        self._declare_dram()
        self._declare_sbuf()
        self._declare_sems()

    def add(self, e, f):
        self.q[e].append(f)

    def dma(self, qeng, out, in_, sem=None, **kw):
        if sem is None:
            sem = self.S_sq if qeng == "sync" else self.S_gq
        sem.v += 16
        def f(e, out=out, in_=in_, kw=kw, s=sem.h):
            e.dma_start(out=out, in_=in_, **kw).then_inc(s, 16)
        self.add(qeng, f)
        return sem.v

    def W(self, eng, sem, tgt):
        if tgt and tgt > 0:
            self.add(eng, lambda e, s=sem.h, t=tgt: e.wait_ge(s, t))

    def OP(self, eng, fn, sem=None, drain=False):
        if sem is not None:
            sem.v += 1
        tgt = sem.v if sem is not None else None
        def f(e, fn=fn, s=(sem.h if sem else None), dr=drain):
            if dr:
                e.drain()
            ins = fn(e)
            if s is not None:
                ins.then_inc(s, 1)
        self.add(eng, f)
        return tgt

    FM_SHAPES = {"Wx": (D, D), "Wr": (D, D), "Wk": (D, D), "Wo": (D, D),
                 "WqT": (D, D), "WkT": (D, D), "w1": (D, FF), "w2": (FF, D),
                 "tw1": (D, FF), "tw2": (D, FF), "th1": (FF, FF), "th2": (FF, FF),
                 "tw3": (FF, D)}
    TOK_NAMES = ("Ww", "Wv", "WvT")

    def _declare_dram(self):
        nc, T = self.nc, self.T
        def din(name, shape, dtype=F32):
            return nc.dram_tensor(name, list(shape), dtype, kind="ExternalInput")
        self.x_in = din("x", [T, D])
        self.w_in, self.wbf = {}, {}
        for nm, (DIN_, DOUT_) in self.FM_SHAPES.items():
            shape = [128, DOUT_ // 128, DIN_]
            self.w_in[nm] = din(nm, shape)
            self.wbf[nm] = nc.dram_tensor("b_" + nm, shape, BF16, kind="Internal")
        for nm in self.TOK_NAMES:
            shape = [128, 2, 4096]
            self.w_in[nm] = din(nm, shape)
            self.wbf[nm] = nc.dram_tensor("b_" + nm, shape, BF16, kind="Internal")
        self.v_lng = din("v_lng", [128, DK]); self.v_lnb = din("v_lnb", [128, DK])
        self.r_bw_in = din("r_bw", [1, D])
        self.v_b1 = din("v_b1", [128, FK]); self.v_b2 = din("v_b2", [128, DK])
        self.v_s1 = din("v_s1", [128, FK]); self.v_s2 = din("v_s2", [128, FK])
        self.v_sh1 = din("v_sh1", [128, FK]); self.v_sh2 = din("v_sh2", [128, FK])
        self.v_s3 = din("v_s3", [128, DK])
        self.identf_in = din("identf", [128, 128])
        self.identb_in = din("identb", [128, 128], BF16)
        self.u64_in = din("u64f", [128, CH])
        self.mask_in = din("scanmask", [128, 1024], U8)
        self.oh4_in = din("oh4", [128, 4])
        self.x1p_d = nc.dram_tensor("x1p_d", [D, T], F32, kind="Internal")
        self.x1_d = nc.dram_tensor("x1_d", [D, T], F32, kind="Internal")
        self.q_d = nc.dram_tensor("q_d", [D, T], BF16, kind="Internal")
        self.lT_d = nc.dram_tensor("lT_d", [self.NQT, 128], F32, kind="Internal")
        self.ccF_in = nc.dram_tensor("ccF_in", [128, 520], F32, kind="Internal")
        self.ccF_out = nc.dram_tensor("ccF_out", [4 * 128, 520], F32, kind="Internal")
        self.ccK_in = nc.dram_tensor("ccK_in", [D, T], BF16, kind="Internal")
        self.ccK_out = nc.dram_tensor("ccK_out", [4 * D, T], BF16, kind="Internal")
        self.ccV_in = nc.dram_tensor("ccV_in", [T, D], BF16, kind="Internal")
        self.ccV_out = nc.dram_tensor("ccV_out", [4 * T, D], BF16, kind="Internal")
        self.out_d = nc.dram_tensor("out", [D, T], F32, kind="ExternalOutput")

    def ddbg(self, name, shape, dtype=F32):
        if name in self.dbg and name not in self.dbg_outs:
            self.dbg_outs[name] = self.nc.dram_tensor("dbg_" + name, list(shape), dtype,
                                                      kind="ExternalOutput")
        return self.dbg_outs.get(name)

    def dump(self, name, buf_ap):
        if name not in self.dbg:
            return
        if name not in self.dbg_outs:
            self.dbg_outs[name] = self.nc.dram_tensor(
                "dbg_" + name, list(buf_ap.shape), buf_ap.dtype, kind="ExternalOutput")
        dd = self.dbg_outs[name]
        for sem in (self.S_te, self.S_ve, self.S_se):
            self.W("sync", sem, sem.v)
        self.W("sync", self.S_sq, self.S_sq.v)
        t = self.dma("sync", dd.ap(), buf_ap)
        for eng in ("tensor", "vector", "scalar", "gpsimd"):
            self.W(eng, self.S_sq, t)

    def _declare_sbuf(self):
        nc, T, TG, TT4, NCH, NQT = self.nc, self.T, self.TG, self.TT4, self.NCH, self.NQT
        ARENA = 207 * 1024
        arena = nc.alloc_sbuf_tensor("arena", [128, ARENA], U8)
        base = nc.lookup_mloc(arena).addr
        self._arena_sz = ARENA
        TOP = [0]
        def sb(name, shape, dtype, at=None):
            per = int(np.prod(shape[1:])) * mybir.dt.size(dtype)
            per = (per + 31) & ~31
            if at is None:
                at = TOP[0]
                TOP[0] = at + per
            else:
                TOP[0] = max(TOP[0], at + per)
            return nc.alloc_sbuf_tensor_at(name, list(shape), dtype, offset=base + at)

        self.c_identf = sb("c_identf", [128, 128], F32)
        self.c_identb = sb("c_identb", [128, 128], BF16)
        self.c_u64 = sb("c_u64", [128, CH], F32)
        self.c_mask = sb("c_mask", [128, 1024], U8)
        self.c_oh4 = sb("c_oh4", [128, 4], F32)
        self.c_ones1f = sb("c_ones1f", [1, 128], F32)
        self.c_onescb = sb("c_onescb", [128, 1], BF16)
        self.s_lng = sb("s_lng", [128, DK], F32)
        self.s_lnb = sb("s_lnb", [128, DK], F32)
        self.s_b1 = sb("s_b1", [128, FK], F32)
        self.s_b2 = sb("s_b2", [128, DK], F32)
        self.s_s1 = sb("s_s1", [128, FK], F32)
        self.s_s2 = sb("s_s2", [128, FK], F32)
        self.s_sh1 = sb("s_sh1", [128, FK], F32)
        self.s_sh2 = sb("s_sh2", [128, FK], F32)
        self.s_s3 = sb("s_s3", [128, DK], F32)
        self.bc_bw = sb("bc_bw", [128, 1024], BF16)
        self.r_mean = sb("r_mean", [1, TG], F32)
        self.r_rstd = sb("r_rstd", [1, TG], F32)
        self.r_mrstd = sb("r_mrstd", [1, TG], F32)
        self.r_tmp = sb("r_tmp", [1, TG], F32)
        self.bc_rstd = sb("bc_rstd", [128, TG], BF16)
        self.bc_mrstd = sb("bc_mrstd", [128, TG], BF16)
        self.bc_r2 = sb("bc_r2", [128, TG], BF16)
        self.stg = [sb("stg0", [128, 512], F32), sb("stg1", [128, 512], F32)]
        self._stg0_off = None
        self.x1l = [sb("x1l0", [128, 512], F32), sb("x1l1", [128, 512], F32)]
        self.sqb = [sb("sqb0", [128, 512], BF16), sb("sqb1", [128, 512], BF16)]
        NMB = 4 * T // self.KMB
        LP = NQT * NMB
        self.misc = sb("misc", [128, LP + 5 * NQT + 16], F32)
        self.r_bwrow = None
        m = self.misc.ap()
        self.v_lparts = m[:, 0:LP]
        o = LP
        self.v_qn2 = m[:, o:o + NQT]; o += NQT
        self.v_negB = m[:, o:o + NQT]; o += NQT
        self.v_lsum = m[:, o:o + NQT]; o += NQT
        self.v_linv = m[:, o:o + NQT]; o += NQT
        self.v_Mb = m[:, o:o + 1]

        SCR0 = TOP[0]
        self.s_arg = sb("s_arg", [128, 512], F32)
        self.s_eL = sb("s_eL", [128, 512], BF16)
        self.s_eN = sb("s_eN", [128, 512], BF16)
        self.s_rtl = sb("s_rtl", [128, 2 * 512], BF16)
        self.s_ktl = sb("s_ktl", [128, 2 * 512], BF16)
        self.s_kbr = sb("s_kbr", [128, 2 * 512], BF16)
        self.s_mSc = sb("s_mSc", [128, 1024], BF16)
        self.s_kbtok = sb("s_kbtok", [128, 1024], BF16)
        self.s_S = sb("s_S", [128, 512], F32)
        self.s_Sbf = sb("s_Sbf", [128, 512], BF16)
        self.s_Dpre = sb("s_Dpre", [128, NCH * 8], F32)
        self.s_bb = sb("s_bb", [128, 8], F32)
        self.s_pch = sb("s_pch", [128, 8], F32)
        self.s_pcf = sb("s_pcf", [128, 2 * 8], F32)
        self.s_Dcur = sb("s_Dcur", [128, 8], F32)
        SCR_END = TOP[0]
        KMB = self.KMB
        NKC = KMB // 128
        off = SCR0
        self.s_Kmb = sb("s_Kmb", [128, 2 * DK * KMB], BF16, at=off); off += 2 * DK * KMB * 2
        self.s_Vmb = sb("s_Vmb", [128, 2 * NKC * 1024], BF16, at=off); off += 2 * NKC * 1024 * 2
        self.s_PT = sb("s_PT", [128, NKC * 512], BF16, at=off); off += NKC * 512 * 2
        TOP[0] = max(SCR_END, off)

        self.r_bwrow = nc.alloc_sbuf_tensor_at("r_bwrow", [1, D], F32,
                                                offset=nc.lookup_mloc(self.stg[0]).addr)
        self.wb = [sb("wb0", [128, FF], BF16), sb("wb1", [128, FF], BF16)]
        self.wbBIG = nc.alloc_sbuf_tensor_at("wbBIG", [128, 2 * FF], BF16,
                                             offset=base + TOP[0] - 2 * FF * 2)

        RA0 = TOP[0]
        self.s_xn2g = sb("s_xn2g", [128, DK * TG], BF16, at=RA0)
        self.s_t1g = sb("s_t1g", [128, DK * TG], BF16, at=RA0 + DK * TG * 2)
        self.s_kTg = sb("s_kTg", [128, DK * TG], BF16, at=RA0 + 2 * DK * TG * 2)
        self.s_rTg = sb("s_rTg", [128, DK * TG], BF16, at=RA0 + 3 * DK * TG * 2)
        self.s_hg = sb("s_hg", [128, FK * TG], BF16, at=RA0)
        self.s_xn1 = sb("s_xn1", [128, DK * T], BF16, at=RA0)
        self.s_q = self.s_xn1
        TOP[0] = RA0 + max(4 * DK * TG * 2, FK * TG * 2, DK * T * 2)
        RB0 = TOP[0]
        self.s_ypre = sb("s_ypre", [128, DK * T], BF16, at=RB0)
        self.s_u2 = sb("s_u2", [128, FK * TG], BF16, at=RB0)
        self.s_accL = sb("s_accL", [128, 4 * T], F32, at=RB0)
        TOP[0] = RB0 + max(DK * T * 2, FK * TG * 2, 4 * T * 4)
        RC0 = TOP[0]
        self.s_rtlp = sb("s_rtlp", [128, DK * T], BF16, at=RC0)
        self.s_yffn1 = sb("s_yffn1", [128, DK * T], BF16, at=RC0)
        TOP[0] = RC0 + DK * T * 2
        RD0 = TOP[0]
        self.s_xTg = sb("s_xTg", [128, DK * TG], F32, at=RD0)
        self.s_xng = sb("s_xng", [128, DK * TG], BF16, at=RD0 + DK * TG * 4)
        self.s_vtg = sb("s_vtg", [128, TT4 * 1024], BF16, at=RD0 + DK * TG * 4 + DK * TG * 2)
        self.s_gbuf = sb("s_gbuf", [128, 4 * 520], F32, at=RD0)
        self.s_Sin = sb("s_Sin", [128, 512], F32, at=RD0 + 8320)
        self.s_Sinb = sb("s_Sinb", [128, 512], BF16, at=RD0 + 8320 + 2048)
        self.s_Spre = sb("s_Spre", [128, 512], F32, at=RD0 + 8320 + 3072)
        self.s_rpp = sb("s_rpp", [128, 512], BF16, at=RD0 + 8320 + 5120)
        self.s_u1 = sb("s_u1", [128, FK * TG], BF16, at=RD0)
        self.s_accH = sb("s_accH", [128, 4 * T], F32, at=RD0)
        TOP[0] = RD0 + max(DK * TG * 4 + DK * TG * 2 + TT4 * 1024 * 2, FK * TG * 2, 4 * T * 4)
        RL0 = TOP[0]
        self.s_latg = sb("s_latg", [128, TT4 * 1024], F32, at=RL0)
        self.s_xrow = sb("s_xrow", [128, TT4 * 1024], F32, at=RL0)
        self.s_x1tw = sb("s_x1tw", [128, DK * 512], F32, at=RL0)
        self.wbRL = nc.alloc_sbuf_tensor_at("wbRL", [128, 2 * FF], BF16, offset=base + RL0)
        self.s_linvb = nc.alloc_sbuf_tensor_at("s_linvb", [128, 512], F32, offset=base + RL0)
        TOP[0] = RL0 + max(TT4 * 1024 * 4, DK * 512 * 4)
        assert TOP[0] <= ARENA, f"sbuf over arena: {TOP[0]} > {ARENA}"
        self.sbuf_used = TOP[0]

        self.P8 = nc.alloc_psum_tensor("P8", [128, 4096], F32)

        def v3(t, c, w):
            return t.ap().rearrange("p (c w) -> p c w", c=c)
        T_, TG_ = T, TG
        self.xn2g3 = v3(self.s_xn2g, DK, TG_); self.t1g3 = v3(self.s_t1g, DK, TG_)
        self.kTg3 = v3(self.s_kTg, DK, TG_); self.rTg3 = v3(self.s_rTg, DK, TG_)
        self.xng3 = v3(self.s_xng, DK, TG_); self.xTg3 = v3(self.s_xTg, DK, TG_)
        self.vtg3 = v3(self.s_vtg, TT4, 1024); self.latg3 = v3(self.s_latg, TT4, 1024)
        self.xrow3 = v3(self.s_xrow, TT4, 1024)
        self.ypre3 = v3(self.s_ypre, DK, T_); self.rtlp3 = v3(self.s_rtlp, DK, T_)
        self.hg3 = v3(self.s_hg, FK, TG_)
        self.xn13 = v3(self.s_xn1, DK, T_); self.q3 = v3(self.s_q, DK, T_)
        self.u13 = v3(self.s_u1, FK, TG_); self.u23 = v3(self.s_u2, FK, TG_)
        self.yffn13 = v3(self.s_yffn1, DK, T_)
        self.eL3 = v3(self.s_eL, 8, CH); self.eN3 = v3(self.s_eN, 8, CH)
        self.rtl4 = self.s_rtl.ap().rearrange("p (b c w) -> p b c w", b=2, c=8)
        self.ktl4 = self.s_ktl.ap().rearrange("p (b c w) -> p b c w", b=2, c=8)
        self.kbr4 = self.s_kbr.ap().rearrange("p (b c w) -> p b c w", b=2, c=8)
        self.pcf3 = self.s_pcf.ap().rearrange("p (b c) -> p b c", b=2)
        self.kbtok3 = v3(self.s_kbtok, 8, 128)
        self.S3 = v3(self.s_S, 8, HS); self.Sbf3 = v3(self.s_Sbf, 8, HS)
        self.Sin3 = v3(self.s_Sin, 8, HS); self.Sinb3 = v3(self.s_Sinb, 8, HS)
        self.Spre3 = v3(self.s_Spre, 8, HS)
        self.Dpre3 = v3(self.s_Dpre, NCH, 8)
        self.rpp3 = v3(self.s_rpp, 8, CH)
        self.Kmb4 = self.s_Kmb.ap().rearrange("p (b c w) -> p b c w", b=2, c=DK)
        self.Vmb4 = self.s_Vmb.ap().rearrange("p (b c w) -> p b c w", b=2, c=self.KMB // 128)
        self.PT3 = v3(self.s_PT, self.KMB // 128, 512)
        self.accL3 = v3(self.s_accL, 4, T_); self.accH3 = v3(self.s_accH, 4, T_)
        self.x1tw3 = v3(self.s_x1tw, DK, 512)
        self.gbuf3 = v3(self.s_gbuf, 4, 520)
        self.lp3 = self.v_lparts.rearrange("p (qt mb) -> p qt mb", qt=NQT)
        self.arg3 = self.s_arg.ap().rearrange("p (c w) -> p c w", c=8)
        self.mSc = self.s_mSc.ap()

    def pbank(self, i):
        return self.P8.ap()[:, 512 * i:512 * (i + 1)]

    def _declare_sems(self):
        nc = self.nc
        self.S_sq, self.S_gq = Sem(nc, "sq"), Sem(nc, "gq")
        self.S_te, self.S_ve, self.S_se = Sem(nc, "te"), Sem(nc, "ve"), Sem(nc, "se")
        self.S_ge = Sem(nc, "ge")
        self.S_cc1, self.S_cc2 = Sem(nc, "cc1"), Sem(nc, "cc2")
        self.S_wbs = [Sem(nc, "wb0s"), Sem(nc, "wb1s")]
        self.S_w2s = [Sem(nc, "w2s0"), Sem(nc, "w2s1")]
        self.S_cast = Sem(nc, "cast")
        self.S_wbB = Sem(nc, "wbBs")
        self.S_xrow = Sem(nc, "xrow")
        self.S_stgs = [Sem(nc, "stg0s"), Sem(nc, "stg1s")]
        self.S_x1ls = [Sem(nc, "x1l0s"), Sem(nc, "x1l1s")]
        self.S_sqbs = [Sem(nc, "sqb0s"), Sem(nc, "sqb1s")]
        self.S_eld = Sem(nc, "eld")
        self.S_ktld = Sem(nc, "ktld")
        self.S_nrm = Sem(nc, "nrm")
        self.S_gbuf = Sem(nc, "gbufs")
        self.S_kmb = Sem(nc, "kmbs")
        self.S_vmb = Sem(nc, "vmbs")
        self.S_mld = Sem(nc, "mlds")
        self.S_qld = Sem(nc, "qlds")
        self.S_lt = Sem(nc, "lts")
        self.S_pub = Sem(nc, "pubs")
        self.ALL_DMA_SEMS = ([self.S_sq, self.S_gq, self.S_wbB, self.S_xrow, self.S_eld,
                              self.S_ktld, self.S_nrm, self.S_gbuf, self.S_kmb, self.S_vmb,
                              self.S_mld, self.S_qld, self.S_lt, self.S_pub, self.S_cast]
                             + self.S_wbs + self.S_w2s + self.S_stgs + self.S_x1ls + self.S_sqbs)
        self.bankg = [(self.S_ve, 0)] * 8
        self.wbg = [(self.S_te, 0), (self.S_te, 0)]
        self.wbg_big = (self.S_te, 0)
        self.stgg = [(self.S_se, 0), (self.S_se, 0)]

    def wait_bank(self, i, eng="tensor"):
        sem, tgt = self.bankg[i]
        self.W(eng, sem, tgt)

    def emit_init(self):
        t_c = 0
        for dst, src in ((self.c_identf, self.identf_in), (self.c_identb, self.identb_in),
                         (self.c_u64, self.u64_in), (self.c_mask, self.mask_in),
                         (self.c_oh4, self.oh4_in), (self.s_lng, self.v_lng),
                         (self.s_lnb, self.v_lnb), (self.s_b1, self.v_b1), (self.s_b2, self.v_b2),
                         (self.s_s1, self.v_s1), (self.s_s2, self.v_s2), (self.s_sh1, self.v_sh1),
                         (self.s_sh2, self.v_sh2), (self.s_s3, self.v_s3),
                         (self.r_bwrow, self.r_bw_in)):
            t_c = self.dma("sync", dst.ap(), src.ap())
        self.T_CONST = t_c
        self.OP("gpsimd", lambda e: e.memset(self.c_ones1f.ap(), 1.0))
        self.T_GE_INIT = self.OP("gpsimd", lambda e: e.memset(self.c_onescb.ap(), 1.0),
                                 sem=self.S_ge, drain=True)
        self.W("gpsimd", self.S_sq, self.T_CONST)
        self.cast_tgt = {}
        for nm in ("Wx", "Ww", "Wr", "Wk", "Wv", "w1", "w2", "Wo", "WqT", "WkT",
                   "WvT"):
            self.emit_cast(nm)
        self.OP("vector", lambda e: e.memset(self.s_mSc.ap(), 0.0))
        self.OP("vector", lambda e: e.memset(self.s_S.ap(), 0.0), drain=True)
        self.OP("vector", lambda e: e.memset(self.s_Sbf.ap(), 0.0), drain=True)
        self.T_VE_INIT = self.OP("vector", lambda e: e.memset(self.s_Dcur.ap(), 1.0),
                                 sem=self.S_ve, drain=True)
        self.W("tensor", self.S_sq, self.T_CONST)
        self.W("tensor", self.S_ge, self.T_GE_INIT)
        self.OP("tensor", lambda e: e.matmul(self.pbank(0), self.c_ones1f.ap(),
                                             self.r_bwrow.ap()[:, 0:512], start=True, stop=True))
        t = self.OP("tensor", lambda e: e.matmul(self.pbank(1), self.c_ones1f.ap(),
                                                 self.r_bwrow.ap()[:, 512:1024], start=True, stop=True),
                    sem=self.S_te)
        self.W("vector", self.S_te, t)
        self.OP("vector", lambda e: e.tensor_copy(self.bc_bw.ap()[:, 0:512], self.pbank(0)), drain=True)
        t2 = self.OP("vector", lambda e: e.tensor_copy(self.bc_bw.ap()[:, 512:1024], self.pbank(1)),
                     sem=self.S_ve, drain=True)
        self.bankg[0] = (self.S_ve, t2)
        self.bankg[1] = (self.S_ve, t2)

    def emit_cast(self, nm):
        src, dst = self.w_in[nm], self.wbf[nm]
        NOB, cpo = src.shape[1], src.shape[2]
        chunk = max(1, (16 * 1024 * 1024) // (128 * cpo * 4))
        t = 0
        for o0 in range(0, NOB, chunk):
            o1 = min(NOB, o0 + chunk)
            t = self.dma("gpsimd", dst.ap()[:, o0:o1, :], src.ap()[:, o0:o1, :],
                         sem=self.S_cast)
        self.cast_tgt[nm] = t

    def fm_proj(self, key, DIN, DOUT, rhs_fn, epi, rhs_waits=(), bank_base=0, n_tok=None):
        KIN = DIN // 128
        n = n_tok or 512
        cpo = KIN * 128
        st = self.stream
        last = None
        ob = 0
        while ob < DOUT // 128:
            i, par, t_w, (_k, o0, o1, _c) = st.use(key)
            assert o0 == ob and _c == cpo
            t_mm = None
            for j in range(o0, o1):
                if j == o0:
                    self.W("tensor", st.sems[par], t_w)
                if j == 0:
                    for sem, tgt in rhs_waits:
                        self.W("tensor", sem, tgt)
                bi = bank_base + j % 2
                self.wait_bank(bi)
                bank = self.pbank(bi)[:, 0:n]
                base = (j - o0) * cpo
                for kb in range(KIN):
                    t_mm = self.OP("tensor", (lambda e, b=bank,
                                              w=st.bufs[par].ap()[:, base + kb * 128:base + (kb + 1) * 128],
                                              r=rhs_fn(kb),
                                              st_=(kb == 0), sp=(kb == KIN - 1):
                                              e.matmul(b, w, r, start=st_, stop=sp)),
                                   sem=self.S_te if kb == KIN - 1 else None)
                last = epi(j, bank, t_mm)
                self.bankg[bi] = last
            st.done(i, (self.S_te, t_mm))
            ob = o1
        return last

    def tok_proj(self, key, lhs_waits, lhs_fn, epi, bank_base=0):
        st = self.stream
        halves = [st.use(key), st.use(key, la=False)]
        for h, (i, par, t_w, _s) in enumerate(halves):
            self.W("tensor", st.sems[par], t_w)
        for sem, tgt in lhs_waits:
            self.W("tensor", sem, tgt)
        t_last = None
        for tt in range(self.TT4):
            for oh in range(2):
                bi = bank_base + (tt * 2 + oh) % 2
                self.wait_bank(bi)
                bank = self.pbank(bi)
                t_mm = None
                for kb in range(DK):
                    par = halves[kb // 4][1]
                    t_mm = self.OP("tensor", (lambda e, b=bank, w=lhs_fn(kb, tt),
                                              r=st.bufs[par].ap()[:, (kb % 4) * 1024 + oh * 512:
                                                                  (kb % 4) * 1024 + (oh + 1) * 512],
                                              st_=(kb == 0), sp=(kb == DK - 1):
                                              e.matmul(b, w, r, start=st_, stop=sp)),
                                   sem=self.S_te if kb == DK - 1 else None)
                t_last = t_mm
                self.bankg[bi] = epi(tt, oh, bank, t_mm)
        for i, par, t_w, _s in halves:
            st.done(i, (self.S_te, t_last))

    def ln_stats(self, srcb_fn, n, sq_waits=()):
        self.wait_bank(6)
        self.wait_bank(7)
        t_sq_guard = [0, 0]
        for kb in range(DK):
            for sem, tgt in (sq_waits if kb == 0 else ()):
                self.W("scalar", sem, tgt)
                self.W("tensor", sem, tgt)
            if kb >= 2:
                self.W("scalar", self.S_te, t_sq_guard[kb % 2])
            t_s = self.OP("scalar", (lambda e, o=self.sqb[kb % 2].ap()[:, 0:n], i=srcb_fn(kb):
                                     e.activation(o, i, AF.Square)), sem=self.S_se, drain=True)
            self.OP("tensor", (lambda e, b=self.pbank(6)[0:1, 0:n], r=srcb_fn(kb),
                               st=(kb == 0), sp=(kb == DK - 1):
                               e.matmul(b, self.c_onescb.ap(), r, start=st, stop=sp)))
            self.W("tensor", self.S_se, t_s)
            t_sq_guard[kb % 2] = self.OP("tensor", (lambda e, b=self.pbank(7)[0:1, 0:n],
                                                    r=self.sqb[kb % 2].ap()[:, 0:n],
                                                    st=(kb == 0), sp=(kb == DK - 1):
                                                    e.matmul(b, self.c_onescb.ap(), r, start=st, stop=sp)),
                                         sem=self.S_te)
        t_mm_done = t_sq_guard[(DK - 1) % 2]
        self.W("vector", self.S_te, t_mm_done)
        mean, rstd, mrstd, tmp = self.r_mean.ap()[:, 0:n], self.r_rstd.ap()[:, 0:n], \
            self.r_mrstd.ap()[:, 0:n], self.r_tmp.ap()[:, 0:n]
        self.OP("vector", lambda e: e.tensor_scalar(mean, self.pbank(6)[0:1, 0:n], 1.0 / D, None, ALU.mult))
        self.OP("vector", lambda e: e.tensor_scalar(tmp, self.pbank(7)[0:1, 0:n], 1.0 / D, EPS, ALU.mult, ALU.add), drain=True)
        self.OP("vector", lambda e: e.tensor_tensor(rstd, mean, mean, ALU.mult), drain=True)
        t_v = self.OP("vector", lambda e: e.tensor_tensor(tmp, tmp, rstd, ALU.subtract),
                      sem=self.S_ve, drain=True)
        self.W("scalar", self.S_ve, t_v)
        self.OP("scalar", lambda e: e.activation(tmp, tmp, AF.Ln), drain=True)
        t_r = self.OP("scalar", lambda e: e.activation(rstd, tmp, AF.Exp, scale=-0.5),
                      sem=self.S_se, drain=True)
        self.W("vector", self.S_se, t_r)
        t_v2 = self.OP("vector", lambda e: e.tensor_tensor(mrstd, mean, rstd, ALU.mult),
                       sem=self.S_ve, drain=True)
        self.W("tensor", self.S_ve, t_v2)
        self.OP("tensor", lambda e: e.matmul(self.pbank(6)[:, 0:n], self.c_ones1f.ap(), rstd, start=True, stop=True))
        t_b = self.OP("tensor", lambda e: e.matmul(self.pbank(7)[:, 0:n], self.c_ones1f.ap(), mrstd, start=True, stop=True),
                      sem=self.S_te)
        self.W("vector", self.S_te, t_b)
        self.OP("vector", lambda e: e.tensor_copy(self.bc_rstd.ap()[:, 0:n], self.pbank(6)[:, 0:n]), drain=True)
        t_e = self.OP("vector", lambda e: e.tensor_copy(self.bc_mrstd.ap()[:, 0:n], self.pbank(7)[:, 0:n]),
                      sem=self.S_ve, drain=True)
        self.bankg[6] = (self.S_ve, t_e)
        self.bankg[7] = (self.S_ve, t_e)
        return t_e

    def ln_apply(self, srcb_fn, dst_fn, n, affine, t_bc):
        t_last = None
        for kb in range(DK):
            if kb == 0:
                self.W("vector", self.S_ve, t_bc)
            self.OP("vector", (lambda e, o=dst_fn(kb), i=srcb_fn(kb):
                               e.tensor_tensor(o, i, self.bc_rstd.ap()[:, 0:n], ALU.mult)), drain=True)
            if affine:
                self.OP("vector", (lambda e, o=dst_fn(kb):
                                   e.tensor_tensor(o, o, self.bc_mrstd.ap()[:, 0:n], ALU.subtract)), drain=True)
                t_last = self.OP("vector", (lambda e, o=dst_fn(kb), g=self.s_lng.ap()[:, kb:kb + 1],
                                            b=self.s_lnb.ap()[:, kb:kb + 1]:
                                            e.tensor_scalar(o, o, g, b, ALU.mult, ALU.add)),
                                 sem=self.S_ve, drain=True)
            else:
                t_last = self.OP("vector", (lambda e, o=dst_fn(kb):
                                            e.tensor_tensor(o, o, self.bc_mrstd.ap()[:, 0:n], ALU.subtract)),
                                 sem=self.S_ve, drain=True)
        return t_last

    def emit_l0_group(self, g):
        TG, TT4 = self.TG, self.TT4
        t0 = g * TG
        self.W("sync", self.S_te, self.t_latg_free)
        self.W("sync", self.S_ve, self.t_ypre_ve)
        t_x = self.dma("sync", self.xrow3,
                       self.x_in.ap()[t0:t0 + TG, :].rearrange("(tt p) d -> p tt d", p=128),
                       sem=self.S_xrow)
        self.W("tensor", self.S_xrow, t_x)
        self.W("tensor", self.S_ve, self.t_xT_free)
        t_ev = 0
        for tt in range(TT4):
            for dk in range(DK):
                bi = 2 + (tt * DK + dk) % 2
                self.wait_bank(bi)
                t_tr = self.OP("tensor", (lambda e, b=self.pbank(bi)[:, 0:128],
                                          src=self.xrow3[:, tt, dk * 128:(dk + 1) * 128]:
                                          e.transpose(b, src, self.c_identf.ap())), sem=self.S_te)
                self.W("vector", self.S_te, t_tr)
                t_ev = self.OP("vector", (lambda e, o=self.xTg3[:, dk, tt * 128:(tt + 1) * 128],
                                          b=self.pbank(bi)[:, 0:128]:
                                          e.tensor_copy(o, b)), sem=self.S_ve, drain=True)
                self.bankg[bi] = (self.S_ve, t_ev)
        t_xb = 0
        for dk in range(DK):
            t_xb = self.OP("vector", (lambda e, o=self.xng3[:, dk, :], i=self.xTg3[:, dk, :]:
                                      e.tensor_copy(o, i)), sem=self.S_ve, drain=True)
        t_bc = self.ln_stats(lambda kb: self.xng3[:, kb, :], TG, sq_waits=((self.S_ve, t_xb),))
        rstd, tmp = self.r_rstd.ap()[:, 0:TG], self.r_tmp.ap()[:, 0:TG]
        self.OP("vector", lambda e: e.tensor_tensor(tmp, rstd, rstd, ALU.mult), drain=True)
        t2v = self.OP("vector", lambda e: e.tensor_scalar(tmp, tmp, -EPS, 1.0 + EPS,
                                                          ALU.mult, ALU.add),
                      sem=self.S_ve, drain=True)
        self.W("scalar", self.S_ve, t2v)
        self.OP("scalar", lambda e: e.activation(tmp, tmp, AF.Ln), drain=True)
        t3s = self.OP("scalar", lambda e: e.activation(tmp, tmp, AF.Exp, scale=-0.5),
                      sem=self.S_se, drain=True)
        self.W("tensor", self.S_se, t3s)
        self.wait_bank(6)
        t4t = self.OP("tensor", lambda e: e.matmul(self.pbank(6)[:, 0:TG], self.c_ones1f.ap(),
                                                   tmp[0:1, :], start=True, stop=True),
                      sem=self.S_te)
        self.W("vector", self.S_te, t4t)
        t5v = self.OP("vector", lambda e: e.tensor_copy(self.bc_r2.ap()[:, 0:TG],
                                                        self.pbank(6)[:, 0:TG]),
                      sem=self.S_ve, drain=True)
        self.bankg[6] = (self.S_ve, t5v)
        t_xn = self.ln_apply(lambda kb: self.xng3[:, kb, :], lambda kb: self.xng3[:, kb, :],
                             TG, affine=False, t_bc=t_bc)
        t_xn2 = None
        for kb in range(DK):
            if kb == 0:
                self.W("vector", self.S_ve, t5v)
            self.OP("vector", (lambda e, kb=kb: e.tensor_tensor(
                self.xn2g3[:, kb, :], self.xng3[:, kb, :], self.bc_r2.ap()[:, 0:TG],
                ALU.mult)), drain=True)
            t_xn2 = self.OP("vector", (lambda e, kb=kb: e.tensor_scalar(
                self.xn2g3[:, kb, :], self.xn2g3[:, kb, :], self.s_lng.ap()[:, kb:kb + 1],
                self.s_lnb.ap()[:, kb:kb + 1], ALU.mult, ALU.add)),
                sem=self.S_ve, drain=True)

        def epi_copy(dst3, func=AF.Copy):
            def epi(ob, bank, t_mm):
                self.W("scalar", self.S_te, t_mm)
                t_e = self.OP("scalar", (lambda e, o=dst3[:, ob, :], b=bank:
                                         e.activation(o, b, func)), sem=self.S_se, drain=True)
                return (self.S_se, t_e)
            return epi
        self.fm_proj("Wx", D, D, lambda kb: self.xn2g3[:, kb, :], epi_copy(self.t1g3),
                     rhs_waits=((self.S_ve, t_xn2),))
        t_t1 = self.S_se.v
        def epi_z(tt, oh, bank, t_mm):
            self.W("vector", self.S_te, t_mm)
            ssem, stgt = self.stgg[oh]
            self.W("vector", ssem, stgt)
            t_v = self.OP("vector", (lambda e, o=self.stg[oh].ap(), b=bank,
                                     c=self.bc_bw.ap()[:, oh * 512:(oh + 1) * 512]:
                                     e.tensor_tensor(o, b, c, ALU.add)), sem=self.S_ve, drain=True)
            self.W("scalar", self.S_ve, t_v)
            t_s = self.OP("scalar", (lambda e, o=self.latg3[:, tt, oh * 512:(oh + 1) * 512],
                                     i=self.stg[oh].ap():
                                     e.activation(o, i, AF.Sigmoid, scale=-1.0)),
                          sem=self.S_se, drain=True)
            self.stgg[oh] = (self.S_se, t_s)
            return (self.S_ve, t_v)
        self.tok_proj("Ww", ((self.S_se, t_t1),),
                      lambda kb, tt: self.t1g3[:, kb, tt * 128:(tt + 1) * 128], epi_z)
        self.fm_proj("Wr", D, D, lambda kb: self.xn2g3[:, kb, :], epi_copy(self.rTg3, AF.Sigmoid))
        t_la = None
        for tt in range(TT4):
            for oh in range(2):
                t_la = self.OP("scalar", (lambda e, o=self.latg3[:, tt, oh * 512:(oh + 1) * 512]:
                                          e.activation(o, o, AF.Ln)), sem=self.S_se, drain=True)
        self.t_la = t_la
        self.fm_proj("Wk", D, D, lambda kb: self.xn2g3[:, kb, :], epi_copy(self.kTg3))
        def epi_v(tt, oh, bank, t_mm):
            self.W("vector", self.S_te, t_mm)
            t_e = self.OP("vector", (lambda e, o=self.vtg3[:, tt, oh * 512:(oh + 1) * 512], b=bank:
                                     e.tensor_copy(o, b)), sem=self.S_ve, drain=True)
            self.t_vtg = t_e
            return (self.S_ve, t_e)
        self.tok_proj("Wv", (), lambda kb, tt: self.xn2g3[:, kb, tt * 128:(tt + 1) * 128], epi_v)
        if g == 0:
            self.dump("xT", self.s_xTg.ap())
            self.dump("xn", self.s_xng.ap())
            self.dump("xn2", self.s_xn2g.ap())
            self.dump("t1", self.s_t1g.ap())
            self.dump("la", self.s_latg.ap())
            self.dump("kT", self.s_kTg.ap())
            self.dump("rT", self.s_rTg.ap())
            self.dump("vt", self.s_vtg.ap())

    def emit_scan_h1(self, g, ct, cg):
        par = ct % 2
        b0 = 64 * par
        tt = ct // 2
        PL = 0
        self.wait_bank(PL)
        self.W("tensor", self.S_se, self.t_la)
        t_cum = None
        for dk in range(DK):
            t_cum = self.OP("tensor", (lambda e, b=self.pbank(PL)[:, dk * 64:(dk + 1) * 64],
                                       w=self.latg3[b0:b0 + 64, tt, dk * 128:(dk + 1) * 128],
                                       r=self.c_u64.ap()[b0:b0 + 64, :],
                                       st=(dk == 0), sp=(dk == DK - 1):
                                       e.matmul(b, w, r, start=st, stop=sp)),
                            sem=self.S_te if dk == DK - 1 else None)
        self.t_cum_last = t_cum
        self.W("vector", self.S_te, t_cum)
        self.W("vector", self.S_se, self.t_exp_prev)
        PL3 = self.pbank(PL).rearrange("p (c w) -> p c w", c=8)
        self.OP("vector", lambda e: e.tensor_scalar(self.s_bb.ap(), PL3[:, :, 63:64], 0.5, None, ALU.mult), drain=True)
        t_arg = self.OP("vector", (lambda e:
                        e.tensor_tensor(self.arg3, PL3,
                                        self.s_bb.ap()[:, :, None].broadcast_to([128, 8, CH]),
                                        ALU.subtract)), sem=self.S_ve, drain=True)
        self.bankg[PL] = (self.S_ve, t_arg)
        self.W("scalar", self.S_ve, t_arg)
        self.W("scalar", self.S_ve, self.t_fac_prev)
        self.OP("scalar", lambda e: e.activation(self.s_eL.ap(), self.s_arg.ap(), AF.Exp), drain=True)
        self.OP("scalar", lambda e: e.activation(self.s_eN.ap(), self.s_arg.ap(), AF.Exp, scale=-1.0), drain=True)
        self.OP("scalar", lambda e: e.activation(self.s_pch.ap(), self.s_bb.ap(), AF.Exp), drain=True)
        t_exp = self.OP("scalar", (lambda e, par=par: e.activation(
            self.pcf3[:, par, :], self.s_bb.ap(), AF.Exp, scale=2.0)),
            sem=self.S_se, drain=True)
        self.t_exp_prev = t_exp
        self.W("vector", self.S_se, t_exp)
        self.W("vector", self.S_te, self.scan_te_g[par])
        pch_b = self.s_pch.ap()[:, :, None].broadcast_to([128, 8, CH])
        self.OP("vector", (lambda e, par=par: e.tensor_tensor(
            self.rtl4[:, par], self.rTg3[:, :, ct * CH:(ct + 1) * CH], self.eL3, ALU.mult)), drain=True)
        self.OP("vector", (lambda e, par=par: e.tensor_tensor(
            self.ktl4[:, par], self.kTg3[:, :, ct * CH:(ct + 1) * CH], self.eN3, ALU.mult)), drain=True)
        self.OP("vector", (lambda e, par=par, p=pch_b: e.tensor_tensor(
            self.kbr4[:, par], self.ktl4[:, par], p, ALU.mult)), drain=True)
        t_fac = self.OP("vector", (lambda e, par=par, p=pch_b: e.tensor_tensor(
            self.rtlp3[:, :, cg * CH:(cg + 1) * CH], self.rtl4[:, par], p, ALU.mult)),
            sem=self.S_ve, drain=True)
        self.t_fac_prev = t_fac
        if not hasattr(self, "t_fac_of"):
            self.t_fac_of = {}
        self.t_fac_of[cg] = t_fac

    def emit_scan_h2(self, g, ct, cg):
        parts = getattr(self, "scan_parts", {"scores", "msk", "tr", "y", "state"})
        par = ct % 2
        b0 = 64 * par
        tt = ct // 2
        PL, PS1, PS2, PST = 0, 1, 2, 6
        PYA, PYB = 3, 4
        t_fac = self.t_fac_of[cg]
        if "scores" not in parts:
            return
        self.W("tensor", self.S_ve, t_fac)
        self.wait_bank(PS1)
        self.wait_bank(PS2)
        t_sc = None
        for h in range(H):
            bank = self.pbank(PS1 if h % 2 == 0 else PS2)
            hb = 64 * (h % 2)
            t_sc = self.OP("tensor", (lambda e, b=bank[b0:b0 + 64, 64 * (h // 2):64 * (h // 2) + 64],
                                      w=self.ktl4[hb:hb + 64, par, h // 2, :],
                                      r=self.rtl4[hb:hb + 64, par, h // 2, :],
                                      st=(h < 2), sp=(h >= H - 2):
                                      e.matmul(b, w, r, start=st, stop=sp)),
                           sem=self.S_te if h == H - 1 else None)
        if "msk" not in parts:
            return
        self.W("vector", self.S_te, t_sc)
        self.OP("vector", (lambda e: e.copy_predicated(self.mSc[b0:b0 + 64, 0:512],
                                                       self.c_mask.ap()[b0:b0 + 64, 0:512],
                                                       self.pbank(PS1)[b0:b0 + 64, :])), drain=True)
        t_msk = self.OP("vector", (lambda e: e.copy_predicated(self.mSc[b0:b0 + 64, 512:1024],
                                                               self.c_mask.ap()[b0:b0 + 64, 512:1024],
                                                               self.pbank(PS2)[b0:b0 + 64, :])),
                        sem=self.S_ve, drain=True)
        self.bankg[PS1] = (self.S_ve, t_msk)
        self.bankg[PS2] = (self.S_ve, t_msk)
        if cg == 0:
            self.dump("c0_mSc", self.s_mSc.ap())
        if "tr" not in parts:
            return
        t_kbt = 0
        t_ktr = None
        for kc in range(DK):
            bi = 5 if kc % 2 == 0 else 7
            self.wait_bank(bi)
            t_ktr = self.OP("tensor", (lambda e, b=self.pbank(bi).bitcast(BF16)[b0:b0 + 64, 0:128],
                                       src=self.kbr4[:, par, kc, :]:
                                       e.transpose(b, src, self.c_identb.ap())), sem=self.S_te)
            self.W("vector", self.S_te, t_ktr)
            t_kbt = self.OP("vector", (lambda e, o=self.kbtok3[b0:b0 + 64, kc, :],
                                       b=self.pbank(bi).bitcast(BF16)[b0:b0 + 64, 0:128]:
                                       e.tensor_copy(o, b)), sem=self.S_ve, drain=True)
            self.bankg[bi] = (self.S_ve, t_kbt)
        self.scan_te_g[par] = t_ktr
        if "y" not in parts:
            return
        self.W("tensor", self.S_ve, t_msk)
        self.W("tensor", self.S_ve, self.t_supd_prev)
        self.W("tensor", self.S_ve, self.t_vtg)
        self.wait_bank(PYA)
        self.wait_bank(PYB)
        t_y = None
        started = [False, False]
        for h in range(H):
            hb = 64 * (h % 2)
            yout = self.pbank(PYA)[hb:hb + 64, 64 * (h // 2):64 * (h // 2) + 64]
            st = not started[h % 2]
            started[h % 2] = True
            mcol = 512 * (h % 2) + 64 * (h // 2)
            self.OP("tensor", (lambda e, b=yout,
                               w=self.vtg3[b0:b0 + 64, tt, 64 * h:64 * h + 64],
                               r=self.mSc[b0:b0 + 64, mcol:mcol + 64],
                               st=st, sp=(h >= H - 2): e.matmul(b, w, r, start=st, stop=sp)))
        started = [False, False]
        for h in range(H):
            hb = 64 * (h % 2)
            yout = self.pbank(PYB)[hb:hb + 64, 64 * (h // 2):64 * (h // 2) + 64]
            st = not started[h % 2]
            started[h % 2] = True
            t_y = self.OP("tensor", (lambda e, b=yout,
                                     w=self.Sbf3[hb:hb + 64, h // 2, :],
                                     r=self.rtlp3[hb:hb + 64, h // 2, cg * CH:(cg + 1) * CH],
                                     st=st, sp=(h >= H - 2): e.matmul(b, w, r, start=st, stop=sp)),
                          sem=self.S_te if h == H - 1 else None)
        self.W("vector", self.S_te, t_y)
        PA3 = self.pbank(PYA).rearrange("p (c w) -> p c w", c=8)
        PB3 = self.pbank(PYB).rearrange("p (c w) -> p c w", c=8)
        self.OP("vector", (lambda e: e.tensor_copy(self.ypre3[:, :, cg * CH:(cg + 1) * CH], PA3)), drain=True)
        t_yev = self.OP("vector", (lambda e: e.tensor_tensor(self.ypre3[:, :, cg * CH:(cg + 1) * CH],
                                                             self.ypre3[:, :, cg * CH:(cg + 1) * CH],
                                                             PB3, ALU.add)),
                        sem=self.S_ve, drain=True)
        self.bankg[PYA] = (self.S_ve, t_yev)
        self.bankg[PYB] = (self.S_ve, t_yev)
        self.t_ypre_ve = t_yev
        if "state" not in parts:
            self.t_latg_free = self.t_cum_last
            return
        self.W("tensor", self.S_ve, t_kbt)
        self.wait_bank(PST)
        t_st = None
        started = [False, False]
        for h in range(H):
            hb = 64 * (h % 2)
            st = not started[h % 2]
            started[h % 2] = True
            t_st = self.OP("tensor", (lambda e, b=self.pbank(PST)[hb:hb + 64, 64 * (h // 2):64 * (h // 2) + 64],
                                      w=self.kbtok3[b0:b0 + 64, h // 2, 64 * (h % 2):64 * (h % 2) + 64],
                                      r=self.vtg3[b0:b0 + 64, tt, 64 * h:64 * h + 64],
                                      st=st, sp=(h >= H - 2):
                                      e.matmul(b, w, r, start=st, stop=sp)),
                           sem=self.S_te if h == H - 1 else None)
        self.W("vector", self.S_te, t_st)
        pcf_b = self.pcf3[:, par, :, None].broadcast_to([128, 8, HS])
        self.OP("vector", (lambda e: e.tensor_copy(self.Dpre3[:, cg, :], self.s_Dcur.ap())), drain=True)
        self.OP("vector", (lambda e, par=par: e.tensor_tensor(self.s_Dcur.ap(), self.s_Dcur.ap(),
                                                              self.pcf3[:, par, :], ALU.mult)), drain=True)
        self.OP("vector", (lambda e, p=pcf_b: e.tensor_tensor(self.S3, self.S3, p, ALU.mult)), drain=True)
        PST3 = self.pbank(PST).rearrange("p (c w) -> p c w", c=8)
        self.OP("vector", (lambda e: e.tensor_tensor(self.S3, self.S3, PST3, ALU.add)), drain=True)
        t_supd = self.OP("vector", (lambda e: e.tensor_copy(self.s_Sbf.ap(), self.s_S.ap())),
                         sem=self.S_ve, drain=True)
        self.bankg[PST] = (self.S_ve, t_supd)
        self.t_supd_prev = t_supd
        self.t_latg_free = self.t_cum_last
        self.t_kv_scan_free = max(t_sc or 0, t_st or 0)

    def emit_ffn0_group(self, g):
        TG = self.TG
        t0 = g * TG
        def epi_h(ob, bank, t_mm):
            self.W("scalar", self.S_te, t_mm)
            if ob == 0:
                self.W("scalar", self.S_te, self.t_kv_scan_free)
                self.W("scalar", self.S_ve, self.t_fac_prev)
            t_e = self.OP("scalar", (lambda e, o=self.hg3[:, ob, :], b=bank,
                                     bi=self.s_b1.ap()[:, ob:ob + 1]:
                                     e.activation(o, b, AF.Gelu, bias=bi)), sem=self.S_se, drain=True)
            return (self.S_se, t_e)
        self.fm_proj("w1", D, FF, lambda kb: self.xng3[:, kb, :], epi_h)
        t_h = self.S_se.v
        def epi_x1p(ob, bank, t_mm):
            par = ob % 2
            self.W("vector", self.S_te, t_mm)
            ssem, stgt = self.stgg[par]
            self.W("vector", ssem, stgt)
            t_v = self.OP("vector", (lambda e, o=self.stg[par].ap(), b=bank,
                                     s=self.s_b2.ap()[:, ob:ob + 1], x=self.xTg3[:, ob, :]:
                                     e.scalar_tensor_tensor(o, b, s, x, ALU.add, ALU.add)),
                          sem=self.S_ve, drain=True)
            self.t_xT_free = t_v
            self.W("sync", self.S_ve, t_v)
            t_d = self.dma("sync",
                           self.x1p_d.ap().rearrange("(dk p) t -> p dk t", p=128)[:, ob, t0:t0 + TG],
                           self.stg[par].ap(), sem=self.S_stgs[par])
            self.stgg[par] = (self.S_stgs[par], t_d)
            return (self.S_ve, t_v)
        self.fm_proj("w2", FF, D, lambda kb: self.hg3[:, kb, :], epi_x1p,
                     rhs_waits=((self.S_se, t_h),))

    def emit_cc1(self):
        self.W("gpsimd", self.S_ve, self.t_supd_prev)
        self.dma("gpsimd", self.ccF_in.ap()[:, 0:512], self.s_S.ap(), sem=self.S_pub)
        t_p = self.dma("gpsimd", self.ccF_in.ap()[:, 512:520], self.s_Dcur.ap(), sem=self.S_pub)
        self.W("gpsimd", self.S_pub, t_p)
        self.OP("gpsimd", (lambda e: e.collective_compute(
            "AllGather", ALU.bypass,
            replica_groups=[[0, 1, 2, 3], [4, 5, 6, 7]],
            ins=[self.ccF_in.ap().opt()],
            outs=[self.ccF_out.ap().opt()])), sem=self.S_cc1)

    def emit_p3(self):
        T, NCH_ = self.T, self.NCH
        self.W("gpsimd", self.S_cc1, 1)
        self.W("gpsimd", self.S_ve, self.t_xT_free)
        self.W("gpsimd", self.S_te, self.t_kv_scan_free)
        t_gb = self.dma("gpsimd", self.gbuf3,
                        self.ccF_out.ap().rearrange("(r p) c -> p r c", p=128),
                        sem=self.S_gbuf)
        self.W("vector", self.S_gbuf, t_gb)
        self.OP("vector", lambda e: e.memset(self.s_Spre.ap(), 0.0), drain=True)
        self.OP("vector", lambda e: e.memset(self.s_Sin.ap(), 0.0), drain=True)
        for c in range(4):
            self.OP("vector", (lambda e, c=c: e.scalar_tensor_tensor(
                self.s_Sin.ap(), self.s_Spre.ap(), self.c_oh4.ap()[:, c:c + 1],
                self.s_Sin.ap(), ALU.mult, ALU.add)), drain=True)
            self.OP("vector", (lambda e, c=c: e.tensor_tensor(
                self.Spre3, self.Spre3,
                self.gbuf3[:, c, 512:520][:, :, None].broadcast_to([128, 8, HS]),
                ALU.mult)), drain=True)
            self.OP("vector", (lambda e, c=c: e.tensor_tensor(
                self.s_Spre.ap(), self.s_Spre.ap(), self.gbuf3[:, c, 0:512], ALU.add)), drain=True)
        t_sin = self.OP("vector", lambda e: e.tensor_copy(self.s_Sinb.ap(), self.s_Sin.ap()),
                        sem=self.S_ve, drain=True)
        PYC = 3
        t_rpp_guard = 0
        for cg in range(NCH_):
            self.W("vector", self.S_te, t_rpp_guard)
            dpre_b = self.Dpre3[:, cg, :][:, :, None].broadcast_to([128, 8, CH])
            t_rpp = self.OP("vector", (lambda e, cg=cg, d=dpre_b: e.tensor_tensor(
                self.rpp3, self.rtlp3[:, :, cg * CH:(cg + 1) * CH], d, ALU.mult)),
                sem=self.S_ve, drain=True)
            self.W("tensor", self.S_ve, t_rpp)
            if cg == 0:
                self.W("tensor", self.S_ve, t_sin)
            self.wait_bank(PYC)
            t_cor = None
            started = [False, False]
            for h in range(H):
                hb = 64 * (h % 2)
                st = not started[h % 2]
                started[h % 2] = True
                t_cor = self.OP("tensor", (lambda e, b=self.pbank(PYC)[hb:hb + 64, 64 * (h // 2):64 * (h // 2) + 64],
                                           w=self.Sinb3[hb:hb + 64, h // 2, :],
                                           r=self.rpp3[hb:hb + 64, h // 2, :],
                                           st=st, sp=(h >= H - 2):
                                           e.matmul(b, w, r, start=st, stop=sp)),
                                sem=self.S_te if h == H - 1 else None)
            t_rpp_guard = t_cor
            self.W("vector", self.S_te, t_cor)
            PYC3 = self.pbank(PYC).rearrange("p (c w) -> p c w", c=8)
            t_yc = self.OP("vector", (lambda e, cg=cg, p3=PYC3: e.tensor_tensor(
                self.ypre3[:, :, cg * CH:(cg + 1) * CH],
                self.ypre3[:, :, cg * CH:(cg + 1) * CH], p3, ALU.add)),
                sem=self.S_ve, drain=True)
            self.bankg[PYC] = (self.S_ve, t_yc)
            t_yc_last = t_yc
        self.dump("ypre", self.s_ypre.ap())
        self.W("sync", self.S_stgs[0], self.S_stgs[0].v)
        self.W("sync", self.S_stgs[1], self.S_stgs[1].v)
        for tw in range(T // 512):
            w0 = tw * 512
            def epi_x1(ob, bank, t_mm, w0=w0):
                par = ob % 2
                ssem, stgt = self.x1lg[par]
                self.W("sync", ssem, stgt)
                t_l = self.dma("sync", self.x1l[par].ap(),
                               self.x1p_d.ap().rearrange("(dk p) t -> p dk t", p=128)[:, ob, w0:w0 + 512],
                               sem=self.S_x1ls[par])
                self.W("vector", self.S_te, t_mm)
                self.W("vector", self.S_x1ls[par], t_l)
                self.W("vector", self.S_stgs[par], self.t_x1s_guard[par])
                self.OP("vector", (lambda e, b=bank, p=par, ob=ob: e.tensor_tensor(
                    self.x1tw3[:, ob, :], b, self.x1l[p].ap(), ALU.add)), drain=True)
                t_v = self.OP("vector", (lambda e, ob=ob, w0=w0: e.tensor_copy(
                    self.xn13[:, ob, w0:w0 + 512], self.x1tw3[:, ob, :])),
                    sem=self.S_ve, drain=True)
                self.W("sync", self.S_ve, t_v)
                t_d = self.dma("sync",
                               self.x1_d.ap().rearrange("(dk p) t -> p dk t", p=128)[:, ob, w0:w0 + 512],
                               self.x1tw3[:, ob, :], sem=self.S_stgs[par])
                self.x1lg[par] = (self.S_ve, t_v)
                self.t_x1s_guard[par] = self.S_stgs[par].v
                return (self.S_ve, t_v)
            last = self.fm_proj("Wo", D, D, lambda kb: self.ypre3[:, kb, w0:w0 + 512], epi_x1,
                                rhs_waits=((self.S_ve, t_yc_last),))
            t_bc = self.ln_stats(lambda kb, w0=w0: self.xn13[:, kb, w0:w0 + 512], 512,
                                 sq_waits=(last,))
            self.t_xn1_ready = self.ln_apply(
                lambda kb, w0=w0: self.xn13[:, kb, w0:w0 + 512],
                lambda kb, w0=w0: self.xn13[:, kb, w0:w0 + 512], 512,
                affine=False, t_bc=t_bc)
        self.dump("xn1", self.s_xn1.ap())

    def emit_p4(self):
        T = self.T
        NB = 6
        t_q_store = 0
        for tw in range(T // 512):
            w0 = tw * 512
            def epi_q(ob, bank, t_mm, w0=w0):
                par = ob % 2
                self.W("scalar", self.S_te, t_mm)
                self.W("scalar", self.S_sqbs[par], self.sqbg[par])
                t_e = self.OP("scalar", (lambda e, b=bank, p=par: e.activation(
                    self.sqb[p].ap(), b, AF.Copy)), sem=self.S_se, drain=True)
                self.W("sync", self.S_se, t_e)
                t_d = self.dma("sync",
                               self.q_d.ap().rearrange("(dk p) t -> p dk t", p=128)[:, ob, w0:w0 + 512],
                               self.sqb[par].ap(), sem=self.S_sqbs[par])
                self.sqbg[par] = t_d
                return (self.S_se, t_e)
            self.fm_proj("WqT", D, D, lambda kb, w0=w0: self.xn13[:, kb, w0:w0 + 512], epi_q,
                         rhs_waits=((self.S_ve, self.t_xn1_ready),))
            def epi_k(ob, bank, t_mm, w0=w0):
                self.W("scalar", self.S_te, t_mm)
                self.W("scalar", self.S_eld, self.t_kstage_guard)
                t_e = self.OP("scalar", (lambda e, b=bank: e.activation(
                    self.s_eL.ap(), b, AF.Copy)), sem=self.S_se, drain=True)
                self.W("sync", self.S_se, t_e)
                self.t_kstage_guard = self.dma(
                    "sync",
                    self.ccK_in.ap().rearrange("(dk p) t -> p dk t", p=128)[:, ob, w0:w0 + 512],
                    self.s_eL.ap(), sem=self.S_eld)
                return (self.S_se, t_e)
            self.t_kstage_guard = 0
            self.fm_proj("WkT", D, D, lambda kb, w0=w0: self.xn13[:, kb, w0:w0 + 512], epi_k)
            def epi_vv(tt, oh, bank, t_mm, tw=tw):
                self.W("vector", self.S_te, t_mm)
                self.W("vector", self.S_ktld, self.t_vstage_guard)
                t_e = self.OP("vector", (lambda e, b=bank: e.tensor_copy(self.s_ktl.ap()[:, 0:512], b)),
                              sem=self.S_ve, drain=True)
                self.W("sync", self.S_ve, t_e)
                row0 = (tw * 4 + tt) * 128
                self.t_vstage_guard = self.dma(
                    "sync", self.ccV_in.ap()[row0:row0 + 128, oh * 512:(oh + 1) * 512],
                    self.s_ktl.ap()[:, 0:512], sem=self.S_ktld)
                return (self.S_ve, t_e)
            self.t_vstage_guard = 0
            self.tok_proj("WvT", (),
                          lambda kb, tt, w0=w0: self.xn13[:, kb, w0 + tt * 128:w0 + (tt + 1) * 128],
                          epi_vv)
        self.W("gpsimd", self.S_eld, self.S_eld.v)
        self.W("gpsimd", self.S_ktld, self.S_ktld.v)
        for cin, cout in ((self.ccK_in, self.ccK_out), (self.ccV_in, self.ccV_out)):
            self.OP("gpsimd", (lambda e, ci=cin, co=cout: e.collective_compute(
                "AllGather", ALU.bypass,
                replica_groups=[[0, 1, 2, 3], [4, 5, 6, 7]],
                ins=[ci.ap().opt()],
                outs=[co.ap().opt()])), sem=self.S_cc2)

    def emit_p5(self):
        s2 = SlabStream(self, [self.wbBIG, self.wbRL], self.S_w2s,
                        init_guards=[(self.S_te, self.S_te.v),
                                     (self.S_stgs[0], self.S_stgs[0].v),
                                     (self.S_stgs[1], self.S_stgs[1].v)])
        for g in range(self.NTG):
            s2.reg("tw1", 32, 1024, 8)
            s2.reg("tw2", 32, 1024, 8)
            s2.reg("th1", 32, 4096, 2)
            s2.reg("th2", 32, 4096, 2)
            s2.reg("tw3", 8, 4096, 2)
        self.stream = s2
        for g in range(self.NTG):
            t0 = g * self.TG
            def epi_act(dst3, scale_col, func):
                def epi(ob, bank, t_mm):
                    self.W("scalar", self.S_te, t_mm)
                    t_e = self.OP("scalar", (lambda e, o=dst3[:, ob, :], b=bank,
                                             s=scale_col[:, ob:ob + 1]:
                                             e.activation(o, b, func, scale=s)),
                                  sem=self.S_se, drain=True)
                    return (self.S_se, t_e)
                return epi
            self.fm_proj("tw1", D, FF, lambda kb, t0=t0: self.xn13[:, kb, t0:t0 + self.TG],
                         epi_act(self.u13, self.s_s1.ap(), AF.Silu),
                         rhs_waits=((self.S_te, self.t_u1_guard),))
            t_u1 = self.S_se.v
            self.fm_proj("tw2", D, FF, lambda kb, t0=t0: self.xn13[:, kb, t0:t0 + self.TG],
                         epi_act(self.u23, self.s_s2.ap(), AF.Copy),
                         rhs_waits=((self.S_te, self.t_u2_guard),))
            t_u2 = self.S_se.v
            t_h = None
            for ob in range(FK):
                if ob == 0:
                    self.W("vector", self.S_se, max(t_u1, t_u2))
                t_h = self.OP("vector", (lambda e, ob=ob: e.tensor_tensor(
                    self.u13[:, ob, :], self.u13[:, ob, :], self.u23[:, ob, :], ALU.mult)),
                    sem=self.S_ve, drain=True)
            def epi_g1(ob, bank, t_mm):
                self.W("scalar", self.S_te, t_mm)
                t_e = self.OP("scalar", (lambda e, o=self.u23[:, ob, :], b=bank,
                                         s=self.s_sh1.ap()[:, ob:ob + 1]:
                                         e.activation(o, b, AF.Gelu, scale=s)),
                              sem=self.S_se, drain=True)
                return (self.S_se, t_e)
            self.fm_proj("th1", FF, FF, lambda kb: self.u13[:, kb, :], epi_g1,
                         rhs_waits=((self.S_ve, t_h),))
            t_g1 = self.S_se.v
            self.fm_proj("th2", FF, FF, lambda kb: self.u23[:, kb, :],
                         epi_act(self.u13, self.s_sh2.ap(), AF.Gelu),
                         rhs_waits=((self.S_se, t_g1),))
            t_g2 = self.S_se.v
            self.t_u2_guard = self.S_te.v
            def epi_y3(ob, bank, t_mm, t0=t0):
                self.W("scalar", self.S_te, t_mm)
                t_e = self.OP("scalar", (lambda e, o=self.yffn13[:, ob, t0:t0 + self.TG], b=bank,
                                         s=self.s_s3.ap()[:, ob:ob + 1]:
                                         e.activation(o, b, AF.Copy, scale=s)),
                              sem=self.S_se, drain=True)
                return (self.S_se, t_e)
            self.fm_proj("tw3", FF, D, lambda kb: self.u13[:, kb, :], epi_y3,
                         rhs_waits=((self.S_se, t_g2),))
            self.t_u1_guard = self.S_te.v
        self.dump("yffn1", self.s_yffn1.ap())

    def emit_p6(self):
        T = self.T
        KMB = self.KMB
        NKC = KMB // 128
        NMB = 4 * T // KMB
        NQG = T // 512
        self.W("sync", self.S_te, self.S_te.v)
        self.W("sync", self.S_se, self.S_se.v)
        t_q = self.dma("sync", self.q3, self.q_d.ap().rearrange("(dk p) t -> p dk t", p=128),
                       sem=self.S_qld)
        self.W("sync", self.S_cc2, 2)
        kv_guard = [0, 0]
        t_kld = [0, 0]
        t_vld = [0, 0]
        LBANK = (2, 5, 6, 7)
        t_avadd = 0
        t_av_prev = 0
        for mbi in range(NMB):
            par = mbi % 2
            r, l = (mbi * KMB) // T, (mbi * KMB) % T
            self.W("sync", self.S_te, kv_guard[par])
            t_kld[par] = self.dma(
                "sync", self.Kmb4[:, par, :, :],
                self.ccK_out.ap().rearrange("(r kb p) t -> p r kb t", p=128, kb=DK)[:, r, :, l:l + KMB],
                sem=self.S_kmb)
            a0 = (r * T + l) // 128
            t_vld[par] = self.dma(
                "sync", self.Vmb4[:, par, :, :],
                self.ccV_out.ap().rearrange("(a p) d -> p a d", p=128)[:, a0:a0 + NKC, :],
                sem=self.S_vmb)
            for qw in range(NQG):
                q0 = qw * 512
                t_exp = [None, None]
                for kc in range(NKC):
                    bi = kc
                    self.wait_bank(bi)
                    if qw == 0 and kc == 0:
                        self.W("tensor", self.S_kmb, t_kld[par])
                        if mbi == 0:
                            self.W("tensor", self.S_qld, t_q)
                    t_sc = None
                    for kb in range(DK):
                        t_sc = self.OP("tensor", (lambda e, b=self.pbank(bi)[:, 0:512],
                                                  w=self.Kmb4[:, par, kb, kc * 128:(kc + 1) * 128],
                                                  r=self.q3[:, kb, q0:q0 + 512],
                                                  st=(kb == 0), sp=(kb == DK - 1):
                                                  e.matmul(b, w, r, start=st, stop=sp)),
                                       sem=self.S_te if kb == DK - 1 else None)
                    self.W("scalar", self.S_te, t_sc)
                    self.W("scalar", self.S_te, t_av_prev)
                    t_exp[kc] = self.OP("scalar", (lambda e, b=self.pbank(bi)[:, 0:512], kc=kc:
                                                   e.activation(self.PT3[:, kc, 0:512], b, AF.Exp,
                                                                scale=1.0 / 32.0)),
                                        sem=self.S_se, drain=True)
                    self.bankg[bi] = (self.S_se, t_exp[kc])
                if qw == 0:
                    self.W("tensor", self.S_vmb, t_vld[par])
                for oc in range(DK):
                    bi = 3 + (oc % 2)
                    self.wait_bank(bi)
                    if oc == 0:
                        self.W("tensor", self.S_se, t_exp[0])
                    t_av = None
                    for kc in range(NKC):
                        if oc == 0 and kc == 1:
                            self.W("tensor", self.S_se, t_exp[NKC - 1])
                        t_av = self.OP("tensor", (lambda e, b=self.pbank(bi)[:, 0:512],
                                                  w=self.Vmb4[:, par, kc, oc * 128:(oc + 1) * 128],
                                                  r=self.PT3[:, kc, 0:512],
                                                  st=(kc == 0), sp=(kc == NKC - 1):
                                                  e.matmul(b, w, r, start=st, stop=sp)),
                                       sem=self.S_te if kc == NKC - 1 else None)
                    self.W("vector", self.S_te, t_av)
                    acc = self.accL3 if oc < 4 else self.accH3
                    aslice = acc[:, oc % 4, q0:q0 + 512]
                    if mbi == 0:
                        t_avadd = self.OP("vector", (lambda e, a=aslice, b=self.pbank(bi)[:, 0:512]:
                                                     e.tensor_copy(a, b)), sem=self.S_ve, drain=True)
                    else:
                        t_avadd = self.OP("vector", (lambda e, a=aslice, b=self.pbank(bi)[:, 0:512]:
                                                     e.tensor_tensor(a, a, b, ALU.add)), sem=self.S_ve, drain=True)
                    self.bankg[bi] = (self.S_ve, t_avadd)
                t_l = None
                for kc in range(NKC):
                    t_l = self.OP("tensor", (lambda e, b=self.pbank(LBANK[qw])[0:1, 0:512], kc=kc,
                                             st=(mbi == 0 and kc == 0),
                                             sp=(mbi == NMB - 1 and kc == NKC - 1):
                                             e.matmul(b, self.c_onescb.ap(), self.PT3[:, kc, 0:512],
                                                      start=st, stop=sp)),
                                  sem=self.S_te if kc == NKC - 1 else None)
                t_av_prev = t_l
            kv_guard[par] = self.S_te.v
        self.t_attn_done = t_avadd
        self.LBANK = LBANK

    def emit_p7(self):
        T = self.T
        self.W("sync", self.S_stgs[0], self.S_stgs[0].v)
        self.W("sync", self.S_stgs[1], self.S_stgs[1].v)
        self.W("vector", self.S_ve, self.t_attn_done)
        self.W("vector", self.S_se, self.S_se.v)
        for tw in range(T // 512):
            w0 = tw * 512
            t_lr = self.OP("vector", (lambda e, tw=tw: e.reciprocal(
                self.r_tmp.ap()[:, 0:512], self.pbank(self.LBANK[tw])[0:1, 0:512])),
                sem=self.S_ve, drain=True)
            self.W("tensor", self.S_ve, t_lr)
            self.wait_bank(2)
            t_b = self.OP("tensor", (lambda e: e.matmul(
                self.pbank(2), self.c_ones1f.ap(), self.r_tmp.ap()[0:1, 0:512],
                start=True, stop=True)), sem=self.S_te)
            self.W("vector", self.S_te, t_b)
            t_lb = self.OP("vector", (lambda e: e.tensor_copy(self.s_linvb.ap(), self.pbank(2))),
                           sem=self.S_ve, drain=True)
            self.bankg[2] = (self.S_ve, t_lb)
            for ob in range(DK):
                par = ob % 2
                ssem, stgt = self.x1lg[par]
                self.W("sync", ssem, stgt)
                t_l = self.dma("sync", self.x1l[par].ap(),
                               self.x1_d.ap().rearrange("(dk p) t -> p dk t", p=128)[:, ob, w0:w0 + 512],
                               sem=self.S_x1ls[par])
                acc = self.accL3 if ob < 4 else self.accH3
                aslice = acc[:, ob % 4, w0:w0 + 512]
                self.W("vector", self.S_x1ls[par], t_l)
                if ob == 0:
                    self.W("vector", self.S_ve, t_lb)
                self.W("vector", self.S_stgs[par], self.stg_p7g[par])
                self.OP("vector", (lambda e, a=aslice, p=par: e.tensor_tensor(
                    self.stg[p].ap(), a, self.s_linvb.ap(), ALU.mult)), drain=True)
                self.OP("vector", (lambda e, p=par: e.tensor_tensor(
                    self.stg[p].ap(), self.stg[p].ap(), self.x1l[p].ap(), ALU.add)), drain=True)
                t_f = self.OP("vector", (lambda e, p=par, ob=ob, w0=w0: e.tensor_tensor(
                    self.stg[p].ap(), self.stg[p].ap(), self.yffn13[:, ob, w0:w0 + 512], ALU.add)),
                    sem=self.S_ve, drain=True)
                self.x1lg[par] = (self.S_ve, t_f)
                self.W("sync", self.S_ve, t_f)
                self.W("sync", self.S_stgs[par], self.stg_p7g[par])
                t_d = self.dma("sync",
                               self.out_d.ap().rearrange("(dk p) t -> p dk t", p=128)[:, ob, w0:w0 + 512],
                               self.stg[par].ap(), sem=self.S_stgs[par])
                self.stg_p7g[par] = t_d
        for s in self.ALL_DMA_SEMS:
            self.W("sync", s, s.v)

    def build(self, stop_after=None):
        self._stop = stop_after
        self.t_latg_free = 0
        self.t_ypre_ve = 0
        self.t_xT_free = 0
        self.t_vtg = 0
        self.t_la = 0
        self.t_exp_prev = 0
        self.t_fac_prev = 0
        self.scan_te_g = [0, 0]
        self.t_cum_last = 0
        self.t_supd_prev = self.T_VE_INIT if hasattr(self, "T_VE_INIT") else 0
        self.t_kv_scan_free = 0
        self.t_u1_guard = 0
        self.t_u2_guard = 0
        self.t_kmb_guard = 0
        self.t_vmb_guard = 0
        self.t_pt_guard = 0
        self.x1lg = [(self.S_ve, 0), (self.S_ve, 0)]
        self.t_x1s_guard = [0, 0]
        self.sqbg = [0, 0]
        self.stg_p7g = [0, 0]
        self.emit_init()
        s1 = SlabStream(self, self.wb, self.S_wbs)
        for g in range(self.NTG):
            s1.reg("Wx", 8, 1024, 4)
            s1.reg("Ww", 2, 4096, 1)
            s1.reg("Wr", 8, 1024, 4)
            s1.reg("Wk", 8, 1024, 4)
            s1.reg("Wv", 2, 4096, 1)
            s1.reg("w1", 32, 1024, 4)
            s1.reg("w2", 8, 4096, 1)
        for tw in range(self.T // 512):
            s1.reg("Wo", 8, 1024, 4)
        for tw in range(self.T // 512):
            s1.reg("WqT", 8, 1024, 4)
            s1.reg("WkT", 8, 1024, 4)
            s1.reg("WvT", 2, 4096, 1)
        self.stream = s1
        self.t_supd_prev = self.T_VE_INIT
        def done(stage):
            return self._stop == stage
        go = True
        CAST_DRIP = {1: ("tw1", "tw2"), 2: ("th1",), 3: ("th2", "tw3")}
        for g in range(self.NTG):
            for nm in CAST_DRIP.get(g, ()):
                self.W("gpsimd", self.S_te, self.S_te.v)
                self.emit_cast(nm)
            self.emit_l0_group(g)
            if done("p1"):
                go = False
                break
            NCG = self.TG // CH
            self.emit_scan_h1(g, 0, g * NCG)
            for ct in range(NCG):
                if ct + 1 < NCG:
                    self.emit_scan_h1(g, ct + 1, g * NCG + ct + 1)
                self.emit_scan_h2(g, ct, g * NCG + ct)
            if done("scan"):
                go = False
                break
            self.emit_ffn0_group(g)
        self.dump("ypre_s", self.s_ypre.ap())
        if go and not done("ffn0"):
            self.emit_cc1()
            if not done("cc1"):
                self.emit_p3()
                if not done("p3"):
                    self.emit_p4()
                    if not done("p4"):
                        self.emit_p5()
                        if not done("p5"):
                            self.emit_p6()
                            if not done("p6"):
                                self.emit_p7()
        if self._stop is not None:
            for s in self.ALL_DMA_SEMS:
                self.W("sync", s, s.v)
        nc = self.nc
        with nc.Block() as block:
            @block.tensor
            def _(e):
                for f in self.q["tensor"]:
                    f(e)

            @block.vector
            def _(e):
                for f in self.q["vector"]:
                    f(e)

            @block.scalar
            def _(e):
                for f in self.q["scalar"]:
                    f(e)

            @block.gpsimd
            def _(e):
                for f in self.q["gpsimd"]:
                    f(e)

            @block.sync
            def _(e):
                for f in self.q["sync"]:
                    f(e)
        return nc


def _build_scanmask():
    m64 = np.triu(np.ones((CH, CH), np.uint8))
    block = np.concatenate([m64] * 16, axis=1)
    return np.concatenate([block, block], axis=0)


def host_inputs(inputs, T=2048):
    f32 = np.float32
    ct = np.ascontiguousarray

    def fmr(w):
        DIN_, DOUT_ = w.shape
        KIN, NOB = DIN_ // 128, DOUT_ // 128
        return ct(w.reshape(KIN, 128, NOB, 128).transpose(1, 2, 0, 3)
                  .reshape(128, NOB, KIN * 128))

    def tokr(w):
        return ct(w.reshape(8, 128, 1024).transpose(1, 0, 2).reshape(128, 2, 4096))

    x = np.asarray(inputs["L0_attn_x"], f32)
    base = {
        "Wx": fmr(np.asarray(inputs["L0_attn_Wx"], f32)),
        "Ww": tokr(np.asarray(inputs["L0_attn_Ww"], f32)),
        "Wk": fmr(np.asarray(inputs["L0_attn_Wk"], f32)),
        "Wv": tokr(np.asarray(inputs["L0_attn_Wv"], f32)),
        "Wr": fmr(np.asarray(inputs["L0_attn_Wr"], f32)),
        "Wo": fmr(np.asarray(inputs["L0_attn_Wo"], f32)),
        "w1": fmr(np.asarray(inputs["L0_ffn_w1"], f32)),
        "w2": fmr(np.asarray(inputs["L0_ffn_w2"], f32)),
        "WqT": fmr(np.asarray(inputs["L1_attn_Wq"], f32).T),
        "WkT": fmr(np.asarray(inputs["L1_attn_Wk"], f32).T),
        "WvT": tokr(np.asarray(inputs["L1_attn_Wv"], f32).T),
        "tw1": fmr(np.asarray(inputs["L1_ffn_w1_w"], f32).T),
        "tw2": fmr(np.asarray(inputs["L1_ffn_w2_w"], f32).T),
        "th1": fmr(np.asarray(inputs["L1_ffn_h1_w"], f32).T),
        "th2": fmr(np.asarray(inputs["L1_ffn_h2_w"], f32).T),
        "tw3": fmr(np.asarray(inputs["L1_ffn_w3_w"], f32).T),
        "v_lng": ct(np.asarray(inputs["L0_attn_ln_g"], f32).reshape(DK, 128).T),
        "v_lnb": ct(np.asarray(inputs["L0_attn_ln_b"], f32).reshape(DK, 128).T),
        "r_bw": np.asarray(inputs["L0_attn_bw"], f32).reshape(1, D),
        "v_b1": ct(np.asarray(inputs["L0_ffn_b1"], f32).reshape(FK, 128).T),
        "v_b2": ct(np.asarray(inputs["L0_ffn_b2"], f32).reshape(DK, 128).T),
        "v_s1": ct(np.asarray(inputs["L1_ffn_w1_s"], f32).reshape(FK, 128).T),
        "v_s2": ct(np.asarray(inputs["L1_ffn_w2_s"], f32).reshape(FK, 128).T),
        "v_sh1": ct(np.asarray(inputs["L1_ffn_h1_s"], f32).reshape(FK, 128).T),
        "v_sh2": ct(np.asarray(inputs["L1_ffn_h2_s"], f32).reshape(FK, 128).T),
        "v_s3": ct(np.asarray(inputs["L1_ffn_w3_s"], f32).reshape(DK, 128).T),
        "identf": np.eye(128, dtype=f32),
        "identb": np.eye(128, dtype=ml_dtypes.bfloat16),
        "u64f": np.tile(np.triu(np.ones((CH, CH), f32)), (2, 1)),
        "scanmask": _build_scanmask(),
    }
    in_maps = []
    for c in range(8):
        b, pos = c // 4, c % 4
        m = dict(base)
        m["x"] = ct(x[b, pos * T:(pos + 1) * T, :])
        oh = np.zeros((128, 4), f32)
        oh[:, pos] = 1.0
        m["oh4"] = oh
        in_maps.append(m)
    return in_maps


def kernel(**inputs):
    from concourse.bass_utils import run_bass_kernel_spmd
    T = 2048
    kb = KB(T=T)
    nc = kb.build()
    in_maps = host_inputs(inputs, T)
    res = run_bass_kernel_spmd(nc, in_maps, core_ids=list(range(8)))
    out = np.zeros((2, 4 * T, D), np.float32)
    for c in range(8):
        b, pos = c // 4, c % 4
        out[b, pos * T:(pos + 1) * T, :] = np.asarray(res.results[c]["out"]).T
    return out


# version 12 (final)
# speedup vs baseline: 1.0479x; 1.0403x over previous
import sys
if '/opt/trn_rl_repo' not in sys.path:
    sys.path.insert(0, '/opt/trn_rl_repo')
import numpy as np
import ml_dtypes
import concourse.bass as bass
import concourse.mybir as mybir

F32 = mybir.dt.float32
BF16 = mybir.dt.bfloat16
U8 = mybir.dt.uint8
AF = mybir.ActivationFunctionType
ALU = mybir.AluOpType
AX = mybir.AxisListType.X

ENGINES = ("tensor", "vector", "scalar", "gpsimd", "sync")
D, DK, H, HS, FF, FK, CH = 1024, 8, 16, 64, 4096, 32, 64
EPS = 1e-5


class Sem:
    def __init__(self, nc, name):
        self.h = nc.alloc_semaphore(name=name)
        self.v = 0


class SlabStream:

    def __init__(self, kb, bufs, sems, init_guards=()):
        self.kb = kb
        self.bufs = bufs
        self.sems = sems
        self.seq = []
        self.next_emit = 0
        self.next_use = 0
        self.consumed = {}
        self.t_w = {}
        self.init_guards = list(init_guards)

    def reg(self, key, NOB, cpo, sl):
        for o0 in range(0, NOB, sl):
            self.seq.append((key, o0, min(NOB, o0 + sl), cpo))

    def _emit(self):
        kb = self.kb
        i = self.next_emit
        key, o0, o1, cpo = self.seq[i]
        par = i % 2
        if i < 2:
            for sem, tgt in self.init_guards:
                kb.W("scalar", sem, tgt)
        g = self.consumed.get(i - 2)
        if g:
            kb.W("scalar", g[0], g[1])
        kb.W("scalar", kb.S_cast, kb.cast_tgt[key])
        out = self.bufs[par].ap()[:, 0:(o1 - o0) * cpo].rearrange(
            "p (s c) -> p s c", c=cpo)
        self.t_w[i] = kb.dma("scalar", out, kb.wbf[key].ap()[:, o0:o1, :],
                             sem=self.sems[par])
        self.next_emit += 1

    def use(self, key, la=True):
        i = self.next_use
        assert self.seq[i][0] == key, (self.seq[i], key)
        while self.next_emit <= min(i + (1 if la else 0), len(self.seq) - 1):
            self._emit()
        self.next_use += 1
        return i, i % 2, self.t_w[i], self.seq[i]

    def done(self, i, guard):
        self.consumed[i] = guard


class KB:

    def __init__(self, T=2048, dbg=(), use_cc=True):
        self.T = T
        self.TG = min(512, T)
        self.NTG = T // self.TG
        self.TT4 = self.TG // 128
        self.NCH = T // CH
        self.NQT = T // 128
        self.KMB = min(256, T)
        self.dbg = set(dbg)
        self.use_cc = use_cc
        self.nc = bass.Bass("TRN2", target_bir_lowering=False)
        self.q = {e: [] for e in ENGINES}
        self.dbg_outs = {}
        self._declare_dram()
        self._declare_sbuf()
        self._declare_sems()

    def add(self, e, f):
        self.q[e].append(f)

    def dma(self, qeng, out, in_, sem=None, **kw):
        if sem is None:
            sem = self.S_sq if qeng == "sync" else self.S_gq
        sem.v += 16
        def f(e, out=out, in_=in_, kw=kw, s=sem.h):
            e.dma_start(out=out, in_=in_, **kw).then_inc(s, 16)
        self.add(qeng, f)
        return sem.v

    def W(self, eng, sem, tgt):
        if tgt and tgt > 0:
            self.add(eng, lambda e, s=sem.h, t=tgt: e.wait_ge(s, t))

    def OP(self, eng, fn, sem=None, drain=False):
        if sem is not None:
            sem.v += 1
        tgt = sem.v if sem is not None else None
        def f(e, fn=fn, s=(sem.h if sem else None), dr=drain):
            if dr:
                e.drain()
            ins = fn(e)
            if s is not None:
                ins.then_inc(s, 1)
        self.add(eng, f)
        return tgt

    FM_SHAPES = {"Wx": (D, D), "Wr": (D, D), "Wk": (D, D), "Wo": (D, D),
                 "WqT": (D, D), "WkT": (D, D), "w1": (D, FF), "w2": (FF, D),
                 "tw1": (D, FF), "tw2": (D, FF), "th1": (FF, FF), "th2": (FF, FF),
                 "tw3": (FF, D)}
    TOK_NAMES = ("Ww", "Wv", "WvT")

    def _declare_dram(self):
        nc, T = self.nc, self.T
        def din(name, shape, dtype=F32):
            return nc.dram_tensor(name, list(shape), dtype, kind="ExternalInput")
        self.x_in = din("x", [T, D])
        self.w_in, self.wbf = {}, {}
        for nm, (DIN_, DOUT_) in self.FM_SHAPES.items():
            shape = [128, DOUT_ // 128, DIN_]
            self.w_in[nm] = din(nm, shape)
            self.wbf[nm] = nc.dram_tensor("b_" + nm, shape, BF16, kind="Internal")
        for nm in self.TOK_NAMES:
            shape = [128, 2, 4096]
            self.w_in[nm] = din(nm, shape)
            self.wbf[nm] = nc.dram_tensor("b_" + nm, shape, BF16, kind="Internal")
        self.v_lng = din("v_lng", [128, DK]); self.v_lnb = din("v_lnb", [128, DK])
        self.r_bw_in = din("r_bw", [1, D])
        self.v_b1 = din("v_b1", [128, FK]); self.v_b2 = din("v_b2", [128, DK])
        self.v_s1 = din("v_s1", [128, FK]); self.v_s2 = din("v_s2", [128, FK])
        self.v_sh1 = din("v_sh1", [128, FK]); self.v_sh2 = din("v_sh2", [128, FK])
        self.v_s3 = din("v_s3", [128, DK])
        self.identf_in = din("identf", [128, 128])
        self.identb_in = din("identb", [128, 128], BF16)
        self.u64_in = din("u64f", [128, CH])
        self.mask_in = din("scanmask", [128, 1024], U8)
        self.oh4_in = din("oh4", [128, 4])
        self.x1p_d = nc.dram_tensor("x1p_d", [D, T], F32, kind="Internal")
        self.x1_d = nc.dram_tensor("x1_d", [D, T], F32, kind="Internal")
        self.q_d = nc.dram_tensor("q_d", [D, T], BF16, kind="Internal")
        self.lT_d = nc.dram_tensor("lT_d", [self.NQT, 128], F32, kind="Internal")
        self.ccF_in = nc.dram_tensor("ccF_in", [128, 520], F32, kind="Internal")
        self.ccF_out = nc.dram_tensor("ccF_out", [4 * 128, 520], F32, kind="Internal")
        self.ccK_in = nc.dram_tensor("ccK_in", [D, T], BF16, kind="Internal")
        self.ccK_out = nc.dram_tensor("ccK_out", [4 * D, T], BF16, kind="Internal")
        self.ccV_in = nc.dram_tensor("ccV_in", [T, D], BF16, kind="Internal")
        self.ccV_out = nc.dram_tensor("ccV_out", [4 * T, D], BF16, kind="Internal")
        self.out_d = nc.dram_tensor("out", [D, T], F32, kind="ExternalOutput")

    def ddbg(self, name, shape, dtype=F32):
        if name in self.dbg and name not in self.dbg_outs:
            self.dbg_outs[name] = self.nc.dram_tensor("dbg_" + name, list(shape), dtype,
                                                      kind="ExternalOutput")
        return self.dbg_outs.get(name)

    def dump(self, name, buf_ap):
        if name not in self.dbg:
            return
        if name not in self.dbg_outs:
            self.dbg_outs[name] = self.nc.dram_tensor(
                "dbg_" + name, list(buf_ap.shape), buf_ap.dtype, kind="ExternalOutput")
        dd = self.dbg_outs[name]
        for sem in (self.S_te, self.S_ve, self.S_se):
            self.W("sync", sem, sem.v)
        self.W("sync", self.S_sq, self.S_sq.v)
        t = self.dma("sync", dd.ap(), buf_ap)
        for eng in ("tensor", "vector", "scalar", "gpsimd"):
            self.W(eng, self.S_sq, t)

    def _declare_sbuf(self):
        nc, T, TG, TT4, NCH, NQT = self.nc, self.T, self.TG, self.TT4, self.NCH, self.NQT
        ARENA = 207 * 1024
        arena = nc.alloc_sbuf_tensor("arena", [128, ARENA], U8)
        base = nc.lookup_mloc(arena).addr
        self._arena_sz = ARENA
        TOP = [0]
        def sb(name, shape, dtype, at=None):
            per = int(np.prod(shape[1:])) * mybir.dt.size(dtype)
            per = (per + 31) & ~31
            if at is None:
                at = TOP[0]
                TOP[0] = at + per
            else:
                TOP[0] = max(TOP[0], at + per)
            return nc.alloc_sbuf_tensor_at(name, list(shape), dtype, offset=base + at)

        self.c_identf = sb("c_identf", [128, 128], F32)
        self.c_identb = sb("c_identb", [128, 128], BF16)
        self.c_u64 = sb("c_u64", [128, CH], F32)
        self.c_mask = sb("c_mask", [128, 1024], U8)
        self.c_oh4 = sb("c_oh4", [128, 4], F32)
        self.c_ones1f = sb("c_ones1f", [1, 128], F32)
        self.c_onescb = sb("c_onescb", [128, 1], BF16)
        self.s_lng = sb("s_lng", [128, DK], F32)
        self.s_lnb = sb("s_lnb", [128, DK], F32)
        self.s_b1 = sb("s_b1", [128, FK], F32)
        self.s_b2 = sb("s_b2", [128, DK], F32)
        self.s_s1 = sb("s_s1", [128, FK], F32)
        self.s_s2 = sb("s_s2", [128, FK], F32)
        self.s_sh1 = sb("s_sh1", [128, FK], F32)
        self.s_sh2 = sb("s_sh2", [128, FK], F32)
        self.s_s3 = sb("s_s3", [128, DK], F32)
        self.bc_bw = sb("bc_bw", [128, 1024], BF16)
        self.r_mean = sb("r_mean", [1, TG], F32)
        self.r_rstd = sb("r_rstd", [1, TG], F32)
        self.r_mrstd = sb("r_mrstd", [1, TG], F32)
        self.r_tmp = sb("r_tmp", [1, TG], F32)
        self.bc_rstd = sb("bc_rstd", [128, TG], BF16)
        self.bc_mrstd = sb("bc_mrstd", [128, TG], BF16)
        self.bc_r2 = sb("bc_r2", [128, TG], BF16)
        self.stg = [sb("stg0", [128, 512], F32), sb("stg1", [128, 512], F32)]
        self._stg0_off = None
        self.x1l = [sb("x1l0", [128, 512], F32), sb("x1l1", [128, 512], F32)]
        self.sqb = [sb("sqb0", [128, 512], BF16), sb("sqb1", [128, 512], BF16)]
        NMB = 4 * T // self.KMB
        LP = NQT * NMB
        self.misc = sb("misc", [128, LP + 5 * NQT + 16], F32)
        self.r_bwrow = None
        m = self.misc.ap()
        self.v_lparts = m[:, 0:LP]
        o = LP
        self.v_qn2 = m[:, o:o + NQT]; o += NQT
        self.v_negB = m[:, o:o + NQT]; o += NQT
        self.v_lsum = m[:, o:o + NQT]; o += NQT
        self.v_linv = m[:, o:o + NQT]; o += NQT
        self.v_Mb = m[:, o:o + 1]

        SCR0 = TOP[0]
        self.s_arg = sb("s_arg", [128, 512], F32)
        self.s_eL = sb("s_eL", [128, 512], BF16)
        self.s_eN = sb("s_eN", [128, 512], BF16)
        self.s_rtl = sb("s_rtl", [128, 2 * 512], BF16)
        self.s_ktl = sb("s_ktl", [128, 2 * 512], BF16)
        self.s_kbr = sb("s_kbr", [128, 2 * 512], BF16)
        self.s_mSc = sb("s_mSc", [128, 1024], BF16)
        self.s_kbtok = sb("s_kbtok", [128, 1024], BF16)
        self.s_S = sb("s_S", [128, 512], F32)
        self.s_Sbf = sb("s_Sbf", [128, 512], BF16)
        self.s_Dpre = sb("s_Dpre", [128, NCH * 8], F32)
        self.s_bb = sb("s_bb", [128, 8], F32)
        self.s_pch = sb("s_pch", [128, 8], F32)
        self.s_pcf = sb("s_pcf", [128, 2 * 8], F32)
        self.s_Dcur = sb("s_Dcur", [128, 8], F32)
        SCR_END = TOP[0]
        KMB = self.KMB
        NKC = KMB // 128
        off = SCR0
        self.s_Kmb = sb("s_Kmb", [128, 2 * DK * KMB], BF16, at=off); off += 2 * DK * KMB * 2
        self.s_Vmb = sb("s_Vmb", [128, 2 * NKC * 1024], BF16, at=off); off += 2 * NKC * 1024 * 2
        self.s_PT = sb("s_PT", [128, NKC * 512], BF16, at=off); off += NKC * 512 * 2
        TOP[0] = max(SCR_END, off)

        self.r_bwrow = nc.alloc_sbuf_tensor_at("r_bwrow", [1, D], F32,
                                                offset=nc.lookup_mloc(self.stg[0]).addr)
        self.wb = [sb("wb0", [128, FF], BF16), sb("wb1", [128, FF], BF16)]
        self.wbBIG = nc.alloc_sbuf_tensor_at("wbBIG", [128, 2 * FF], BF16,
                                             offset=base + TOP[0] - 2 * FF * 2)

        RA0 = TOP[0]
        self.s_xn2g = sb("s_xn2g", [128, DK * TG], BF16, at=RA0)
        self.s_t1g = sb("s_t1g", [128, DK * TG], BF16, at=RA0 + DK * TG * 2)
        self.s_kTg = sb("s_kTg", [128, DK * TG], BF16, at=RA0 + 2 * DK * TG * 2)
        self.s_rTg = sb("s_rTg", [128, DK * TG], BF16, at=RA0 + 3 * DK * TG * 2)
        self.s_hg = sb("s_hg", [128, FK * TG], BF16, at=RA0)
        self.s_xn1 = sb("s_xn1", [128, DK * T], BF16, at=RA0)
        self.s_q = self.s_xn1
        TOP[0] = RA0 + max(4 * DK * TG * 2, FK * TG * 2, DK * T * 2)
        RB0 = TOP[0]
        self.s_ypre = sb("s_ypre", [128, DK * T], BF16, at=RB0)
        self.s_u2 = sb("s_u2", [128, FK * TG], BF16, at=RB0)
        self.s_accL = sb("s_accL", [128, 4 * T], F32, at=RB0)
        TOP[0] = RB0 + max(DK * T * 2, FK * TG * 2, 4 * T * 4)
        RC0 = TOP[0]
        self.s_rtlp = sb("s_rtlp", [128, DK * T], BF16, at=RC0)
        self.s_yffn1 = sb("s_yffn1", [128, DK * T], BF16, at=RC0)
        TOP[0] = RC0 + DK * T * 2
        RD0 = TOP[0]
        self.s_xTg = sb("s_xTg", [128, DK * TG], F32, at=RD0)
        self.s_xng = sb("s_xng", [128, DK * TG], BF16, at=RD0 + DK * TG * 4)
        self.s_vtg = sb("s_vtg", [128, TT4 * 1024], BF16, at=RD0 + DK * TG * 4 + DK * TG * 2)
        self.s_gbuf = sb("s_gbuf", [128, 4 * 520], F32, at=RD0)
        self.s_Sin = sb("s_Sin", [128, 512], F32, at=RD0 + 8320)
        self.s_Sinb = sb("s_Sinb", [128, 512], BF16, at=RD0 + 8320 + 2048)
        self.s_Spre = sb("s_Spre", [128, 512], F32, at=RD0 + 8320 + 3072)
        self.s_rpp = sb("s_rpp", [128, 512], BF16, at=RD0 + 8320 + 5120)
        self.s_u1 = sb("s_u1", [128, FK * TG], BF16, at=RD0)
        self.s_accH = sb("s_accH", [128, 4 * T], F32, at=RD0)
        TOP[0] = RD0 + max(DK * TG * 4 + DK * TG * 2 + TT4 * 1024 * 2, FK * TG * 2, 4 * T * 4)
        RL0 = TOP[0]
        self.s_latg = sb("s_latg", [128, TT4 * 1024], F32, at=RL0)
        self.s_xrow = sb("s_xrow", [128, TT4 * 1024], F32, at=RL0)
        self.s_x1tw = sb("s_x1tw", [128, DK * 512], F32, at=RL0)
        self.wbRL = nc.alloc_sbuf_tensor_at("wbRL", [128, 2 * FF], BF16, offset=base + RL0)
        self.s_linvb = nc.alloc_sbuf_tensor_at("s_linvb", [128, 512], F32, offset=base + RL0)
        TOP[0] = RL0 + max(TT4 * 1024 * 4, DK * 512 * 4)
        assert TOP[0] <= ARENA, f"sbuf over arena: {TOP[0]} > {ARENA}"
        self.sbuf_used = TOP[0]

        self.P8 = nc.alloc_psum_tensor("P8", [128, 4096], F32)

        def v3(t, c, w):
            return t.ap().rearrange("p (c w) -> p c w", c=c)
        T_, TG_ = T, TG
        self.xn2g3 = v3(self.s_xn2g, DK, TG_); self.t1g3 = v3(self.s_t1g, DK, TG_)
        self.kTg3 = v3(self.s_kTg, DK, TG_); self.rTg3 = v3(self.s_rTg, DK, TG_)
        self.xng3 = v3(self.s_xng, DK, TG_); self.xTg3 = v3(self.s_xTg, DK, TG_)
        self.vtg3 = v3(self.s_vtg, TT4, 1024); self.latg3 = v3(self.s_latg, TT4, 1024)
        self.xrow3 = v3(self.s_xrow, TT4, 1024)
        self.ypre3 = v3(self.s_ypre, DK, T_); self.rtlp3 = v3(self.s_rtlp, DK, T_)
        self.hg3 = v3(self.s_hg, FK, TG_)
        self.xn13 = v3(self.s_xn1, DK, T_); self.q3 = v3(self.s_q, DK, T_)
        self.u13 = v3(self.s_u1, FK, TG_); self.u23 = v3(self.s_u2, FK, TG_)
        self.yffn13 = v3(self.s_yffn1, DK, T_)
        self.eL3 = v3(self.s_eL, 8, CH); self.eN3 = v3(self.s_eN, 8, CH)
        self.rtl4 = self.s_rtl.ap().rearrange("p (b c w) -> p b c w", b=2, c=8)
        self.ktl4 = self.s_ktl.ap().rearrange("p (b c w) -> p b c w", b=2, c=8)
        self.kbr4 = self.s_kbr.ap().rearrange("p (b c w) -> p b c w", b=2, c=8)
        self.pcf3 = self.s_pcf.ap().rearrange("p (b c) -> p b c", b=2)
        self.kbtok3 = v3(self.s_kbtok, 8, 128)
        self.S3 = v3(self.s_S, 8, HS); self.Sbf3 = v3(self.s_Sbf, 8, HS)
        self.Sin3 = v3(self.s_Sin, 8, HS); self.Sinb3 = v3(self.s_Sinb, 8, HS)
        self.Spre3 = v3(self.s_Spre, 8, HS)
        self.Dpre3 = v3(self.s_Dpre, NCH, 8)
        self.rpp3 = v3(self.s_rpp, 8, CH)
        self.Kmb4 = self.s_Kmb.ap().rearrange("p (b c w) -> p b c w", b=2, c=DK)
        self.Vmb4 = self.s_Vmb.ap().rearrange("p (b c w) -> p b c w", b=2, c=self.KMB // 128)
        self.PT3 = v3(self.s_PT, self.KMB // 128, 512)
        self.accL3 = v3(self.s_accL, 4, T_); self.accH3 = v3(self.s_accH, 4, T_)
        self.x1tw3 = v3(self.s_x1tw, DK, 512)
        self.gbuf3 = v3(self.s_gbuf, 4, 520)
        self.lp3 = self.v_lparts.rearrange("p (qt mb) -> p qt mb", qt=NQT)
        self.arg3 = self.s_arg.ap().rearrange("p (c w) -> p c w", c=8)
        self.mSc = self.s_mSc.ap()

    def pbank(self, i):
        return self.P8.ap()[:, 512 * i:512 * (i + 1)]

    def _declare_sems(self):
        nc = self.nc
        self.S_sq, self.S_gq = Sem(nc, "sq"), Sem(nc, "gq")
        self.S_te, self.S_ve, self.S_se = Sem(nc, "te"), Sem(nc, "ve"), Sem(nc, "se")
        self.S_ge = Sem(nc, "ge")
        self.S_cc1, self.S_cc2 = Sem(nc, "cc1"), Sem(nc, "cc2")
        self.S_wbs = [Sem(nc, "wb0s"), Sem(nc, "wb1s")]
        self.S_w2s = [Sem(nc, "w2s0"), Sem(nc, "w2s1")]
        self.S_cast = Sem(nc, "cast")
        self.S_wbB = Sem(nc, "wbBs")
        self.S_xrow = Sem(nc, "xrow")
        self.S_stgs = [Sem(nc, "stg0s"), Sem(nc, "stg1s")]
        self.S_x1ls = [Sem(nc, "x1l0s"), Sem(nc, "x1l1s")]
        self.S_sqbs = [Sem(nc, "sqb0s"), Sem(nc, "sqb1s")]
        self.S_eld = Sem(nc, "eld")
        self.S_ktld = Sem(nc, "ktld")
        self.S_nrm = Sem(nc, "nrm")
        self.S_gbuf = Sem(nc, "gbufs")
        self.S_kmbs = [Sem(nc, "kmb0"), Sem(nc, "kmb1")]
        self.S_vmbs = [Sem(nc, "vmb0"), Sem(nc, "vmb1")]
        self.S_mld = Sem(nc, "mlds")
        self.S_qld = Sem(nc, "qlds")
        self.S_lt = Sem(nc, "lts")
        self.S_pub = Sem(nc, "pubs")
        self.ALL_DMA_SEMS = ([self.S_sq, self.S_gq, self.S_wbB, self.S_xrow, self.S_eld,
                              self.S_ktld, self.S_nrm, self.S_gbuf,
                              self.S_mld, self.S_qld, self.S_lt, self.S_pub, self.S_cast]
                             + self.S_wbs + self.S_w2s + self.S_stgs + self.S_x1ls
                             + self.S_sqbs + self.S_kmbs + self.S_vmbs)
        self.bankg = [(self.S_ve, 0)] * 8
        self.wbg = [(self.S_te, 0), (self.S_te, 0)]
        self.wbg_big = (self.S_te, 0)
        self.stgg = [(self.S_se, 0), (self.S_se, 0)]

    def wait_bank(self, i, eng="tensor"):
        sem, tgt = self.bankg[i]
        self.W(eng, sem, tgt)

    def emit_init(self):
        t_c = 0
        for dst, src in ((self.c_identf, self.identf_in), (self.c_identb, self.identb_in),
                         (self.c_u64, self.u64_in), (self.c_mask, self.mask_in),
                         (self.c_oh4, self.oh4_in), (self.s_lng, self.v_lng),
                         (self.s_lnb, self.v_lnb), (self.s_b1, self.v_b1), (self.s_b2, self.v_b2),
                         (self.s_s1, self.v_s1), (self.s_s2, self.v_s2), (self.s_sh1, self.v_sh1),
                         (self.s_sh2, self.v_sh2), (self.s_s3, self.v_s3),
                         (self.r_bwrow, self.r_bw_in)):
            t_c = self.dma("sync", dst.ap(), src.ap())
        self.T_CONST = t_c
        self.OP("gpsimd", lambda e: e.memset(self.c_ones1f.ap(), 1.0))
        self.T_GE_INIT = self.OP("gpsimd", lambda e: e.memset(self.c_onescb.ap(), 1.0),
                                 sem=self.S_ge, drain=True)
        self.W("gpsimd", self.S_sq, self.T_CONST)
        self.cast_tgt = {}
        for nm in ("Wx", "Ww", "Wr", "Wk", "Wv", "w1", "w2", "Wo", "WqT", "WkT",
                   "WvT"):
            self.emit_cast(nm)
        self.OP("vector", lambda e: e.memset(self.s_mSc.ap(), 0.0))
        self.OP("vector", lambda e: e.memset(self.s_S.ap(), 0.0), drain=True)
        self.OP("vector", lambda e: e.memset(self.s_Sbf.ap(), 0.0), drain=True)
        self.T_VE_INIT = self.OP("vector", lambda e: e.memset(self.s_Dcur.ap(), 1.0),
                                 sem=self.S_ve, drain=True)
        self.W("tensor", self.S_sq, self.T_CONST)
        self.W("tensor", self.S_ge, self.T_GE_INIT)
        self.OP("tensor", lambda e: e.matmul(self.pbank(0), self.c_ones1f.ap(),
                                             self.r_bwrow.ap()[:, 0:512], start=True, stop=True))
        t = self.OP("tensor", lambda e: e.matmul(self.pbank(1), self.c_ones1f.ap(),
                                                 self.r_bwrow.ap()[:, 512:1024], start=True, stop=True),
                    sem=self.S_te)
        self.W("vector", self.S_te, t)
        self.OP("vector", lambda e: e.tensor_copy(self.bc_bw.ap()[:, 0:512], self.pbank(0)), drain=True)
        t2 = self.OP("vector", lambda e: e.tensor_copy(self.bc_bw.ap()[:, 512:1024], self.pbank(1)),
                     sem=self.S_ve, drain=True)
        self.bankg[0] = (self.S_ve, t2)
        self.bankg[1] = (self.S_ve, t2)

    def emit_cast(self, nm):
        src, dst = self.w_in[nm], self.wbf[nm]
        NOB, cpo = src.shape[1], src.shape[2]
        chunk = max(1, (16 * 1024 * 1024) // (128 * cpo * 4))
        t = 0
        for o0 in range(0, NOB, chunk):
            o1 = min(NOB, o0 + chunk)
            self.W("gpsimd", self.S_cast, self.S_cast.v)
            t = self.dma("gpsimd", dst.ap()[:, o0:o1, :], src.ap()[:, o0:o1, :],
                         sem=self.S_cast)
        self.cast_tgt[nm] = t

    def fm_proj(self, key, DIN, DOUT, rhs_fn, epi, rhs_waits=(), bank_base=0, n_tok=None):
        KIN = DIN // 128
        n = n_tok or 512
        cpo = KIN * 128
        st = self.stream
        last = None
        ob = 0
        while ob < DOUT // 128:
            i, par, t_w, (_k, o0, o1, _c) = st.use(key)
            assert o0 == ob and _c == cpo
            t_mm = None
            for j in range(o0, o1):
                if j == o0:
                    self.W("tensor", st.sems[par], t_w)
                if j == 0:
                    for sem, tgt in rhs_waits:
                        self.W("tensor", sem, tgt)
                bi = bank_base + j % 2
                self.wait_bank(bi)
                bank = self.pbank(bi)[:, 0:n]
                base = (j - o0) * cpo
                for kb in range(KIN):
                    t_mm = self.OP("tensor", (lambda e, b=bank,
                                              w=st.bufs[par].ap()[:, base + kb * 128:base + (kb + 1) * 128],
                                              r=rhs_fn(kb),
                                              st_=(kb == 0), sp=(kb == KIN - 1):
                                              e.matmul(b, w, r, start=st_, stop=sp)),
                                   sem=self.S_te if kb == KIN - 1 else None)
                last = epi(j, bank, t_mm)
                self.bankg[bi] = last
            st.done(i, (self.S_te, t_mm))
            ob = o1
        return last

    def tok_proj(self, key, lhs_waits, lhs_fn, epi, bank_base=0):
        st = self.stream
        halves = [st.use(key), st.use(key, la=False)]
        for h, (i, par, t_w, _s) in enumerate(halves):
            self.W("tensor", st.sems[par], t_w)
        for sem, tgt in lhs_waits:
            self.W("tensor", sem, tgt)
        t_last = None
        for tt in range(self.TT4):
            for oh in range(2):
                bi = bank_base + (tt * 2 + oh) % 2
                self.wait_bank(bi)
                bank = self.pbank(bi)
                t_mm = None
                for kb in range(DK):
                    par = halves[kb // 4][1]
                    t_mm = self.OP("tensor", (lambda e, b=bank, w=lhs_fn(kb, tt),
                                              r=st.bufs[par].ap()[:, (kb % 4) * 1024 + oh * 512:
                                                                  (kb % 4) * 1024 + (oh + 1) * 512],
                                              st_=(kb == 0), sp=(kb == DK - 1):
                                              e.matmul(b, w, r, start=st_, stop=sp)),
                                   sem=self.S_te if kb == DK - 1 else None)
                t_last = t_mm
                self.bankg[bi] = epi(tt, oh, bank, t_mm)
        for i, par, t_w, _s in halves:
            st.done(i, (self.S_te, t_last))

    def ln_stats(self, srcb_fn, n, sq_waits=()):
        self.wait_bank(6)
        self.wait_bank(7)
        t_sq_guard = [0, 0]
        for kb in range(DK):
            for sem, tgt in (sq_waits if kb == 0 else ()):
                self.W("scalar", sem, tgt)
                self.W("tensor", sem, tgt)
            if kb >= 2:
                self.W("scalar", self.S_te, t_sq_guard[kb % 2])
            t_s = self.OP("scalar", (lambda e, o=self.sqb[kb % 2].ap()[:, 0:n], i=srcb_fn(kb):
                                     e.activation(o, i, AF.Square)), sem=self.S_se, drain=True)
            self.OP("tensor", (lambda e, b=self.pbank(6)[0:1, 0:n], r=srcb_fn(kb),
                               st=(kb == 0), sp=(kb == DK - 1):
                               e.matmul(b, self.c_onescb.ap(), r, start=st, stop=sp)))
            self.W("tensor", self.S_se, t_s)
            t_sq_guard[kb % 2] = self.OP("tensor", (lambda e, b=self.pbank(7)[0:1, 0:n],
                                                    r=self.sqb[kb % 2].ap()[:, 0:n],
                                                    st=(kb == 0), sp=(kb == DK - 1):
                                                    e.matmul(b, self.c_onescb.ap(), r, start=st, stop=sp)),
                                         sem=self.S_te)
        t_mm_done = t_sq_guard[(DK - 1) % 2]
        self.W("vector", self.S_te, t_mm_done)
        mean, rstd, mrstd, tmp = self.r_mean.ap()[:, 0:n], self.r_rstd.ap()[:, 0:n], \
            self.r_mrstd.ap()[:, 0:n], self.r_tmp.ap()[:, 0:n]
        self.OP("vector", lambda e: e.tensor_scalar(mean, self.pbank(6)[0:1, 0:n], 1.0 / D, None, ALU.mult))
        self.OP("vector", lambda e: e.tensor_scalar(tmp, self.pbank(7)[0:1, 0:n], 1.0 / D, EPS, ALU.mult, ALU.add), drain=True)
        self.OP("vector", lambda e: e.tensor_tensor(rstd, mean, mean, ALU.mult), drain=True)
        t_v = self.OP("vector", lambda e: e.tensor_tensor(tmp, tmp, rstd, ALU.subtract),
                      sem=self.S_ve, drain=True)
        self.W("scalar", self.S_ve, t_v)
        self.OP("scalar", lambda e: e.activation(tmp, tmp, AF.Ln), drain=True)
        t_r = self.OP("scalar", lambda e: e.activation(rstd, tmp, AF.Exp, scale=-0.5),
                      sem=self.S_se, drain=True)
        self.W("vector", self.S_se, t_r)
        t_v2 = self.OP("vector", lambda e: e.tensor_tensor(mrstd, mean, rstd, ALU.mult),
                       sem=self.S_ve, drain=True)
        self.W("tensor", self.S_ve, t_v2)
        self.OP("tensor", lambda e: e.matmul(self.pbank(6)[:, 0:n], self.c_ones1f.ap(), rstd, start=True, stop=True))
        t_b = self.OP("tensor", lambda e: e.matmul(self.pbank(7)[:, 0:n], self.c_ones1f.ap(), mrstd, start=True, stop=True),
                      sem=self.S_te)
        self.W("vector", self.S_te, t_b)
        self.OP("vector", lambda e: e.tensor_copy(self.bc_rstd.ap()[:, 0:n], self.pbank(6)[:, 0:n]), drain=True)
        t_e = self.OP("vector", lambda e: e.tensor_copy(self.bc_mrstd.ap()[:, 0:n], self.pbank(7)[:, 0:n]),
                      sem=self.S_ve, drain=True)
        self.bankg[6] = (self.S_ve, t_e)
        self.bankg[7] = (self.S_ve, t_e)
        return t_e

    def ln_apply(self, srcb_fn, dst_fn, n, affine, t_bc):
        t_last = None
        for kb in range(DK):
            if kb == 0:
                self.W("vector", self.S_ve, t_bc)
            self.OP("vector", (lambda e, o=dst_fn(kb), i=srcb_fn(kb):
                               e.tensor_tensor(o, i, self.bc_rstd.ap()[:, 0:n], ALU.mult)), drain=True)
            if affine:
                self.OP("vector", (lambda e, o=dst_fn(kb):
                                   e.tensor_tensor(o, o, self.bc_mrstd.ap()[:, 0:n], ALU.subtract)), drain=True)
                t_last = self.OP("vector", (lambda e, o=dst_fn(kb), g=self.s_lng.ap()[:, kb:kb + 1],
                                            b=self.s_lnb.ap()[:, kb:kb + 1]:
                                            e.tensor_scalar(o, o, g, b, ALU.mult, ALU.add)),
                                 sem=self.S_ve, drain=True)
            else:
                t_last = self.OP("vector", (lambda e, o=dst_fn(kb):
                                            e.tensor_tensor(o, o, self.bc_mrstd.ap()[:, 0:n], ALU.subtract)),
                                 sem=self.S_ve, drain=True)
        return t_last

    def emit_l0_group(self, g):
        TG, TT4 = self.TG, self.TT4
        t0 = g * TG
        self.W("sync", self.S_te, self.t_latg_free)
        self.W("sync", self.S_ve, self.t_ypre_ve)
        t_x = self.dma("sync", self.xrow3,
                       self.x_in.ap()[t0:t0 + TG, :].rearrange("(tt p) d -> p tt d", p=128),
                       sem=self.S_xrow)
        self.W("tensor", self.S_xrow, t_x)
        self.W("tensor", self.S_ve, self.t_xT_free)
        t_ev = 0
        for tt in range(TT4):
            for dk in range(DK):
                bi = 2 + (tt * DK + dk) % 2
                self.wait_bank(bi)
                t_tr = self.OP("tensor", (lambda e, b=self.pbank(bi)[:, 0:128],
                                          src=self.xrow3[:, tt, dk * 128:(dk + 1) * 128]:
                                          e.transpose(b, src, self.c_identf.ap())), sem=self.S_te)
                self.W("vector", self.S_te, t_tr)
                t_ev = self.OP("vector", (lambda e, o=self.xTg3[:, dk, tt * 128:(tt + 1) * 128],
                                          b=self.pbank(bi)[:, 0:128]:
                                          e.tensor_copy(o, b)), sem=self.S_ve, drain=True)
                self.bankg[bi] = (self.S_ve, t_ev)
        t_xb = 0
        for dk in range(DK):
            t_xb = self.OP("vector", (lambda e, o=self.xng3[:, dk, :], i=self.xTg3[:, dk, :]:
                                      e.tensor_copy(o, i)), sem=self.S_ve, drain=True)
        t_bc = self.ln_stats(lambda kb: self.xng3[:, kb, :], TG, sq_waits=((self.S_ve, t_xb),))
        rstd, tmp = self.r_rstd.ap()[:, 0:TG], self.r_tmp.ap()[:, 0:TG]
        self.OP("vector", lambda e: e.tensor_tensor(tmp, rstd, rstd, ALU.mult), drain=True)
        t2v = self.OP("vector", lambda e: e.tensor_scalar(tmp, tmp, -EPS, 1.0 + EPS,
                                                          ALU.mult, ALU.add),
                      sem=self.S_ve, drain=True)
        self.W("scalar", self.S_ve, t2v)
        self.OP("scalar", lambda e: e.activation(tmp, tmp, AF.Ln), drain=True)
        t3s = self.OP("scalar", lambda e: e.activation(tmp, tmp, AF.Exp, scale=-0.5),
                      sem=self.S_se, drain=True)
        self.W("tensor", self.S_se, t3s)
        self.wait_bank(6)
        t4t = self.OP("tensor", lambda e: e.matmul(self.pbank(6)[:, 0:TG], self.c_ones1f.ap(),
                                                   tmp[0:1, :], start=True, stop=True),
                      sem=self.S_te)
        self.W("vector", self.S_te, t4t)
        t5v = self.OP("vector", lambda e: e.tensor_copy(self.bc_r2.ap()[:, 0:TG],
                                                        self.pbank(6)[:, 0:TG]),
                      sem=self.S_ve, drain=True)
        self.bankg[6] = (self.S_ve, t5v)
        t_xn = self.ln_apply(lambda kb: self.xng3[:, kb, :], lambda kb: self.xng3[:, kb, :],
                             TG, affine=False, t_bc=t_bc)
        t_xn2 = None
        for kb in range(DK):
            if kb == 0:
                self.W("vector", self.S_ve, t5v)
            self.OP("vector", (lambda e, kb=kb: e.tensor_tensor(
                self.xn2g3[:, kb, :], self.xng3[:, kb, :], self.bc_r2.ap()[:, 0:TG],
                ALU.mult)), drain=True)
            t_xn2 = self.OP("vector", (lambda e, kb=kb: e.tensor_scalar(
                self.xn2g3[:, kb, :], self.xn2g3[:, kb, :], self.s_lng.ap()[:, kb:kb + 1],
                self.s_lnb.ap()[:, kb:kb + 1], ALU.mult, ALU.add)),
                sem=self.S_ve, drain=True)

        def epi_copy(dst3, func=AF.Copy):
            def epi(ob, bank, t_mm):
                self.W("scalar", self.S_te, t_mm)
                t_e = self.OP("scalar", (lambda e, o=dst3[:, ob, :], b=bank:
                                         e.activation(o, b, func)), sem=self.S_se, drain=True)
                return (self.S_se, t_e)
            return epi
        self.fm_proj("Wx", D, D, lambda kb: self.xn2g3[:, kb, :], epi_copy(self.t1g3),
                     rhs_waits=((self.S_ve, t_xn2),))
        t_t1 = self.S_se.v
        def epi_z(tt, oh, bank, t_mm):
            self.W("vector", self.S_te, t_mm)
            ssem, stgt = self.stgg[oh]
            self.W("vector", ssem, stgt)
            t_v = self.OP("vector", (lambda e, o=self.stg[oh].ap(), b=bank,
                                     c=self.bc_bw.ap()[:, oh * 512:(oh + 1) * 512]:
                                     e.tensor_tensor(o, b, c, ALU.add)), sem=self.S_ve, drain=True)
            self.W("scalar", self.S_ve, t_v)
            t_s = self.OP("scalar", (lambda e, o=self.latg3[:, tt, oh * 512:(oh + 1) * 512],
                                     i=self.stg[oh].ap():
                                     e.activation(o, i, AF.Sigmoid, scale=-1.0)),
                          sem=self.S_se, drain=True)
            self.stgg[oh] = (self.S_se, t_s)
            return (self.S_ve, t_v)
        self.tok_proj("Ww", ((self.S_se, t_t1),),
                      lambda kb, tt: self.t1g3[:, kb, tt * 128:(tt + 1) * 128], epi_z)
        self.fm_proj("Wr", D, D, lambda kb: self.xn2g3[:, kb, :], epi_copy(self.rTg3, AF.Sigmoid))
        t_la = None
        for tt in range(TT4):
            for oh in range(2):
                t_la = self.OP("scalar", (lambda e, o=self.latg3[:, tt, oh * 512:(oh + 1) * 512]:
                                          e.activation(o, o, AF.Ln)), sem=self.S_se, drain=True)
        self.t_la = t_la
        self.fm_proj("Wk", D, D, lambda kb: self.xn2g3[:, kb, :], epi_copy(self.kTg3))
        def epi_v(tt, oh, bank, t_mm):
            self.W("vector", self.S_te, t_mm)
            t_e = self.OP("vector", (lambda e, o=self.vtg3[:, tt, oh * 512:(oh + 1) * 512], b=bank:
                                     e.tensor_copy(o, b)), sem=self.S_ve, drain=True)
            self.t_vtg = t_e
            return (self.S_ve, t_e)
        self.tok_proj("Wv", (), lambda kb, tt: self.xn2g3[:, kb, tt * 128:(tt + 1) * 128], epi_v)
        if g == 0:
            self.dump("xT", self.s_xTg.ap())
            self.dump("xn", self.s_xng.ap())
            self.dump("xn2", self.s_xn2g.ap())
            self.dump("t1", self.s_t1g.ap())
            self.dump("la", self.s_latg.ap())
            self.dump("kT", self.s_kTg.ap())
            self.dump("rT", self.s_rTg.ap())
            self.dump("vt", self.s_vtg.ap())

    def emit_scan_h1(self, g, ct, cg):
        par = ct % 2
        b0 = 64 * par
        tt = ct // 2
        PL = 0
        self.wait_bank(PL)
        self.W("tensor", self.S_se, self.t_la)
        t_cum = None
        for dk in range(DK):
            t_cum = self.OP("tensor", (lambda e, b=self.pbank(PL)[:, dk * 64:(dk + 1) * 64],
                                       w=self.latg3[b0:b0 + 64, tt, dk * 128:(dk + 1) * 128],
                                       r=self.c_u64.ap()[b0:b0 + 64, :],
                                       st=(dk == 0), sp=(dk == DK - 1):
                                       e.matmul(b, w, r, start=st, stop=sp)),
                            sem=self.S_te if dk == DK - 1 else None)
        self.t_cum_last = t_cum
        self.W("vector", self.S_te, t_cum)
        self.W("vector", self.S_se, self.t_exp_prev)
        PL3 = self.pbank(PL).rearrange("p (c w) -> p c w", c=8)
        self.OP("vector", lambda e: e.tensor_scalar(self.s_bb.ap(), PL3[:, :, 63:64], 0.5, None, ALU.mult), drain=True)
        t_arg = self.OP("vector", (lambda e:
                        e.tensor_tensor(self.arg3, PL3,
                                        self.s_bb.ap()[:, :, None].broadcast_to([128, 8, CH]),
                                        ALU.subtract)), sem=self.S_ve, drain=True)
        self.bankg[PL] = (self.S_ve, t_arg)
        self.W("scalar", self.S_ve, t_arg)
        self.W("scalar", self.S_ve, self.t_fac_prev)
        self.OP("scalar", lambda e: e.activation(self.s_eL.ap(), self.s_arg.ap(), AF.Exp), drain=True)
        self.OP("scalar", lambda e: e.activation(self.s_eN.ap(), self.s_arg.ap(), AF.Exp, scale=-1.0), drain=True)
        self.OP("scalar", lambda e: e.activation(self.s_pch.ap(), self.s_bb.ap(), AF.Exp), drain=True)
        t_exp = self.OP("scalar", (lambda e, par=par: e.activation(
            self.pcf3[:, par, :], self.s_bb.ap(), AF.Exp, scale=2.0)),
            sem=self.S_se, drain=True)
        self.t_exp_prev = t_exp
        self.W("vector", self.S_se, t_exp)
        self.W("vector", self.S_te, self.scan_te_g[par])
        pch_b = self.s_pch.ap()[:, :, None].broadcast_to([128, 8, CH])
        self.OP("vector", (lambda e, par=par: e.tensor_tensor(
            self.rtl4[:, par], self.rTg3[:, :, ct * CH:(ct + 1) * CH], self.eL3, ALU.mult)), drain=True)
        self.OP("vector", (lambda e, par=par: e.tensor_tensor(
            self.ktl4[:, par], self.kTg3[:, :, ct * CH:(ct + 1) * CH], self.eN3, ALU.mult)), drain=True)
        self.OP("vector", (lambda e, par=par, p=pch_b: e.tensor_tensor(
            self.kbr4[:, par], self.ktl4[:, par], p, ALU.mult)), drain=True)
        t_fac = self.OP("vector", (lambda e, par=par, p=pch_b: e.tensor_tensor(
            self.rtlp3[:, :, cg * CH:(cg + 1) * CH], self.rtl4[:, par], p, ALU.mult)),
            sem=self.S_ve, drain=True)
        self.t_fac_prev = t_fac
        if not hasattr(self, "t_fac_of"):
            self.t_fac_of = {}
        self.t_fac_of[cg] = t_fac

    def emit_scan_h2(self, g, ct, cg):
        parts = getattr(self, "scan_parts", {"scores", "msk", "tr", "y", "state"})
        par = ct % 2
        b0 = 64 * par
        tt = ct // 2
        PL, PS1, PS2, PST = 0, 1, 2, 6
        PYA, PYB = 3, 4
        t_fac = self.t_fac_of[cg]
        if "scores" not in parts:
            return
        self.W("tensor", self.S_ve, t_fac)
        self.wait_bank(PS1)
        self.wait_bank(PS2)
        t_sc = None
        for h in range(H):
            bank = self.pbank(PS1 if h % 2 == 0 else PS2)
            hb = 64 * (h % 2)
            t_sc = self.OP("tensor", (lambda e, b=bank[b0:b0 + 64, 64 * (h // 2):64 * (h // 2) + 64],
                                      w=self.ktl4[hb:hb + 64, par, h // 2, :],
                                      r=self.rtl4[hb:hb + 64, par, h // 2, :],
                                      st=(h < 2), sp=(h >= H - 2):
                                      e.matmul(b, w, r, start=st, stop=sp)),
                           sem=self.S_te if h == H - 1 else None)
        if "msk" not in parts:
            return
        self.W("vector", self.S_te, t_sc)
        self.OP("vector", (lambda e: e.copy_predicated(self.mSc[b0:b0 + 64, 0:512],
                                                       self.c_mask.ap()[b0:b0 + 64, 0:512],
                                                       self.pbank(PS1)[b0:b0 + 64, :])), drain=True)
        t_msk = self.OP("vector", (lambda e: e.copy_predicated(self.mSc[b0:b0 + 64, 512:1024],
                                                               self.c_mask.ap()[b0:b0 + 64, 512:1024],
                                                               self.pbank(PS2)[b0:b0 + 64, :])),
                        sem=self.S_ve, drain=True)
        self.bankg[PS1] = (self.S_ve, t_msk)
        self.bankg[PS2] = (self.S_ve, t_msk)
        if cg == 0:
            self.dump("c0_mSc", self.s_mSc.ap())
        if "tr" not in parts:
            return
        t_kbt = 0
        t_ktr = None
        for kc in range(DK):
            bi = 5 if kc % 2 == 0 else 7
            self.wait_bank(bi)
            t_ktr = self.OP("tensor", (lambda e, b=self.pbank(bi).bitcast(BF16)[b0:b0 + 64, 0:128],
                                       src=self.kbr4[:, par, kc, :]:
                                       e.transpose(b, src, self.c_identb.ap())), sem=self.S_te)
            self.W("vector", self.S_te, t_ktr)
            t_kbt = self.OP("vector", (lambda e, o=self.kbtok3[b0:b0 + 64, kc, :],
                                       b=self.pbank(bi).bitcast(BF16)[b0:b0 + 64, 0:128]:
                                       e.tensor_copy(o, b)), sem=self.S_ve, drain=True)
            self.bankg[bi] = (self.S_ve, t_kbt)
        self.scan_te_g[par] = t_ktr
        if "y" not in parts:
            return
        self.W("tensor", self.S_ve, t_msk)
        self.W("tensor", self.S_ve, self.t_supd_prev)
        self.W("tensor", self.S_ve, self.t_vtg)
        self.wait_bank(PYA)
        self.wait_bank(PYB)
        t_y = None
        started = [False, False]
        for h in range(H):
            hb = 64 * (h % 2)
            yout = self.pbank(PYA)[hb:hb + 64, 64 * (h // 2):64 * (h // 2) + 64]
            st = not started[h % 2]
            started[h % 2] = True
            mcol = 512 * (h % 2) + 64 * (h // 2)
            self.OP("tensor", (lambda e, b=yout,
                               w=self.vtg3[b0:b0 + 64, tt, 64 * h:64 * h + 64],
                               r=self.mSc[b0:b0 + 64, mcol:mcol + 64],
                               st=st, sp=(h >= H - 2): e.matmul(b, w, r, start=st, stop=sp)))
        started = [False, False]
        for h in range(H):
            hb = 64 * (h % 2)
            yout = self.pbank(PYB)[hb:hb + 64, 64 * (h // 2):64 * (h // 2) + 64]
            st = not started[h % 2]
            started[h % 2] = True
            t_y = self.OP("tensor", (lambda e, b=yout,
                                     w=self.Sbf3[hb:hb + 64, h // 2, :],
                                     r=self.rtlp3[hb:hb + 64, h // 2, cg * CH:(cg + 1) * CH],
                                     st=st, sp=(h >= H - 2): e.matmul(b, w, r, start=st, stop=sp)),
                          sem=self.S_te if h == H - 1 else None)
        self.W("vector", self.S_te, t_y)
        PA3 = self.pbank(PYA).rearrange("p (c w) -> p c w", c=8)
        PB3 = self.pbank(PYB).rearrange("p (c w) -> p c w", c=8)
        self.OP("vector", (lambda e: e.tensor_copy(self.ypre3[:, :, cg * CH:(cg + 1) * CH], PA3)), drain=True)
        t_yev = self.OP("vector", (lambda e: e.tensor_tensor(self.ypre3[:, :, cg * CH:(cg + 1) * CH],
                                                             self.ypre3[:, :, cg * CH:(cg + 1) * CH],
                                                             PB3, ALU.add)),
                        sem=self.S_ve, drain=True)
        self.bankg[PYA] = (self.S_ve, t_yev)
        self.bankg[PYB] = (self.S_ve, t_yev)
        self.t_ypre_ve = t_yev
        if "state" not in parts:
            self.t_latg_free = self.t_cum_last
            return
        self.W("tensor", self.S_ve, t_kbt)
        self.wait_bank(PST)
        t_st = None
        started = [False, False]
        for h in range(H):
            hb = 64 * (h % 2)
            st = not started[h % 2]
            started[h % 2] = True
            t_st = self.OP("tensor", (lambda e, b=self.pbank(PST)[hb:hb + 64, 64 * (h // 2):64 * (h // 2) + 64],
                                      w=self.kbtok3[b0:b0 + 64, h // 2, 64 * (h % 2):64 * (h % 2) + 64],
                                      r=self.vtg3[b0:b0 + 64, tt, 64 * h:64 * h + 64],
                                      st=st, sp=(h >= H - 2):
                                      e.matmul(b, w, r, start=st, stop=sp)),
                           sem=self.S_te if h == H - 1 else None)
        self.W("vector", self.S_te, t_st)
        pcf_b = self.pcf3[:, par, :, None].broadcast_to([128, 8, HS])
        self.OP("vector", (lambda e: e.tensor_copy(self.Dpre3[:, cg, :], self.s_Dcur.ap())), drain=True)
        self.OP("vector", (lambda e, par=par: e.tensor_tensor(self.s_Dcur.ap(), self.s_Dcur.ap(),
                                                              self.pcf3[:, par, :], ALU.mult)), drain=True)
        self.OP("vector", (lambda e, p=pcf_b: e.tensor_tensor(self.S3, self.S3, p, ALU.mult)), drain=True)
        PST3 = self.pbank(PST).rearrange("p (c w) -> p c w", c=8)
        self.OP("vector", (lambda e: e.tensor_tensor(self.S3, self.S3, PST3, ALU.add)), drain=True)
        t_supd = self.OP("vector", (lambda e: e.tensor_copy(self.s_Sbf.ap(), self.s_S.ap())),
                         sem=self.S_ve, drain=True)
        self.bankg[PST] = (self.S_ve, t_supd)
        self.t_supd_prev = t_supd
        self.t_latg_free = self.t_cum_last
        self.t_kv_scan_free = max(t_sc or 0, t_st or 0)

    def emit_ffn0_group(self, g):
        TG = self.TG
        t0 = g * TG
        def epi_h(ob, bank, t_mm):
            self.W("scalar", self.S_te, t_mm)
            if ob == 0:
                self.W("scalar", self.S_te, self.t_kv_scan_free)
                self.W("scalar", self.S_ve, self.t_fac_prev)
            t_e = self.OP("scalar", (lambda e, o=self.hg3[:, ob, :], b=bank,
                                     bi=self.s_b1.ap()[:, ob:ob + 1]:
                                     e.activation(o, b, AF.Gelu, bias=bi)), sem=self.S_se, drain=True)
            return (self.S_se, t_e)
        self.fm_proj("w1", D, FF, lambda kb: self.xng3[:, kb, :], epi_h)
        t_h = self.S_se.v
        def epi_x1p(ob, bank, t_mm):
            par = ob % 2
            self.W("vector", self.S_te, t_mm)
            ssem, stgt = self.stgg[par]
            self.W("vector", ssem, stgt)
            t_v = self.OP("vector", (lambda e, o=self.stg[par].ap(), b=bank,
                                     s=self.s_b2.ap()[:, ob:ob + 1], x=self.xTg3[:, ob, :]:
                                     e.scalar_tensor_tensor(o, b, s, x, ALU.add, ALU.add)),
                          sem=self.S_ve, drain=True)
            self.t_xT_free = t_v
            self.W("sync", self.S_ve, t_v)
            t_d = self.dma("sync",
                           self.x1p_d.ap().rearrange("(dk p) t -> p dk t", p=128)[:, ob, t0:t0 + TG],
                           self.stg[par].ap(), sem=self.S_stgs[par])
            self.stgg[par] = (self.S_stgs[par], t_d)
            return (self.S_ve, t_v)
        self.fm_proj("w2", FF, D, lambda kb: self.hg3[:, kb, :], epi_x1p,
                     rhs_waits=((self.S_se, t_h),))

    def emit_cc1(self):
        self.W("gpsimd", self.S_ve, self.t_supd_prev)
        self.dma("gpsimd", self.ccF_in.ap()[:, 0:512], self.s_S.ap(), sem=self.S_pub)
        t_p = self.dma("gpsimd", self.ccF_in.ap()[:, 512:520], self.s_Dcur.ap(), sem=self.S_pub)
        self.W("gpsimd", self.S_pub, t_p)
        self.OP("gpsimd", (lambda e: e.collective_compute(
            "AllGather", ALU.bypass,
            replica_groups=[[0, 1, 2, 3], [4, 5, 6, 7]],
            ins=[self.ccF_in.ap().opt()],
            outs=[self.ccF_out.ap().opt()])), sem=self.S_cc1)

    def emit_p3(self):
        T, NCH_ = self.T, self.NCH
        self.W("gpsimd", self.S_cc1, 1)
        self.W("gpsimd", self.S_ve, self.t_xT_free)
        self.W("gpsimd", self.S_te, self.t_kv_scan_free)
        t_gb = self.dma("gpsimd", self.gbuf3,
                        self.ccF_out.ap().rearrange("(r p) c -> p r c", p=128),
                        sem=self.S_gbuf)
        self.W("vector", self.S_gbuf, t_gb)
        self.OP("vector", lambda e: e.memset(self.s_Spre.ap(), 0.0), drain=True)
        self.OP("vector", lambda e: e.memset(self.s_Sin.ap(), 0.0), drain=True)
        for c in range(4):
            self.OP("vector", (lambda e, c=c: e.scalar_tensor_tensor(
                self.s_Sin.ap(), self.s_Spre.ap(), self.c_oh4.ap()[:, c:c + 1],
                self.s_Sin.ap(), ALU.mult, ALU.add)), drain=True)
            self.OP("vector", (lambda e, c=c: e.tensor_tensor(
                self.Spre3, self.Spre3,
                self.gbuf3[:, c, 512:520][:, :, None].broadcast_to([128, 8, HS]),
                ALU.mult)), drain=True)
            self.OP("vector", (lambda e, c=c: e.tensor_tensor(
                self.s_Spre.ap(), self.s_Spre.ap(), self.gbuf3[:, c, 0:512], ALU.add)), drain=True)
        t_sin = self.OP("vector", lambda e: e.tensor_copy(self.s_Sinb.ap(), self.s_Sin.ap()),
                        sem=self.S_ve, drain=True)
        PYC = 3
        t_rpp_guard = 0
        for cg in range(NCH_):
            self.W("vector", self.S_te, t_rpp_guard)
            dpre_b = self.Dpre3[:, cg, :][:, :, None].broadcast_to([128, 8, CH])
            t_rpp = self.OP("vector", (lambda e, cg=cg, d=dpre_b: e.tensor_tensor(
                self.rpp3, self.rtlp3[:, :, cg * CH:(cg + 1) * CH], d, ALU.mult)),
                sem=self.S_ve, drain=True)
            self.W("tensor", self.S_ve, t_rpp)
            if cg == 0:
                self.W("tensor", self.S_ve, t_sin)
            self.wait_bank(PYC)
            t_cor = None
            started = [False, False]
            for h in range(H):
                hb = 64 * (h % 2)
                st = not started[h % 2]
                started[h % 2] = True
                t_cor = self.OP("tensor", (lambda e, b=self.pbank(PYC)[hb:hb + 64, 64 * (h // 2):64 * (h // 2) + 64],
                                           w=self.Sinb3[hb:hb + 64, h // 2, :],
                                           r=self.rpp3[hb:hb + 64, h // 2, :],
                                           st=st, sp=(h >= H - 2):
                                           e.matmul(b, w, r, start=st, stop=sp)),
                                sem=self.S_te if h == H - 1 else None)
            t_rpp_guard = t_cor
            self.W("vector", self.S_te, t_cor)
            PYC3 = self.pbank(PYC).rearrange("p (c w) -> p c w", c=8)
            t_yc = self.OP("vector", (lambda e, cg=cg, p3=PYC3: e.tensor_tensor(
                self.ypre3[:, :, cg * CH:(cg + 1) * CH],
                self.ypre3[:, :, cg * CH:(cg + 1) * CH], p3, ALU.add)),
                sem=self.S_ve, drain=True)
            self.bankg[PYC] = (self.S_ve, t_yc)
            t_yc_last = t_yc
        self.dump("ypre", self.s_ypre.ap())
        self.W("sync", self.S_stgs[0], self.S_stgs[0].v)
        self.W("sync", self.S_stgs[1], self.S_stgs[1].v)
        for tw in range(T // 512):
            w0 = tw * 512
            def epi_x1(ob, bank, t_mm, w0=w0):
                par = ob % 2
                ssem, stgt = self.x1lg[par]
                self.W("sync", ssem, stgt)
                t_l = self.dma("sync", self.x1l[par].ap(),
                               self.x1p_d.ap().rearrange("(dk p) t -> p dk t", p=128)[:, ob, w0:w0 + 512],
                               sem=self.S_x1ls[par])
                self.W("vector", self.S_te, t_mm)
                self.W("vector", self.S_x1ls[par], t_l)
                self.W("vector", self.S_stgs[par], self.t_x1s_guard[par])
                self.OP("vector", (lambda e, b=bank, p=par, ob=ob: e.tensor_tensor(
                    self.x1tw3[:, ob, :], b, self.x1l[p].ap(), ALU.add)), drain=True)
                t_v = self.OP("vector", (lambda e, ob=ob, w0=w0: e.tensor_copy(
                    self.xn13[:, ob, w0:w0 + 512], self.x1tw3[:, ob, :])),
                    sem=self.S_ve, drain=True)
                self.W("sync", self.S_ve, t_v)
                t_d = self.dma("sync",
                               self.x1_d.ap().rearrange("(dk p) t -> p dk t", p=128)[:, ob, w0:w0 + 512],
                               self.x1tw3[:, ob, :], sem=self.S_stgs[par])
                self.x1lg[par] = (self.S_ve, t_v)
                self.t_x1s_guard[par] = self.S_stgs[par].v
                return (self.S_ve, t_v)
            last = self.fm_proj("Wo", D, D, lambda kb: self.ypre3[:, kb, w0:w0 + 512], epi_x1,
                                rhs_waits=((self.S_ve, t_yc_last),))
            t_bc = self.ln_stats(lambda kb, w0=w0: self.xn13[:, kb, w0:w0 + 512], 512,
                                 sq_waits=(last,))
            self.t_xn1_ready = self.ln_apply(
                lambda kb, w0=w0: self.xn13[:, kb, w0:w0 + 512],
                lambda kb, w0=w0: self.xn13[:, kb, w0:w0 + 512], 512,
                affine=False, t_bc=t_bc)
        self.dump("xn1", self.s_xn1.ap())

    def emit_p4(self):
        T = self.T
        NB = 6
        t_q_store = 0
        for tw in range(T // 512):
            w0 = tw * 512
            def epi_q(ob, bank, t_mm, w0=w0):
                par = ob % 2
                self.W("scalar", self.S_te, t_mm)
                self.W("scalar", self.S_sqbs[par], self.sqbg[par])
                t_e = self.OP("scalar", (lambda e, b=bank, p=par: e.activation(
                    self.sqb[p].ap(), b, AF.Copy)), sem=self.S_se, drain=True)
                self.W("sync", self.S_se, t_e)
                t_d = self.dma("sync",
                               self.q_d.ap().rearrange("(dk p) t -> p dk t", p=128)[:, ob, w0:w0 + 512],
                               self.sqb[par].ap(), sem=self.S_sqbs[par])
                self.sqbg[par] = t_d
                return (self.S_se, t_e)
            self.fm_proj("WqT", D, D, lambda kb, w0=w0: self.xn13[:, kb, w0:w0 + 512], epi_q,
                         rhs_waits=((self.S_ve, self.t_xn1_ready),))
            def epi_k(ob, bank, t_mm, w0=w0):
                self.W("scalar", self.S_te, t_mm)
                self.W("scalar", self.S_eld, self.t_kstage_guard)
                t_e = self.OP("scalar", (lambda e, b=bank: e.activation(
                    self.s_eL.ap(), b, AF.Copy)), sem=self.S_se, drain=True)
                self.W("sync", self.S_se, t_e)
                self.t_kstage_guard = self.dma(
                    "sync",
                    self.ccK_in.ap().rearrange("(dk p) t -> p dk t", p=128)[:, ob, w0:w0 + 512],
                    self.s_eL.ap(), sem=self.S_eld)
                return (self.S_se, t_e)
            self.t_kstage_guard = 0
            self.fm_proj("WkT", D, D, lambda kb, w0=w0: self.xn13[:, kb, w0:w0 + 512], epi_k)
            def epi_vv(tt, oh, bank, t_mm, tw=tw):
                self.W("vector", self.S_te, t_mm)
                self.W("vector", self.S_ktld, self.t_vstage_guard)
                t_e = self.OP("vector", (lambda e, b=bank: e.tensor_copy(self.s_ktl.ap()[:, 0:512], b)),
                              sem=self.S_ve, drain=True)
                self.W("sync", self.S_ve, t_e)
                row0 = (tw * 4 + tt) * 128
                self.t_vstage_guard = self.dma(
                    "sync", self.ccV_in.ap()[row0:row0 + 128, oh * 512:(oh + 1) * 512],
                    self.s_ktl.ap()[:, 0:512], sem=self.S_ktld)
                return (self.S_ve, t_e)
            self.t_vstage_guard = 0
            self.tok_proj("WvT", (),
                          lambda kb, tt, w0=w0: self.xn13[:, kb, w0 + tt * 128:w0 + (tt + 1) * 128],
                          epi_vv)
        self.W("gpsimd", self.S_eld, self.S_eld.v)
        self.W("gpsimd", self.S_ktld, self.S_ktld.v)
        for cin, cout in ((self.ccK_in, self.ccK_out), (self.ccV_in, self.ccV_out)):
            self.OP("gpsimd", (lambda e, ci=cin, co=cout: e.collective_compute(
                "AllGather", ALU.bypass,
                replica_groups=[[0, 1, 2, 3], [4, 5, 6, 7]],
                ins=[ci.ap().opt()],
                outs=[co.ap().opt()])), sem=self.S_cc2)

    def emit_p5(self):
        s2 = SlabStream(self, [self.wbBIG, self.wbRL], self.S_w2s,
                        init_guards=[(self.S_te, self.S_te.v),
                                     (self.S_stgs[0], self.S_stgs[0].v),
                                     (self.S_stgs[1], self.S_stgs[1].v)])
        for g in range(self.NTG):
            s2.reg("tw1", 32, 1024, 8)
            s2.reg("tw2", 32, 1024, 8)
            s2.reg("th1", 32, 4096, 2)
            s2.reg("th2", 32, 4096, 2)
            s2.reg("tw3", 8, 4096, 2)
        self.stream = s2
        for g in range(self.NTG):
            t0 = g * self.TG
            def epi_act(dst3, scale_col, func):
                def epi(ob, bank, t_mm):
                    self.W("scalar", self.S_te, t_mm)
                    t_e = self.OP("scalar", (lambda e, o=dst3[:, ob, :], b=bank,
                                             s=scale_col[:, ob:ob + 1]:
                                             e.activation(o, b, func, scale=s)),
                                  sem=self.S_se, drain=True)
                    return (self.S_se, t_e)
                return epi
            self.fm_proj("tw1", D, FF, lambda kb, t0=t0: self.xn13[:, kb, t0:t0 + self.TG],
                         epi_act(self.u13, self.s_s1.ap(), AF.Silu),
                         rhs_waits=((self.S_te, self.t_u1_guard),))
            t_u1 = self.S_se.v
            self.fm_proj("tw2", D, FF, lambda kb, t0=t0: self.xn13[:, kb, t0:t0 + self.TG],
                         epi_act(self.u23, self.s_s2.ap(), AF.Copy),
                         rhs_waits=((self.S_te, self.t_u2_guard),))
            t_u2 = self.S_se.v
            t_h = None
            for ob in range(FK):
                if ob == 0:
                    self.W("vector", self.S_se, max(t_u1, t_u2))
                t_h = self.OP("vector", (lambda e, ob=ob: e.tensor_tensor(
                    self.u13[:, ob, :], self.u13[:, ob, :], self.u23[:, ob, :], ALU.mult)),
                    sem=self.S_ve, drain=True)
            def epi_g1(ob, bank, t_mm):
                self.W("scalar", self.S_te, t_mm)
                t_e = self.OP("scalar", (lambda e, o=self.u23[:, ob, :], b=bank,
                                         s=self.s_sh1.ap()[:, ob:ob + 1]:
                                         e.activation(o, b, AF.Gelu, scale=s)),
                              sem=self.S_se, drain=True)
                return (self.S_se, t_e)
            self.fm_proj("th1", FF, FF, lambda kb: self.u13[:, kb, :], epi_g1,
                         rhs_waits=((self.S_ve, t_h),))
            t_g1 = self.S_se.v
            self.fm_proj("th2", FF, FF, lambda kb: self.u23[:, kb, :],
                         epi_act(self.u13, self.s_sh2.ap(), AF.Gelu),
                         rhs_waits=((self.S_se, t_g1),))
            t_g2 = self.S_se.v
            self.t_u2_guard = self.S_te.v
            def epi_y3(ob, bank, t_mm, t0=t0):
                self.W("scalar", self.S_te, t_mm)
                t_e = self.OP("scalar", (lambda e, o=self.yffn13[:, ob, t0:t0 + self.TG], b=bank,
                                         s=self.s_s3.ap()[:, ob:ob + 1]:
                                         e.activation(o, b, AF.Copy, scale=s)),
                              sem=self.S_se, drain=True)
                return (self.S_se, t_e)
            self.fm_proj("tw3", FF, D, lambda kb: self.u13[:, kb, :], epi_y3,
                         rhs_waits=((self.S_se, t_g2),))
            self.t_u1_guard = self.S_te.v
        self.dump("yffn1", self.s_yffn1.ap())

    def emit_p6(self):
        T = self.T
        KMB = self.KMB
        NKC = KMB // 128
        NMB = 4 * T // KMB
        NQG = T // 512
        self.W("sync", self.S_te, self.S_te.v)
        self.W("sync", self.S_se, self.S_se.v)
        t_q = self.dma("sync", self.q3, self.q_d.ap().rearrange("(dk p) t -> p dk t", p=128),
                       sem=self.S_qld)
        self.W("sync", self.S_cc2, 2)
        kv_guard = [0, 0]
        t_kld = [0, 0]
        t_vld = [0, 0]
        LBANK = (2, 5, 6, 7)
        t_avadd = 0
        t_av_prev = 0
        for mbi in range(NMB):
            par = mbi % 2
            r, l = (mbi * KMB) // T, (mbi * KMB) % T
            self.W("sync", self.S_te, kv_guard[par])
            t_kld[par] = self.dma(
                "sync", self.Kmb4[:, par, :, :],
                self.ccK_out.ap().rearrange("(r kb p) t -> p r kb t", p=128, kb=DK)[:, r, :, l:l + KMB],
                sem=self.S_kmbs[par])
            a0 = (r * T + l) // 128
            t_vld[par] = self.dma(
                "sync", self.Vmb4[:, par, :, :],
                self.ccV_out.ap().rearrange("(a p) d -> p a d", p=128)[:, a0:a0 + NKC, :],
                sem=self.S_vmbs[par])
            for qw in range(NQG):
                q0 = qw * 512
                t_exp = [None, None]
                for kc in range(NKC):
                    bi = kc
                    self.wait_bank(bi)
                    if qw == 0 and kc == 0:
                        self.W("tensor", self.S_kmbs[par], t_kld[par])
                        if mbi == 0:
                            self.W("tensor", self.S_qld, t_q)
                    t_sc = None
                    for kb in range(DK):
                        t_sc = self.OP("tensor", (lambda e, b=self.pbank(bi)[:, 0:512],
                                                  w=self.Kmb4[:, par, kb, kc * 128:(kc + 1) * 128],
                                                  r=self.q3[:, kb, q0:q0 + 512],
                                                  st=(kb == 0), sp=(kb == DK - 1):
                                                  e.matmul(b, w, r, start=st, stop=sp)),
                                       sem=self.S_te if kb == DK - 1 else None)
                    self.W("scalar", self.S_te, t_sc)
                    self.W("scalar", self.S_te, t_av_prev)
                    t_exp[kc] = self.OP("scalar", (lambda e, b=self.pbank(bi)[:, 0:512], kc=kc:
                                                   e.activation(self.PT3[:, kc, 0:512], b, AF.Exp,
                                                                scale=1.0 / 32.0)),
                                        sem=self.S_se, drain=True)
                    self.bankg[bi] = (self.S_se, t_exp[kc])
                if qw == 0:
                    self.W("tensor", self.S_vmbs[par], t_vld[par])
                for oc in range(DK):
                    bi = 3 + (oc % 2)
                    self.wait_bank(bi)
                    if oc == 0:
                        self.W("tensor", self.S_se, t_exp[0])
                    t_av = None
                    for kc in range(NKC):
                        if oc == 0 and kc == 1:
                            self.W("tensor", self.S_se, t_exp[NKC - 1])
                        t_av = self.OP("tensor", (lambda e, b=self.pbank(bi)[:, 0:512],
                                                  w=self.Vmb4[:, par, kc, oc * 128:(oc + 1) * 128],
                                                  r=self.PT3[:, kc, 0:512],
                                                  st=(kc == 0), sp=(kc == NKC - 1):
                                                  e.matmul(b, w, r, start=st, stop=sp)),
                                       sem=self.S_te if kc == NKC - 1 else None)
                    self.W("vector", self.S_te, t_av)
                    acc = self.accL3 if oc < 4 else self.accH3
                    aslice = acc[:, oc % 4, q0:q0 + 512]
                    if mbi == 0:
                        t_avadd = self.OP("vector", (lambda e, a=aslice, b=self.pbank(bi)[:, 0:512]:
                                                     e.tensor_copy(a, b)), sem=self.S_ve, drain=True)
                    else:
                        t_avadd = self.OP("vector", (lambda e, a=aslice, b=self.pbank(bi)[:, 0:512]:
                                                     e.tensor_tensor(a, a, b, ALU.add)), sem=self.S_ve, drain=True)
                    self.bankg[bi] = (self.S_ve, t_avadd)
                t_l = None
                for kc in range(NKC):
                    t_l = self.OP("tensor", (lambda e, b=self.pbank(LBANK[qw])[0:1, 0:512], kc=kc,
                                             st=(mbi == 0 and kc == 0),
                                             sp=(mbi == NMB - 1 and kc == NKC - 1):
                                             e.matmul(b, self.c_onescb.ap(), self.PT3[:, kc, 0:512],
                                                      start=st, stop=sp)),
                                  sem=self.S_te if kc == NKC - 1 else None)
                t_av_prev = t_l
            kv_guard[par] = self.S_te.v
        self.t_attn_done = t_avadd
        self.LBANK = LBANK

    def emit_p7(self):
        T = self.T
        self.W("sync", self.S_stgs[0], self.S_stgs[0].v)
        self.W("sync", self.S_stgs[1], self.S_stgs[1].v)
        self.W("vector", self.S_ve, self.t_attn_done)
        self.W("vector", self.S_se, self.S_se.v)
        for tw in range(T // 512):
            w0 = tw * 512
            t_lr = self.OP("vector", (lambda e, tw=tw: e.reciprocal(
                self.r_tmp.ap()[:, 0:512], self.pbank(self.LBANK[tw])[0:1, 0:512])),
                sem=self.S_ve, drain=True)
            self.W("tensor", self.S_ve, t_lr)
            self.wait_bank(2)
            t_b = self.OP("tensor", (lambda e: e.matmul(
                self.pbank(2), self.c_ones1f.ap(), self.r_tmp.ap()[0:1, 0:512],
                start=True, stop=True)), sem=self.S_te)
            self.W("vector", self.S_te, t_b)
            t_lb = self.OP("vector", (lambda e: e.tensor_copy(self.s_linvb.ap(), self.pbank(2))),
                           sem=self.S_ve, drain=True)
            self.bankg[2] = (self.S_ve, t_lb)
            for ob in range(DK):
                par = ob % 2
                ssem, stgt = self.x1lg[par]
                self.W("sync", ssem, stgt)
                t_l = self.dma("sync", self.x1l[par].ap(),
                               self.x1_d.ap().rearrange("(dk p) t -> p dk t", p=128)[:, ob, w0:w0 + 512],
                               sem=self.S_x1ls[par])
                acc = self.accL3 if ob < 4 else self.accH3
                aslice = acc[:, ob % 4, w0:w0 + 512]
                self.W("vector", self.S_x1ls[par], t_l)
                if ob == 0:
                    self.W("vector", self.S_ve, t_lb)
                self.W("vector", self.S_stgs[par], self.stg_p7g[par])
                self.OP("vector", (lambda e, a=aslice, p=par: e.tensor_tensor(
                    self.stg[p].ap(), a, self.s_linvb.ap(), ALU.mult)), drain=True)
                self.OP("vector", (lambda e, p=par: e.tensor_tensor(
                    self.stg[p].ap(), self.stg[p].ap(), self.x1l[p].ap(), ALU.add)), drain=True)
                t_f = self.OP("vector", (lambda e, p=par, ob=ob, w0=w0: e.tensor_tensor(
                    self.stg[p].ap(), self.stg[p].ap(), self.yffn13[:, ob, w0:w0 + 512], ALU.add)),
                    sem=self.S_ve, drain=True)
                self.x1lg[par] = (self.S_ve, t_f)
                self.W("sync", self.S_ve, t_f)
                self.W("sync", self.S_stgs[par], self.stg_p7g[par])
                t_d = self.dma("sync",
                               self.out_d.ap().rearrange("(dk p) t -> p dk t", p=128)[:, ob, w0:w0 + 512],
                               self.stg[par].ap(), sem=self.S_stgs[par])
                self.stg_p7g[par] = t_d
        for s in self.ALL_DMA_SEMS:
            self.W("sync", s, s.v)

    def build(self, stop_after=None):
        self._stop = stop_after
        self.t_latg_free = 0
        self.t_ypre_ve = 0
        self.t_xT_free = 0
        self.t_vtg = 0
        self.t_la = 0
        self.t_exp_prev = 0
        self.t_fac_prev = 0
        self.scan_te_g = [0, 0]
        self.t_cum_last = 0
        self.t_supd_prev = self.T_VE_INIT if hasattr(self, "T_VE_INIT") else 0
        self.t_kv_scan_free = 0
        self.t_u1_guard = 0
        self.t_u2_guard = 0
        self.t_kmb_guard = 0
        self.t_vmb_guard = 0
        self.t_pt_guard = 0
        self.x1lg = [(self.S_ve, 0), (self.S_ve, 0)]
        self.t_x1s_guard = [0, 0]
        self.sqbg = [0, 0]
        self.stg_p7g = [0, 0]
        self.emit_init()
        s1 = SlabStream(self, self.wb, self.S_wbs)
        for g in range(self.NTG):
            s1.reg("Wx", 8, 1024, 4)
            s1.reg("Ww", 2, 4096, 1)
            s1.reg("Wr", 8, 1024, 4)
            s1.reg("Wk", 8, 1024, 4)
            s1.reg("Wv", 2, 4096, 1)
            s1.reg("w1", 32, 1024, 4)
            s1.reg("w2", 8, 4096, 1)
        for tw in range(self.T // 512):
            s1.reg("Wo", 8, 1024, 4)
        for tw in range(self.T // 512):
            s1.reg("WqT", 8, 1024, 4)
            s1.reg("WkT", 8, 1024, 4)
            s1.reg("WvT", 2, 4096, 1)
        self.stream = s1
        self.t_supd_prev = self.T_VE_INIT
        def done(stage):
            return self._stop == stage
        go = True
        CAST_DRIP = {1: ("tw1", "tw2"), 2: ("th1",), 3: ("th2", "tw3")}
        for g in range(self.NTG):
            for nm in CAST_DRIP.get(g, ()):
                self.W("gpsimd", self.S_te, self.S_te.v)
                self.emit_cast(nm)
            self.emit_l0_group(g)
            if done("p1"):
                go = False
                break
            NCG = self.TG // CH
            self.emit_scan_h1(g, 0, g * NCG)
            for ct in range(NCG):
                if ct + 1 < NCG:
                    self.emit_scan_h1(g, ct + 1, g * NCG + ct + 1)
                self.emit_scan_h2(g, ct, g * NCG + ct)
            if done("scan"):
                go = False
                break
            self.emit_ffn0_group(g)
        self.dump("ypre_s", self.s_ypre.ap())
        if go and not done("ffn0"):
            self.emit_cc1()
            if not done("cc1"):
                self.emit_p3()
                if not done("p3"):
                    self.emit_p4()
                    if not done("p4"):
                        self.emit_p5()
                        if not done("p5"):
                            self.emit_p6()
                            if not done("p6"):
                                self.emit_p7()
        if self._stop is not None:
            for s in self.ALL_DMA_SEMS:
                self.W("sync", s, s.v)
        nc = self.nc
        with nc.Block() as block:
            @block.tensor
            def _(e):
                for f in self.q["tensor"]:
                    f(e)

            @block.vector
            def _(e):
                for f in self.q["vector"]:
                    f(e)

            @block.scalar
            def _(e):
                for f in self.q["scalar"]:
                    f(e)

            @block.gpsimd
            def _(e):
                for f in self.q["gpsimd"]:
                    f(e)

            @block.sync
            def _(e):
                for f in self.q["sync"]:
                    f(e)
        return nc


def _build_scanmask():
    m64 = np.triu(np.ones((CH, CH), np.uint8))
    block = np.concatenate([m64] * 16, axis=1)
    return np.concatenate([block, block], axis=0)


def host_inputs(inputs, T=2048):
    f32 = np.float32
    ct = np.ascontiguousarray

    def fmr(w):
        DIN_, DOUT_ = w.shape
        KIN, NOB = DIN_ // 128, DOUT_ // 128
        return ct(w.reshape(KIN, 128, NOB, 128).transpose(1, 2, 0, 3)
                  .reshape(128, NOB, KIN * 128))

    def tokr(w):
        return ct(w.reshape(8, 128, 1024).transpose(1, 0, 2).reshape(128, 2, 4096))

    x = np.asarray(inputs["L0_attn_x"], f32)
    base = {
        "Wx": fmr(np.asarray(inputs["L0_attn_Wx"], f32)),
        "Ww": tokr(np.asarray(inputs["L0_attn_Ww"], f32)),
        "Wk": fmr(np.asarray(inputs["L0_attn_Wk"], f32)),
        "Wv": tokr(np.asarray(inputs["L0_attn_Wv"], f32)),
        "Wr": fmr(np.asarray(inputs["L0_attn_Wr"], f32)),
        "Wo": fmr(np.asarray(inputs["L0_attn_Wo"], f32)),
        "w1": fmr(np.asarray(inputs["L0_ffn_w1"], f32)),
        "w2": fmr(np.asarray(inputs["L0_ffn_w2"], f32)),
        "WqT": fmr(np.asarray(inputs["L1_attn_Wq"], f32).T),
        "WkT": fmr(np.asarray(inputs["L1_attn_Wk"], f32).T),
        "WvT": tokr(np.asarray(inputs["L1_attn_Wv"], f32).T),
        "tw1": fmr(np.asarray(inputs["L1_ffn_w1_w"], f32).T),
        "tw2": fmr(np.asarray(inputs["L1_ffn_w2_w"], f32).T),
        "th1": fmr(np.asarray(inputs["L1_ffn_h1_w"], f32).T),
        "th2": fmr(np.asarray(inputs["L1_ffn_h2_w"], f32).T),
        "tw3": fmr(np.asarray(inputs["L1_ffn_w3_w"], f32).T),
        "v_lng": ct(np.asarray(inputs["L0_attn_ln_g"], f32).reshape(DK, 128).T),
        "v_lnb": ct(np.asarray(inputs["L0_attn_ln_b"], f32).reshape(DK, 128).T),
        "r_bw": np.asarray(inputs["L0_attn_bw"], f32).reshape(1, D),
        "v_b1": ct(np.asarray(inputs["L0_ffn_b1"], f32).reshape(FK, 128).T),
        "v_b2": ct(np.asarray(inputs["L0_ffn_b2"], f32).reshape(DK, 128).T),
        "v_s1": ct(np.asarray(inputs["L1_ffn_w1_s"], f32).reshape(FK, 128).T),
        "v_s2": ct(np.asarray(inputs["L1_ffn_w2_s"], f32).reshape(FK, 128).T),
        "v_sh1": ct(np.asarray(inputs["L1_ffn_h1_s"], f32).reshape(FK, 128).T),
        "v_sh2": ct(np.asarray(inputs["L1_ffn_h2_s"], f32).reshape(FK, 128).T),
        "v_s3": ct(np.asarray(inputs["L1_ffn_w3_s"], f32).reshape(DK, 128).T),
        "identf": np.eye(128, dtype=f32),
        "identb": np.eye(128, dtype=ml_dtypes.bfloat16),
        "u64f": np.tile(np.triu(np.ones((CH, CH), f32)), (2, 1)),
        "scanmask": _build_scanmask(),
    }
    in_maps = []
    for c in range(8):
        b, pos = c // 4, c % 4
        m = dict(base)
        m["x"] = ct(x[b, pos * T:(pos + 1) * T, :])
        oh = np.zeros((128, 4), f32)
        oh[:, pos] = 1.0
        m["oh4"] = oh
        in_maps.append(m)
    return in_maps


def kernel(**inputs):
    from concourse.bass_utils import run_bass_kernel_spmd
    T = 2048
    kb = KB(T=T)
    nc = kb.build()
    in_maps = host_inputs(inputs, T)
    res = run_bass_kernel_spmd(nc, in_maps, core_ids=list(range(8)))
    out = np.zeros((2, 4 * T, D), np.float32)
    for c in range(8):
        b, pos = c // 4, c % 4
        out[b, pos * T:(pos + 1) * T, :] = np.asarray(res.results[c]["out"]).T
    return out
